# version 21
# baseline (speedup 1.0000x reference)
import numpy as np
import ml_dtypes
from contextlib import ExitStack

import concourse.bass as bass
import concourse.mybir as mybir
from concourse.bass_utils import run_bass_kernel_spmd

F32 = mybir.dt.float32
BF16 = mybir.dt.bfloat16
I32 = mybir.dt.int32
AF = mybir.ActivationFunctionType
ALU = mybir.AluOpType
AX = mybir.AxisListType
bf16 = ml_dtypes.bfloat16

NCORES = 8
D = 4096
NFC = 32


class Sched:
    ENGS = ("pe", "dv", "ac", "gp", "sp", "sc")

    def __init__(self):
        self.ops = {e: [] for e in self.ENGS}
        self.cnt = {}

    def emit(self, eng, fn, waits=(), inc=None, incby=1):
        if inc is not None:
            self.cnt[inc] = self.cnt.get(inc, 0) + incby
            tgt = self.cnt[inc]
        else:
            tgt = None

        def g(e, sems, fn=fn, waits=tuple(waits), inc=inc, incby=incby):
            for s, v in waits:
                v = v() if callable(v) else v
                if v > 0:
                    e.wait_ge(sems[s], v)
            r = fn(e)
            if inc is not None:
                r.then_inc(sems[inc], incby)
        self.ops[eng].append(g)
        return tgt

    def drain(self, eng):
        self.ops[eng].append(lambda e, sems: e.drain())


def build_nc(ntok=1024, stage=99):
    H = ntok // 2
    NTIL = ntok // 128
    HTIL = H // 128
    nc = bass.Bass("TRN2", target_bir_lowering=False)

    d_x = nc.dram_tensor("xT", [D, ntok], F32, kind="ExternalInput").ap()
    d_w5 = nc.dram_tensor("w5", [32 * 32 * 128, 128], F32, kind="ExternalInput").ap()
    d_at = nc.dram_tensor("at", [D, 128], F32, kind="ExternalInput").ap()
    d_bp = nc.dram_tensor("bpT", [128, D], F32, kind="ExternalInput").ap()
    d_rwt = nc.dram_tensor("rwt", [D, 15], F32, kind="ExternalInput").ap()
    d_bb = nc.dram_tensor("bbT", [128, 32], F32, kind="ExternalInput").ap()
    d_rb = nc.dram_tensor("rb15", [15, 1], F32, kind="ExternalInput").ap()
    d_em = nc.dram_tensor("emap", [15, 1], I32, kind="ExternalInput").ap()
    d_sv = nc.dram_tensor("svec", [1, 640], F32, kind="ExternalInput").ap()
    d_vr = nc.dram_tensor("vctr", [64, 64], F32, kind="ExternalInput").ap()
    d_vi = nc.dram_tensor("vcti", [64, 64], F32, kind="ExternalInput").ap()
    d_i128 = nc.dram_tensor("i128", [128, 128], F32, kind="ExternalInput").ap()
    d_j128 = nc.dram_tensor("j128", [128, 128], F32, kind="ExternalInput").ap()
    d_ones = nc.dram_tensor("onesr", [1, 128], F32, kind="ExternalInput").ap()
    d_onescb = nc.dram_tensor("onescb", [128, 128], BF16, kind="ExternalInput").ap()
    d_rep8 = nc.dram_tensor("rep8", [8, 128], BF16, kind="ExternalInput").ap()
    d_io15 = nc.dram_tensor("io15", [15, 8], F32, kind="ExternalInput").ap()
    d_kscr = nc.dram_tensor("kscr", [1, 4608], F32)
    d_y = nc.dram_tensor("yT", [D, ntok], F32, kind="ExternalOutput").ap()

    S = Sched()
    with ExitStack() as ctx:
        E = ctx.enter_context
        xhi = E(nc.sbuf_tensor("xhi", [128, NFC * ntok], BF16))
        wbuf = [E(nc.sbuf_tensor(f"wbuf{i}", [128, 32 * 128], BF16)) for i in range(2)]
        atb = E(nc.sbuf_tensor("atb", [128, NFC * 128], BF16))
        wbuf.append(atb)
        bpb = E(nc.sbuf_tensor("bpb", [128, D], BF16))
        rwt = E(nc.sbuf_tensor("rwt_s", [128, NFC * 15], BF16))
        hwb = E(nc.sbuf_tensor("hwb", [128, ntok], BF16))
        stb = E(nc.sbuf_tensor("stb", [128, ntok], BF16))
        big32 = E(nc.sbuf_tensor("big32", [128, 4096], F32))
        xr32 = E(nc.sbuf_tensor("xr32", [128, 4 * ntok], F32))
        xb16 = E(nc.sbuf_tensor("xb16", [128, 4 * ntok], BF16))
        xq16 = E(nc.sbuf_tensor("xq16", [128, 2 * ntok], BF16))
        xn32 = E(nc.sbuf_tensor("xn32", [128, 2 * ntok], F32))
        krow = E(nc.sbuf_tensor("krow", [1, 1024], F32))
        mbr = E(nc.sbuf_tensor("mbr", [128, 2 * ntok], F32))
        rows = E(nc.sbuf_tensor("rows", [1, 3072], F32))
        mats = E(nc.sbuf_tensor("mats", [128, 23 * 64], F32))
        wpr = E(nc.sbuf_tensor("wpr", [128, 256], F32))
        bstk = E(nc.sbuf_tensor("bstk", [128, 1], F32))
        cols = E(nc.sbuf_tensor("cols", [128, 16], F32))
        mint = E(nc.sbuf_tensor("mint", [128, 256], BF16))
        moutt = E(nc.sbuf_tensor("moutt", [128, 256], BF16))
        ppk = E(nc.sbuf_tensor("ppk", [128, 128], BF16))
        ttri = E(nc.sbuf_tensor("ttri", [128, 128], BF16))
        tful = E(nc.sbuf_tensor("tful", [128, 128], BF16))
        lgsb = E(nc.sbuf_tensor("lgsb", [15, H], F32))
        lgt = E(nc.sbuf_tensor("lgt", [128, NTIL * 15], F32))
        mx8 = E(nc.sbuf_tensor("mx8", [128, NTIL * 8], F32))
        pmt = E(nc.sbuf_tensor("pmt", [128, NTIL * 15], F32))
        scr15 = E(nc.sbuf_tensor("scr15", [128, 15], F32))
        rcol = E(nc.sbuf_tensor("rcol", [128, 8], F32))
        selm8 = E(nc.sbuf_tensor("selm8", [128, 120], F32))
        esum8 = E(nc.sbuf_tensor("esum8", [128, 16], F32))
        rbx = E(nc.sbuf_tensor("rbx", [128, 15], F32))
        scr1 = E(nc.sbuf_tensor("scr1", [128, 2], F32))
        pmts = E(nc.sbuf_tensor("pmts", [15, ntok], BF16))
        w8s = E(nc.sbuf_tensor("w8s", [8, ntok], BF16))
        w16s = E(nc.sbuf_tensor("w16s", [128, ntok], BF16))
        i128 = E(nc.sbuf_tensor("i128_s", [128, 128], F32))
        j128 = E(nc.sbuf_tensor("j128_s", [128, 128], F32))
        onesr = E(nc.sbuf_tensor("onesr_s", [1, 128], F32))
        onescb = E(nc.sbuf_tensor("onescb_s", [128, 128], BF16))
        rep8 = E(nc.sbuf_tensor("rep8_s", [8, 128], BF16))
        io15 = E(nc.sbuf_tensor("io15_s", [15, 8], F32))
        mmapb = E(nc.sbuf_tensor("mmapb", [15, 8], BF16))
        emf = E(nc.sbuf_tensor("emf", [15, 1], F32))
        emi = E(nc.sbuf_tensor("emi", [15, 1], I32))
        rb15 = E(nc.sbuf_tensor("rb15_s", [15, 1], F32))
        bb128 = E(nc.sbuf_tensor("bb128", [128, 32], F32))
        svec = E(nc.sbuf_tensor("svec_s", [1, 640], F32))
        vctr = E(nc.sbuf_tensor("vctr_s", [64, 64], F32))
        vcti = E(nc.sbuf_tensor("vcti_s", [64, 64], F32))
        sc64 = E(nc.sbuf_tensor("sc64", [128, 4], F32))
        tstg = E(nc.sbuf_tensor("tstg", [128, 256], F32))
        rop = E(nc.sbuf_tensor("rop", [128, 128], F32))
        pkstg = E(nc.sbuf_tensor("pkstg", [64, 512], F32))
        wop = E(nc.sbuf_tensor("wop", [128, 128], F32))

        pb = [E(nc.psum_tensor(f"pb{i}", [128, 512], F32)) for i in range(8)]

        sems = {}
        dma_sem_names = (["PAR", "PARB", "K", "K2", "TV", "RW", "X1A", "X1B", "X1C", "X1D", "WA", "WB", "WC",
                          "OUTA", "OUTB"]
                         + [f"X2{i}" for i in range(4)]
                         + [f"X3{i}" for i in range(4)])
        for sname in ["PE", "DV", "AC"] + dma_sem_names:
            sems[sname] = E(nc.semaphore(sname))

        def dma(eng, sem, dst, src, waits=()):
            v = S.emit(eng, lambda e, d=dst, s=src: e.dma_start(d, s),
                       waits=waits, inc=sem, incby=16)
            return (sem, v)

        def build_block():
            block = E(nc.Block())

            @block.tensor
            def _(t):
                for f in S.ops["pe"]:
                    f(t, sems)

            @block.vector
            def _(v):
                for f in S.ops["dv"]:
                    f(v, sems)

            @block.scalar
            def _(a):
                for f in S.ops["sc"]:
                    f(a, sems)
                for f in S.ops["ac"]:
                    f(a, sems)

            @block.gpsimd
            def _(g):
                for f in S.ops["gp"]:
                    f(g, sems)

            @block.sync
            def _(s):
                for f in S.ops["sp"]:
                    f(s, sems)

        def pe(fn, waits=()):
            return S.emit("pe", fn, waits=waits, inc="PE")

        def dv(fn, waits=()):
            return S.emit("dv", fn, waits=waits, inc="DV")

        def ac(fn, waits=()):
            return S.emit("ac", fn, waits=waits, inc="AC")

        def dvd():
            S.drain("dv")

        def M(k):
            return mats[0:64, 64 * k:64 * k + 64]

        def R_(off, ln=64):
            return rows[0:1, off:off + ln]

        def col(k):
            return cols[0:64, k:k + 1]

        LR, LI, PR, PI, QR, QI, CTR, CTI, BPR = (64 * i for i in range(9))
        DLT, HS, NHS = 576, 578, 580
        NDR, NDI, DEN, INVD, FR, FI = 640, 704, 768, 832, 896, 960
        QFR, QFI, UR, UI, T1, T2 = 1024, 1088, 1152, 1216, 1280, 1344
        SWR, SWI, CR_, CI_, WRR, WRI = 1408, 1410, 1412, 1414, 1472, 1536
        E1PR, E1PI, NQI, NUI = 1600, 1664, 1728, 1792
        B0R, B0I = 1856, 1920
        MEAN, VTMP, RSTD = 0, 1024, 2048

        XR, XRT, XI, XIT, XIN, XITN = 0, 1, 2, 3, 4, 5
        YR, YRT, YI, YIT, YIN, YITN = 6, 7, 8, 9, 10, 11
        E1R, E1I, E1RT, E1IT, E1ITN = 12, 13, 14, 15, 16
        TM1, TM2, TM3, TM4 = 17, 18, 19, 20
        WBR, WBI = 21, 22
        X8TR, X8TI = 23, 24
        D_R, D_IN, D_RT, D_ITN, X8TR64, X8TI64 = 0, 1, 2, 3, 4, 5

        def M64(k):
            return mats[64:128, 64 * k:64 * k + 64]

        CPR, CPI, CBP, CB0R, CB0I, CBBR, CBBI, CNPI = range(8)

        dma("gp", "PAR", svec[:, :], d_sv)
        dma("gp", "PAR", vctr[:, :], d_vr)
        dma("gp", "PAR", vcti[:, :], d_vi)
        dma("gp", "PAR", i128[:, :], d_i128)
        dma("gp", "PAR", j128[:, :], d_j128)
        dma("gp", "PAR", onesr[:, :], d_ones)
        dma("gp", "PAR", onescb[:, :], d_onescb)
        dma("gp", "PAR", rep8[:, :], d_rep8)
        dma("gp", "PAR", io15[:, :], d_io15)
        dma("gp", "PAR", emi[:, :], d_em)
        dma("gp", "PAR", rb15[:, :], d_rb)
        W_PAR_LAST = dma("gp", "PAR", bb128[:, :], d_bb)
        W_PAR = (W_PAR_LAST,)

        ac_sq = {}
        pe_st = {}
        x1g = {}
        x1 = {}

        def emit_x1g(g):
            w = [] if g < 2 else                 [("AC", (lambda f=g: ac_sq[2 * (f - 2) + 1])),
                 ("PE", (lambda f=g: pe_st[2 * (f - 2) + 1]))]
            sl = (g % 2) * 2 * ntok
            x1g[g] = dma("gp", "X1" + "ABCD"[g % 4],
                         xb16[:, sl:sl + 2 * ntok],
                         bass.AP(d_x.tensor, 2 * g * 128 * ntok,
                                 [[ntok, 128], [128 * ntok, 2], [1, ntok]]),
                         waits=w)
            x1[2 * g] = x1g[g]
            x1[2 * g + 1] = x1g[g]

        emit_x1g(0)
        emit_x1g(1)
        for g in range(4):
            dma("gp", "PARB", atb[:, g * 1024:(g + 1) * 1024],
                bass.AP(d_at.tensor, g * 8 * 128 * 128,
                        [[128, 128], [128 * 128, 8], [1, 128]]))
        W_PARB_LAST = dma("gp", "PARB", bpb[:, :], d_bp)
        SC_RWT = None
        for g in range(4):
            SC_RWT = dma("gp", "RW", rwt[:, g * 8 * 15:(g + 1) * 8 * 15],
                         bass.AP(d_rwt.tensor, g * 8 * 128 * 15,
                                 [[15, 128], [128 * 15, 8], [1, 15]]))
        for g in range(2, NFC // 2):
            emit_x1g(g)

        EXP = {}

        def _constr_gen():
            def cdv(fn, waits=()):
                r = S.emit("dv", fn, waits=waits, inc="DV")
                S.drain("dv")
                return r

            ac(lambda e: e.activation(R_(DLT, 1), svec[0:1, 577:578], AF.Exp), waits=W_PAR)
            S.drain("ac")
            ac(lambda e: e.activation(R_(HS, 1), R_(DLT, 1), AF.Copy, scale=0.5))
            S.drain("ac")
            AC_H = ac(lambda e: e.activation(R_(NHS, 1), R_(DLT, 1), AF.Copy, scale=-0.5))
            S.drain("ac")
            pe(lambda e: e.matmul(pb[6][0:128, 4:5], onesr[0:1, 0:128], R_(HS, 1),
                                  start=True, stop=True), waits=(("AC", AC_H),))
            pe(lambda e: e.matmul(pb[6][0:128, 5:6], onesr[0:1, 0:128], R_(NHS, 1),
                                  start=True, stop=True))
            PE_BC = pe(lambda e: e.matmul(pb[6][0:128, 6:7], onesr[0:1, 0:128], R_(DLT, 1),
                                          start=True, stop=True))
            cdv(lambda e: e.tensor_copy(sc64[:, 0:1], pb[6][0:128, 4:5]), waits=(("PE", PE_BC),))
            cdv(lambda e: e.tensor_copy(sc64[:, 1:2], pb[6][0:128, 5:6]))
            cdv(lambda e: e.tensor_copy(sc64[:, 2:3], pb[6][0:128, 6:7]))
            dvd()
            yield
            for dst, src in ((LR, 0), (LI, 64), (PR, 128), (PI, 192), (QR, 256),
                             (QI, 320), (CTR, 384), (CTI, 448), (BPR, 512)):
                cdv(lambda e, d=dst, s=src: e.tensor_copy(R_(d), svec[0:1, s:s + 64]),
                   waits=W_PAR)
            dvd()
            yield
            cdv(lambda e: e.tensor_scalar(R_(NDR), R_(LR), R_(HS, 1), -1.0, ALU.mult, ALU.add),
               waits=(("AC", AC_H),))
            cdv(lambda e: e.tensor_scalar(R_(NDI), R_(LI), R_(HS, 1), None, ALU.mult))
            dvd()
            yield
            cdv(lambda e: e.tensor_tensor(R_(DEN), R_(NDR), R_(NDR), ALU.mult))
            cdv(lambda e: e.tensor_tensor(R_(T1), R_(NDI), R_(NDI), ALU.mult))
            dvd()
            yield
            cdv(lambda e: e.tensor_tensor(R_(DEN), R_(DEN), R_(T1), ALU.add))
            dvd()
            yield
            cdv(lambda e: e.reciprocal(R_(INVD), R_(DEN)))
            dvd()
            yield
            cdv(lambda e: e.scalar_tensor_tensor(R_(FR), R_(NDR), -1.0, R_(INVD),
                                                ALU.mult, ALU.mult))
            cdv(lambda e: e.tensor_tensor(R_(FI), R_(NDI), R_(INVD), ALU.mult))
            dvd()
            yield
            cdv(lambda e: e.tensor_tensor(R_(QFR), R_(QR), R_(FR), ALU.mult))
            cdv(lambda e: e.tensor_tensor(R_(T1), R_(QI), R_(FI), ALU.mult))
            cdv(lambda e: e.tensor_tensor(R_(QFI), R_(QR), R_(FI), ALU.mult))
            cdv(lambda e: e.tensor_tensor(R_(T2), R_(QI), R_(FR), ALU.mult))
            dvd()
            yield
            cdv(lambda e: e.tensor_tensor(R_(QFR), R_(QFR), R_(T1), ALU.add))
            cdv(lambda e: e.tensor_tensor(R_(QFI), R_(QFI), R_(T2), ALU.subtract))
            cdv(lambda e: e.tensor_tensor(R_(UR), R_(PR), R_(FR), ALU.mult))
            cdv(lambda e: e.tensor_tensor(R_(T1), R_(PI), R_(FI), ALU.mult))
            cdv(lambda e: e.tensor_tensor(R_(UI), R_(PR), R_(FI), ALU.mult))
            cdv(lambda e: e.tensor_tensor(R_(T2), R_(PI), R_(FR), ALU.mult))
            dvd()
            yield
            cdv(lambda e: e.tensor_tensor(R_(UR), R_(UR), R_(T1), ALU.subtract))
            cdv(lambda e: e.tensor_tensor(R_(UI), R_(UI), R_(T2), ALU.add))
            dvd()
            yield
            cdv(lambda e: e.tensor_tensor(R_(T1), R_(QFR), R_(PR), ALU.mult))
            cdv(lambda e: e.tensor_tensor(R_(T2), R_(QFI), R_(PI), ALU.mult))
            dvd()
            yield
            cdv(lambda e: e.tensor_tensor(R_(T1), R_(T1), R_(T2), ALU.subtract))
            dvd()
            yield
            cdv(lambda e: e.tensor_reduce(R_(SWR, 1), R_(T1), AX.X, ALU.add))
            cdv(lambda e: e.tensor_tensor(R_(T1), R_(QFR), R_(PI), ALU.mult))
            cdv(lambda e: e.tensor_tensor(R_(T2), R_(QFI), R_(PR), ALU.mult))
            dvd()
            yield
            cdv(lambda e: e.tensor_tensor(R_(T1), R_(T1), R_(T2), ALU.add))
            dvd()
            yield
            cdv(lambda e: e.tensor_reduce(R_(SWI, 1), R_(T1), AX.X, ALU.add))
            dvd()
            yield
            cdv(lambda e: e.tensor_scalar(R_(CR_, 1), R_(SWR, 1), R_(HS, 1), 1.0,
                                         ALU.mult, ALU.add))
            cdv(lambda e: e.tensor_scalar(R_(CI_, 1), R_(SWI, 1), R_(HS, 1), None, ALU.mult))
            dvd()
            yield
            cdv(lambda e: e.tensor_tensor(R_(T1, 1), R_(CR_, 1), R_(CR_, 1), ALU.mult))
            cdv(lambda e: e.tensor_tensor(R_(T2, 1), R_(CI_, 1), R_(CI_, 1), ALU.mult))
            dvd()
            yield
            cdv(lambda e: e.tensor_tensor(R_(T1, 1), R_(T1, 1), R_(T2, 1), ALU.add))
            dvd()
            yield
            cdv(lambda e: e.reciprocal(R_(T2, 1), R_(T1, 1)))
            dvd()
            yield
            cdv(lambda e: e.tensor_tensor(R_(T1, 1), R_(CR_, 1), R_(T2, 1), ALU.mult))
            dvd()
            yield
            cdv(lambda e: e.tensor_scalar(R_(CR_, 1), R_(T1, 1), R_(HS, 1), None, ALU.mult))
            cdv(lambda e: e.tensor_tensor(R_(T1, 1), R_(CI_, 1), R_(T2, 1), ALU.mult))
            dvd()
            yield
            cdv(lambda e: e.tensor_scalar(R_(CI_, 1), R_(T1, 1), R_(NHS, 1), None, ALU.mult))
            dvd()
            yield
            cdv(lambda e: e.tensor_scalar(R_(T1), R_(UR), R_(CR_, 1), None, ALU.mult))
            cdv(lambda e: e.tensor_scalar(R_(T2), R_(UI), R_(CI_, 1), None, ALU.mult))
            dvd()
            yield
            cdv(lambda e: e.tensor_tensor(R_(T1), R_(T1), R_(T2), ALU.subtract))
            cdv(lambda e: e.tensor_scalar(R_(T2), R_(UR), R_(CI_, 1), None, ALU.mult))
            dvd()
            yield
            cdv(lambda e: e.tensor_scalar(R_(UI), R_(UI), R_(CR_, 1), None, ALU.mult))
            dvd()
            yield
            cdv(lambda e: e.tensor_tensor(R_(UI), R_(UI), R_(T2), ALU.add))
            cdv(lambda e: e.tensor_copy(R_(UR), R_(T1)))
            dvd()
            yield
            cdv(lambda e: e.tensor_scalar(R_(NUI), R_(UI), -1.0, None, ALU.mult))
            DV_WOOD = cdv(lambda e: e.tensor_scalar(R_(NQI), R_(QI), -1.0, None, ALU.mult))
            dvd()
            yield

            PE_DG = pe(lambda e: e.matmul(pb[6][0:64, 0:64], onesr[0:1, 0:64], R_(FR),
                                          start=True, stop=True), waits=(("DV", DV_WOOD),))
            pe(lambda e: e.matmul(pb[6][0:64, 64:128], onesr[0:1, 0:64], R_(FI),
                                  start=True, stop=True))
            pe(lambda e: e.matmul(pb[7][0:64, 0:64], R_(UR), R_(QFR), start=True, stop=False))
            pe(lambda e: e.matmul(pb[7][0:64, 0:64], R_(NUI), R_(QFI), start=False, stop=True))
            pe(lambda e: e.matmul(pb[7][0:64, 64:128], R_(UR), R_(QFI), start=True, stop=False))
            PE_OUT = pe(lambda e: e.matmul(pb[7][0:64, 64:128], R_(UI), R_(QFR),
                                           start=False, stop=True))
            cdv(lambda e: e.tensor_tensor(M(TM1), pb[6][0:64, 0:64], i128[0:64, 0:64], ALU.mult),
               waits=(("PE", PE_OUT),))
            cdv(lambda e: e.tensor_tensor(M(TM2), pb[6][0:64, 64:128], i128[0:64, 0:64], ALU.mult))
            dvd()
            yield
            cdv(lambda e: e.tensor_tensor(M(E1R), M(TM1), pb[7][0:64, 0:64], ALU.subtract))
            DV_E1 = cdv(lambda e: e.tensor_tensor(M(E1I), M(TM2), pb[7][0:64, 64:128],
                                                 ALU.subtract))
            dvd()
            yield
            pe(lambda e: e.transpose(pb[6][0:64, 0:64], M(E1R), i128[0:64, 0:64]),
               waits=(("DV", DV_E1),))
            PE_E1T = pe(lambda e: e.transpose(pb[6][0:64, 64:128], M(E1I), i128[0:64, 0:64]))
            cdv(lambda e: e.tensor_copy(M(E1RT), pb[6][0:64, 0:64]), waits=(("PE", PE_E1T),))
            cdv(lambda e: e.tensor_copy(M(E1IT), pb[6][0:64, 64:128]))
            dvd()
            yield
            DV_E1TN = cdv(lambda e: e.tensor_scalar(M(E1ITN), M(E1IT), -1.0, None, ALU.mult))
            dvd()
            yield
            pe(lambda e: e.transpose(pb[6][0:64, 0:1], R_(PR), i128[0:1, 0:1]),
               waits=(("DV", DV_E1TN),))
            pe(lambda e: e.transpose(pb[6][0:64, 1:2], R_(PI), i128[0:1, 0:1]))
            PE_COLS = pe(lambda e: e.transpose(pb[6][0:64, 2:3], R_(BPR), i128[0:1, 0:1]))
            cdv(lambda e: e.tensor_copy(col(CPR), pb[6][0:64, 0:1]), waits=(("PE", PE_COLS),))
            cdv(lambda e: e.tensor_copy(col(CPI), pb[6][0:64, 1:2]))
            cdv(lambda e: e.tensor_copy(col(CBP), pb[6][0:64, 2:3]))
            cdv(lambda e: e.tensor_copy(krow[0:1, 0:64], R_(CTR)))
            DV_CT = cdv(lambda e: e.tensor_scalar(krow[0:1, 64:128], R_(CTI), -1.0,
                                                 None, ALU.mult))
            PE_CT = pe(lambda e: e.transpose(pb[6][0:128, 8:9], krow[0:1, 0:128],
                                             i128[0:1, 0:1]), waits=(("DV", DV_CT),))
            cdv(lambda e: e.tensor_copy(big32[:, 0:1], pb[6][0:128, 8:9]),
               waits=(("PE", PE_CT),))
            DV_NPI = cdv(lambda e: e.tensor_scalar(col(CNPI), col(CPI), -1.0, None, ALU.mult))
            pe(lambda e: e.matmul(pb[7][0:1, 0:64], col(CPR), M(E1RT), start=True, stop=False),
               waits=(("DV", DV_NPI),))
            pe(lambda e: e.matmul(pb[7][0:1, 0:64], col(CNPI), M(E1IT), start=False, stop=True))
            pe(lambda e: e.matmul(pb[7][0:1, 64:128], col(CPR), M(E1IT), start=True, stop=False))
            PE_E1P = pe(lambda e: e.matmul(pb[7][0:1, 64:128], col(CPI), M(E1RT),
                                           start=False, stop=True))
            cdv(lambda e: e.tensor_copy(R_(E1PR), pb[7][0:1, 0:64]), waits=(("PE", PE_E1P),))
            cdv(lambda e: e.tensor_copy(R_(E1PI), pb[7][0:1, 64:128]))
            cdv(lambda e: e.tensor_scalar(R_(WRR), R_(NDR), 2.0, None, ALU.add))
            DV_WR = cdv(lambda e: e.tensor_copy(R_(WRI), R_(NDI)))
            pe(lambda e: e.matmul(pb[6][0:64, 0:64], onesr[0:1, 0:64], R_(WRR),
                                  start=True, stop=True), waits=(("DV", DV_WR),))
            PE_WB = pe(lambda e: e.matmul(pb[6][0:64, 64:128], onesr[0:1, 0:64], R_(WRI),
                                          start=True, stop=True))
            cdv(lambda e: e.tensor_copy(M(WBR), pb[6][0:64, 0:64]), waits=(("PE", PE_WB),))
            cdv(lambda e: e.tensor_copy(M(WBI), pb[6][0:64, 64:128]))
            cdv(lambda e: e.tensor_tensor(M(TM1), M(E1R), M(WBR), ALU.mult))
            cdv(lambda e: e.tensor_tensor(M(TM2), M(E1I), M(WBI), ALU.mult))
            cdv(lambda e: e.tensor_tensor(M(TM3), M(E1R), M(WBI), ALU.mult))
            cdv(lambda e: e.tensor_tensor(M(TM4), M(E1I), M(WBR), ALU.mult))
            cdv(lambda e: e.tensor_tensor(M(TM1), M(TM1), M(TM2), ALU.subtract))
            DV_CSC = cdv(lambda e: e.tensor_tensor(M(TM3), M(TM3), M(TM4), ALU.add))
            pe(lambda e: e.matmul(pb[7][0:64, 0:64], R_(E1PR), R_(QR), start=True, stop=False),
               waits=(("DV", DV_CSC),))
            pe(lambda e: e.matmul(pb[7][0:64, 0:64], R_(E1PI), R_(QI), start=False, stop=True))
            pe(lambda e: e.matmul(pb[7][0:64, 64:128], R_(E1PR), R_(NQI), start=True, stop=False))
            PE_HO = pe(lambda e: e.matmul(pb[7][0:64, 64:128], R_(E1PI), R_(QR),
                                          start=False, stop=True))
            cdv(lambda e: e.scalar_tensor_tensor(M(XR), pb[7][0:64, 0:64], sc64[0:64, 1:2],
                                                M(TM1), ALU.mult, ALU.add),
               waits=(("PE", PE_HO),))
            DV_AB = cdv(lambda e: e.scalar_tensor_tensor(M(XI), pb[7][0:64, 64:128],
                                                        sc64[0:64, 1:2], M(TM3),
                                                        ALU.mult, ALU.add))
            pe(lambda e: e.matmul(pb[6][0:1, 0:64], col(CBP), vctr[:, :], start=True,
                                  stop=True), waits=(("DV", DV_AB),))
            PE_B0 = pe(lambda e: e.matmul(pb[6][0:1, 64:128], col(CBP), vcti[:, :],
                                          start=True, stop=True))
            cdv(lambda e: e.tensor_copy(R_(B0R), pb[6][0:1, 0:64]), waits=(("PE", PE_B0),))
            DV_B0 = cdv(lambda e: e.tensor_copy(R_(B0I), pb[6][0:1, 64:128]))
            pe(lambda e: e.transpose(pb[6][0:64, 0:1], R_(B0R), i128[0:1, 0:1]),
               waits=(("DV", DV_B0),))
            PE_B0C = pe(lambda e: e.transpose(pb[6][0:64, 1:2], R_(B0I), i128[0:1, 0:1]))
            cdv(lambda e: e.tensor_copy(col(CB0R), pb[6][0:64, 0:1]), waits=(("PE", PE_B0C),))
            DV_B0C = cdv(lambda e: e.tensor_copy(col(CB0I), pb[6][0:64, 1:2]))
            pe(lambda e: e.matmul(pb[7][0:64, 0:1], M(E1RT), col(CB0R), start=True, stop=False),
               waits=(("DV", DV_B0C),))
            pe(lambda e: e.matmul(pb[7][0:64, 0:1], M(E1ITN), col(CB0I), start=False, stop=True))
            pe(lambda e: e.matmul(pb[7][0:64, 1:2], M(E1IT), col(CB0R), start=True, stop=False))
            PE_BB = pe(lambda e: e.matmul(pb[7][0:64, 1:2], M(E1RT), col(CB0I),
                                          start=False, stop=True))
            cdv(lambda e: e.tensor_scalar(col(CBBR), pb[7][0:64, 0:1], sc64[0:64, 2:3], None,
                                         ALU.mult), waits=(("PE", PE_BB),))
            DV_BB = cdv(lambda e: e.tensor_scalar(col(CBBI), pb[7][0:64, 1:2], sc64[0:64, 2:3],
                                                 None, ALU.mult))
            pe(lambda e: e.transpose(pb[6][0:64, 0:64], M(XR), i128[0:64, 0:64]),
               waits=(("DV", DV_BB),))
            PE_X0T = pe(lambda e: e.transpose(pb[6][0:64, 64:128], M(XI), i128[0:64, 0:64]))
            cdv(lambda e: e.tensor_copy(M(XRT), pb[6][0:64, 0:64]), waits=(("PE", PE_X0T),))
            cdv(lambda e: e.tensor_copy(M(XIT), pb[6][0:64, 64:128]))
            cdv(lambda e: e.tensor_scalar(M(XITN), M(XIT), -1.0, None, ALU.mult))
            DV_LAST = cdv(lambda e: e.tensor_scalar(M(XIN), M(XI), -1.0, None, ALU.mult))

            STG = mats[0:64, 64 * TM3:64 * TM3 + 128]

            def packer(c, target):
                nonlocal DV_LAST, DV_PKC
                blocks = []
                if target in ("rop", "both"):
                    blocks += [(rop, 0, c["rt"], c["itn"]), (rop, 1, c["it"], c["rt"])]
                if target in ("wop", "both"):
                    blocks += [(wop, 0, c["r"], c["inn"]), (wop, 1, c["i"], c["r"])]
                first = True
                for idx, (dstT, cb, p1, p2) in enumerate(blocks):
                    dv(lambda e, idx=idx, p1=p1: e.tensor_copy(
                        pkstg[:, idx * 128:idx * 128 + 64], M(p1)),
                       waits=(("DV", DV_LAST),) if first else ())
                    first = False
                    DV_S = dv(lambda e, idx=idx, p2=p2: e.tensor_copy(
                        pkstg[:, idx * 128 + 64:idx * 128 + 128], M(p2)))
                DV_PKC = DV_S
                dvd()
                PE_P = 0
                for idx, (dstT, cb, p1, p2) in enumerate(blocks):
                    PE_P = pe(lambda e, idx=idx: e.transpose(
                        pb[5][0:128, idx * 64:(idx + 1) * 64],
                        pkstg[:, idx * 128:(idx + 1) * 128], i128[0:64, 0:64]),
                        waits=(("DV", DV_S),) if idx == 0 else ())
                for idx, (dstT, cb, p1, p2) in enumerate(blocks):
                    DV_LAST = dv(lambda e, idx=idx, dstT=dstT, cb=cb: e.tensor_copy(
                        dstT[:, cb * 64:(cb + 1) * 64],
                        pb[5][0:128, idx * 64:(idx + 1) * 64]),
                        waits=(("PE", PE_P),) if idx == 0 else ())
                dvd()

            def stack2(dst, cr, ci, neg_i):
                nonlocal DV_LAST
                pe(lambda e, cr=cr: e.transpose(pb[5][0:1, 384:448], cr, i128[0:64, 0:64]),
                   waits=(("DV", DV_LAST),))
                PE_T = pe(lambda e, ci=ci: e.transpose(pb[5][0:1, 448:512], ci,
                                                       i128[0:64, 0:64]))
                cdv(lambda e: e.tensor_copy(krow[0:1, 0:64], pb[5][0:1, 384:448]),
                   waits=(("PE", PE_T),))
                DV_R = cdv(lambda e: e.tensor_scalar(krow[0:1, 64:128], pb[5][0:1, 448:512],
                                                    -1.0 if neg_i else 1.0, None, ALU.mult))
                PE_T2 = pe(lambda e: e.transpose(pb[5][0:128, 128:129], krow[0:1, 0:128],
                                                 i128[0:1, 0:1]), waits=(("DV", DV_R),))
                DV_LAST = cdv(lambda e, dst=dst: e.tensor_copy(dst, pb[5][0:128, 128:129]),
                             waits=(("PE", PE_T2),))

            def sqfuse(c, n, wait=None):
                nonlocal DV_LAST, DV_SQL
                pe(lambda e, c=c: e.matmul(pb[7][0:64, 0:64], M(c["rt"]), M(c["r"]),
                                           start=True, stop=False),
                   waits=(("DV", DV_LAST if wait is None else wait),))
                pe(lambda e, c=c: e.matmul(pb[7][0:64, 0:64], M(c["itn"]), M(c["i"]),
                                           start=False, stop=True))
                pe(lambda e, c=c: e.matmul(pb[4][0:64, 0:64], M(c["rt"]), M(c["i"]),
                                           start=True, stop=False))
                pe(lambda e, c=c: e.matmul(pb[4][0:64, 0:64], M(c["it"]), M(c["r"]),
                                           start=False, stop=True))
                pe(lambda e, c=c: e.matmul(pb[7][0:64, 64:128], M(c["r"]), M(c["rt"]),
                                           start=True, stop=False))
                pe(lambda e, c=c: e.matmul(pb[7][0:64, 64:128], M(c["inn"]), M(c["it"]),
                                           start=False, stop=True))
                pe(lambda e, c=c: e.matmul(pb[4][0:64, 64:128], M(c["r"]), M(c["it"]),
                                           start=True, stop=False))
                PE_SQ = pe(lambda e, c=c: e.matmul(pb[4][0:64, 64:128], M(c["i"]),
                                                   M(c["rt"]), start=False, stop=True))
                dv(lambda e, n=n: e.tensor_copy(M(n["r"]), pb[7][0:64, 0:64]),
                   waits=(("PE", PE_SQ),))
                dv(lambda e, n=n: e.tensor_copy(M(n["i"]), pb[7][0:64, 64:128]))
                dv(lambda e, n=n: e.tensor_copy(M(n["rt"]), pb[4][0:64, 0:64]))
                dv(lambda e, n=n: e.tensor_copy(M(n["it"]), pb[4][0:64, 64:128]))
                dvd()
                dv(lambda e, n=n: e.tensor_scalar(M(n["itn"]), M(n["it"]), -1.0, None,
                                                  ALU.mult))
                DV_LAST = dv(lambda e, n=n: e.tensor_scalar(M(n["inn"]), M(n["i"]), -1.0,
                                                            None, ALU.mult))
                DV_SQL = DV_LAST
                dvd()

            DV_SQL = 0
            DV_PKC = 0
            cur = dict(r=XR, i=XI, rt=XRT, it=XIT, itn=XITN, inn=XIN)
            nxt = dict(r=YR, i=YI, rt=YRT, it=YIT, itn=YITN, inn=YIN)

            packer(cur, "wop")
            stack2(bstk[:, 0:1], col(CBBR), col(CBBI), False)
            PE_W0 = pe(lambda e: e.matmul(pb[7][0:128, 0:1], wop[:, :], bstk[:, 0:1],
                                          start=True, stop=True), waits=(("DV", DV_LAST),))
            DV_LAST = cdv(lambda e: e.tensor_copy(wpr[:, 0:1], pb[7][0:128, 0:1]),
                         waits=(("PE", PE_W0),))

            setC = dict(r=12, rt=13, i=14, it=15, inn=16, itn=17)
            sets = [cur, nxt, setC]

            def blocks_of(c, target):
                b = []
                if target in ("rop", "both"):
                    b += [(rop, 0, c["rt"], c["itn"]), (rop, 1, c["it"], c["rt"])]
                if target in ("wop", "both"):
                    b += [(wop, 0, c["r"], c["inn"]), (wop, 1, c["i"], c["r"])]
                return b

            def pk_copy(c, target, w0=()):
                bl = blocks_of(c, target)
                AC_S = 0
                for idx, (dstT, cb, p1, p2) in enumerate(bl):
                    ac(lambda e, idx=idx, p1=p1: e.activation(
                        pkstg[:, idx * 128:idx * 128 + 64], M(p1), AF.Copy),
                       waits=w0 if idx == 0 else ())
                    AC_S = ac(lambda e, idx=idx, p2=p2: e.activation(
                        pkstg[:, idx * 128 + 64:idx * 128 + 128], M(p2), AF.Copy))
                S.drain("ac")
                return AC_S

            def pk_t(c, target, ac_s, dv_pke):
                bl = blocks_of(c, target)
                PE_P = 0
                for idx in range(len(bl)):
                    PE_P = pe(lambda e, idx=idx: e.transpose(
                        pb[5][0:128, idx * 64:(idx + 1) * 64],
                        pkstg[:, idx * 128:(idx + 1) * 128], i128[0:64, 0:64]),
                        waits=(("AC", ac_s), ("DV", dv_pke)) if idx == 0 else ())
                return PE_P

            def pk_evac(c, target, pe_p):
                nonlocal DV_LAST
                bl = blocks_of(c, target)
                first = True
                o = 0
                while o < len(bl):
                    dstT = bl[o][0]
                    DV_LAST = dv(lambda e, o=o, dstT=dstT: e.tensor_copy(
                        dstT[:, 0:128], pb[5][0:128, o * 64:o * 64 + 128]),
                        waits=(("PE", pe_p),) if first else ())
                    first = False
                    o += 2
                dvd()
                return DV_LAST

            def sq_mm(c, wait):
                pe(lambda e, c=c: e.matmul(pb[7][0:64, 0:64], M(c["rt"]), M(c["r"]),
                                           start=True, stop=False),
                   waits=(("DV", wait),))
                pe(lambda e, c=c: e.matmul(pb[7][0:64, 0:64], M(c["itn"]), M(c["i"]),
                                           start=False, stop=True))
                pe(lambda e, c=c: e.matmul(pb[4][0:64, 0:64], M(c["rt"]), M(c["i"]),
                                           start=True, stop=False))
                pe(lambda e, c=c: e.matmul(pb[4][0:64, 0:64], M(c["it"]), M(c["r"]),
                                           start=False, stop=True))
                pe(lambda e, c=c: e.matmul(pb[7][0:64, 64:128], M(c["r"]), M(c["rt"]),
                                           start=True, stop=False))
                pe(lambda e, c=c: e.matmul(pb[7][0:64, 64:128], M(c["inn"]), M(c["it"]),
                                           start=False, stop=True))
                pe(lambda e, c=c: e.matmul(pb[4][0:64, 64:128], M(c["r"]), M(c["it"]),
                                           start=True, stop=False))
                return pe(lambda e, c=c: e.matmul(pb[4][0:64, 64:128], M(c["i"]),
                                                  M(c["rt"]), start=False, stop=True))

            def M2(k):
                return mats[0:64, 64 * k:64 * k + 128]

            def sq_evac(n, pe_sq, acw=0):
                nonlocal DV_LAST
                dv(lambda e, n=n: e.tensor_copy(M2(n["r"]), pb[7][0:64, 0:128]),
                   waits=(("PE", pe_sq), ("AC", acw)))
                dv(lambda e, n=n: e.tensor_copy(M2(n["i"]), pb[4][0:64, 0:128]))
                dvd()
                DV_LAST = dv(lambda e, n=n: e.tensor_scalar(M2(n["inn"]), M2(n["i"]),
                                                            -1.0, None, ALU.mult))
                dvd()
                return DV_LAST

            def tgt(k):
                return "both" if k <= 8 else ("rop" if k <= 11 else "wop")

            DV_NEG = DV_LAST
            AC_PKC = {-1: 0, -2: 0}
            PKE_H = {-1: 0}
            PKT_H = {-1: 0}
            DVN_H = {-1: DV_LAST}
            PE_SQn = {}
            for k in range(min(18, stage + 1)):
                ck = sets[k % 3]
                nk = sets[(k + 1) % 3]
                km = k - 1
                cm = sets[km % 3] if km >= 0 else None
                if k < 16:
                    PE_SQn[k] = sq_mm(ck, DV_NEG)
                if km >= 0:
                    PE_PKT = pk_t(cm, tgt(km), AC_PKC[km], PKE_H[km - 1])
                    PKT_H[km] = PE_PKT
                if k < 16:
                    DV_NEG = sq_evac(nk, PE_SQn[k], AC_PKC[k - 2])
                    DVN_H[k] = DV_NEG
                if km >= 0:
                    DV_PKE = pk_evac(cm, tgt(km), PE_PKT)
                    PKE_H[km] = DV_PKE
                if km >= 0 and km <= 11:
                    m = 1 << km
                    ngrp = (m + 511) // 512
                    pe_rg = []
                    dv_rg = []
                    for g in range(ngrp):
                        w = min(512, m - g * 512)
                        o0 = g * 512
                        wt = DV_PKE if g == 0 else dv_rg[g - 1]
                        pe_rg.append(pe(lambda e, o0=o0, w=w: e.matmul(
                            pb[6][0:128, 0:w], rop[:, :], big32[:, o0:o0 + w],
                            start=True, stop=True), waits=(("DV", wt),)))
                        dv_rg.append(dv(lambda e, o0=o0, w=w, m=m: e.tensor_copy(
                            big32[:, m + o0:m + o0 + w], pb[6][0:128, 0:w]),
                            waits=(("PE", pe_rg[g]),)))
                        dvd()
                    DV_LAST = dv_rg[-1]
                    if km < 8:
                        S.ops["pe"].append(lambda e, sems: e.drain())
                        PE_W = pe(lambda e, m=m: e.matmul(
                            pb[6][0:128, 256:256 + m], wop[:, :], wpr[:, 0:m],
                            start=True, stop=True), waits=(("DV", DV_PKE),))
                        DV_LAST = dv(lambda e, m=m: e.tensor_copy(
                            wpr[:, m:2 * m], pb[6][0:128, 256:256 + m]),
                            waits=(("PE", PE_W),))
                        dvd()
                    if km == 8:
                        DV_LAST = dv(lambda e: e.tensor_copy(ppk[:, :], wop[:, :]))
                        dvd()
                if km >= 12:
                    PE_NW = pe(lambda e: e.matmul(pb[6][0:128, 0:256], wop[:, :],
                                                  wpr[:, :], start=True, stop=True),
                               waits=(("DV", DV_PKE),))
                    pe(lambda e, c=cm: e.matmul(pb[6][0:64, 256:257], M(c["rt"]),
                                                col(CBBR), start=True, stop=False))
                    pe(lambda e, c=cm: e.matmul(pb[6][0:64, 256:257], M(c["itn"]),
                                                col(CBBI), start=False, stop=True))
                    pe(lambda e, c=cm: e.matmul(pb[6][0:64, 257:258], M(c["it"]),
                                                col(CBBR), start=True, stop=False))
                    PE_NM = pe(lambda e, c=cm: e.matmul(pb[6][0:64, 257:258], M(c["rt"]),
                                                        col(CBBI), start=False, stop=True))
                    dv(lambda e: e.tensor_tensor(wpr[:, :], wpr[:, :], pb[6][0:128, 0:256],
                                                 ALU.add), waits=(("PE", PE_NM),))
                    dv(lambda e: e.tensor_tensor(col(CBBR), col(CBBR),
                                                 pb[6][0:64, 256:257], ALU.add))
                    DV_LAST = dv(lambda e: e.tensor_tensor(col(CBBI), col(CBBI),
                                                           pb[6][0:64, 257:258], ALU.add))
                    dvd()
                if k <= 16:
                    AC_PKC[k] = pk_copy(ck, tgt(k), w0=(
                        ("DV", DVN_H[k - 1]), ("PE", PKT_H[km])))
                yield

            if stage < 18:
                SP_KPAD = ("K", 0)
                cdv(lambda e: e.memset(stb[:, :], 0.0))
                cdv(lambda e: e.tensor_copy(emf[:, :], emi[:, :]), waits=W_PAR)
                DV_CONSTR = cdv(lambda e: e.tensor_scalar(mmapb[:, :], io15[:, :],
                                                          emf[:, 0:1], 2.0,
                                                          ALU.is_equal, ALU.mult))
                EXP["SP_KPAD"] = SP_KPAD
                EXP["K_ODD"] = ("K2", 0)
                EXP["DV_CONSTR"] = DV_CONSTR
                yield
                return
            stack2(bstk[:, 0:1], col(CBBR), col(CBBI), True)
            DV_BSTK = DV_LAST
            pe_k = {}
            dv_ke = {}
            kdma = {}
            for g in range(8):
                bank = 6 if g % 2 == 0 else 7
                ks = (g % 2) * 512
                wk_pe = (("DV", DV_BSTK),) if g < 2 else (("DV", dv_ke[g - 2]),)
                pe_k[g] = pe(lambda e, g=g, bank=bank: e.matmul(
                    pb[bank][0:1, 0:512], bstk[:, :],
                    big32[:, g * 512:(g + 1) * 512],
                    start=True, stop=True), waits=wk_pe)
                wk = [("PE", pe_k[g])] + ([kdma[g - 2]] if g >= 2 else [])
                dv_ke[g] = cdv(lambda e, bank=bank, ks=ks: e.tensor_copy(
                    krow[0:1, ks:ks + 512], pb[bank][0:1, 0:512]), waits=wk)
                if g == 0:
                    dv_ke[g] = cdv(lambda e: e.tensor_tensor(
                        krow[0:1, 0:1], krow[0:1, 0:1], svec[0:1, 576:577], ALU.add))
                kdma[g] = dma("sp", "K" if g % 2 == 0 else "K2",
                              bass.AP(d_kscr, 128 + g * 512, [[512, 1], [1, 512]]),
                              krow[0:1, ks:ks + 512], waits=(("DV", dv_ke[g]),))
            DV_Z = cdv(lambda e: e.memset(krow[0:1, 0:512], 0.0), waits=(kdma[6],))
            dma("sp", "K", bass.AP(d_kscr, 0, [[128, 1], [1, 128]]), krow[0:1, 0:128],
                waits=(("DV", DV_Z),))
            SP_KPAD = dma("sp", "K", bass.AP(d_kscr, 4224, [[384, 1], [1, 384]]),
                          krow[0:1, 0:384])
            EXP["K_ODD"] = kdma[7]

            cdv(lambda e: e.tensor_copy(moutt[0:64, 0:256], big32[0:64, 0:256]))
            DV_MO = cdv(lambda e: e.tensor_scalar(moutt[64:128, 0:256], big32[64:128, 0:256],
                                                 -1.0, None, ALU.mult))
            for (sub, c0) in ((0, 128), (1, 0)):
                pe(lambda e, c0=c0: e.transpose(pb[6][0:128, 0:64], wpr[0:64, c0:c0 + 128],
                                                i128[0:64, 0:64]), waits=(("DV", DV_MO),))
                S.ops["pe"].append(lambda e, sems: e.drain())
                PE_MT = pe(lambda e, c0=c0: e.transpose(pb[6][0:128, 64:128],
                                                        wpr[64:128, c0:c0 + 128],
                                                        i128[64:128, 64:128]))
                DV_MT = cdv(lambda e: e.tensor_copy(tstg[:, 0:128], pb[6][0:128, 0:128]),
                           waits=(("PE", PE_MT),))
                PE_MJ = pe(lambda e: e.matmul(pb[7][:, 0:128], j128[:, :], tstg[:, 0:128],
                                              start=True, stop=True), waits=(("DV", DV_MT),))
                DV_MO = cdv(lambda e, sub=sub: e.tensor_copy(mint[:, sub * 128:(sub + 1) * 128],
                                                            pb[7][:, 0:128]),
                           waits=(("PE", PE_MJ),))
            DV_MIN = DV_MO
            cdv(lambda e: e.memset(stb[:, :], 0.0))
            cdv(lambda e: e.tensor_copy(emf[:, :], emi[:, :]), waits=W_PAR)
            DV_CONSTR = cdv(lambda e: e.tensor_scalar(mmapb[:, :], io15[:, :], emf[:, 0:1],
                                                     2.0, ALU.is_equal, ALU.mult))


            EXP["SP_KPAD"] = SP_KPAD
            EXP["DV_CONSTR"] = DV_CONSTR
            yield

        _cgen = _constr_gen()

        def pump(n=1):
            for _ in range(n):
                try:
                    next(_cgen)
                except StopIteration:
                    return
        pump(15)
        for fc in range(NFC):
            sl = (fc % 4) * ntok
            ql = (fc % 2) * ntok
            wsq = [x1[fc]]
            if fc >= 2:
                wsq += [("PE", (lambda f=fc: pe_st[f - 2]))]
            ac_sq[fc] = ac(lambda e, sl=sl, ql=ql: e.activation(
                xq16[:, ql:ql + ntok], xb16[:, sl:sl + ntok], AF.Square),
                waits=wsq)
            S.drain("ac")
            for h in range(2):
                pe(lambda e, sl=sl, h=h, fc=fc: e.matmul(
                    pb[h][0:128, 0:H], onescb[:, :],
                    xb16[:, sl + h * H:sl + (h + 1) * H],
                    start=(fc == 0), stop=(fc == NFC - 1)),
                   waits=[x1[fc]] if h == 0 else ())
                pe_st[fc] = pe(lambda e, h=h, fc=fc, ql=ql: e.matmul(
                    pb[2 + h][0:128, 0:H], onescb[:, :],
                    xq16[:, ql + h * H:ql + (h + 1) * H],
                    start=(fc == 0), stop=(fc == NFC - 1)),
                    waits=[("AC", ac_sq[fc])] if h == 0 else ())
            pump(1)
        PE_STATS = pe_st[NFC - 1]
        for h in range(2):
            dv(lambda e, h=h: e.tensor_scalar(mbr[:, h * H:(h + 1) * H],
                                              pb[h][0:128, 0:H],
                                              1.0 / D, None, ALU.mult),
               waits=(("PE", PE_STATS),))
            dv(lambda e, h=h: e.tensor_scalar(mbr[:, ntok + h * H:ntok + (h + 1) * H],
                                              pb[2 + h][0:128, 0:H],
                                              1.0 / D, None, ALU.mult))
        dvd()
        dv(lambda e: e.tensor_tensor(xn32[:, 0:ntok], mbr[:, 0:ntok], mbr[:, 0:ntok],
                                     ALU.mult))
        dvd()
        dv(lambda e: e.tensor_tensor(mbr[:, ntok:2 * ntok], mbr[:, ntok:2 * ntok],
                                     xn32[:, 0:ntok], ALU.subtract))
        dvd()
        DV_VAR = dv(lambda e: e.tensor_scalar(mbr[:, ntok:2 * ntok],
                                              mbr[:, ntok:2 * ntok], 1e-5,
                                              None, ALU.add))
        dvd()
        ac(lambda e: e.activation(xn32[:, ntok:2 * ntok], mbr[:, ntok:2 * ntok], AF.Ln),
           waits=(("DV", DV_VAR),))
        S.drain("ac")
        AC_RSTD = ac(lambda e: e.activation(mbr[:, ntok:2 * ntok], xn32[:, ntok:2 * ntok],
                                            AF.Exp, scale=-0.5))
        S.drain("ac")
        DV_BCAST = DV_VAR

        pump(9999)

        x2 = {}
        dv_xn = {}
        dv_sub = {}
        for fc in range(NFC):
            w = [("PE", PE_STATS)] if fc < 4 else [("DV", (lambda f=fc: dv_sub[f - 4]))]
            x2[fc] = dma("sp", f"X2{fc % 4}",
                         xr32[:, (fc % 4) * ntok:(fc % 4 + 1) * ntok],
                         d_x[fc * 128:(fc + 1) * 128, :], waits=w)
        pe_h = {}
        for fc in range(NFC):
            xsl = (fc % 4) * ntok
            nsl = (fc % 2) * ntok
            dv_sub[fc] = dv(lambda e, xsl=xsl, nsl=nsl: e.tensor_tensor(
                xn32[:, nsl:nsl + ntok], xr32[:, xsl:xsl + ntok], mbr[:, 0:ntok],
                ALU.subtract),
                waits=[x2[fc], ("DV", DV_BCAST), ("AC", AC_RSTD)]
                + ([("PE", pe_h[fc - 2])] if fc >= 2 else []))
            dvd()
            dv_xn[fc] = dv(lambda e, nsl=nsl, fc=fc: e.tensor_tensor(
                xhi[:, fc * ntok:(fc + 1) * ntok], xn32[:, nsl:nsl + ntok],
                mbr[:, ntok:2 * ntok], ALU.mult))
            dvd()
            w = [("DV", dv_xn[fc])]
            if fc == 0:
                w += [SC_RWT, W_PARB_LAST]
            for h in range(2):
                pe(lambda e, fc=fc, h=h: e.matmul(
                    pb[0 + h][0:15, 0:H], rwt[:, fc * 15:(fc + 1) * 15],
                    xhi[:, fc * ntok + h * H:fc * ntok + (h + 1) * H],
                    start=(fc == 0), stop=(fc == NFC - 1)), waits=w if h == 0 else ())
                pe_h[fc] = pe(lambda e, fc=fc, h=h: e.matmul(
                    pb[4 + h][0:128, 0:H], atb[:, fc * 128:(fc + 1) * 128],
                    xhi[:, fc * ntok + h * H:fc * ntok + (h + 1) * H],
                    start=(fc == 0), stop=(fc == NFC - 1)))
            pump(6)
        PE_RTH = pe_h[NFC - 1]
        pump(9999)
        SP_KPAD = EXP["SP_KPAD"]
        K_ODD = EXP["K_ODD"]
        DV_CONSTR = EXP["DV_CONSTR"]

        wd = {}
        for g in range(4):
            wd[0] = dma("gp", "WA", wbuf[0][:, g * 1024:(g + 1) * 1024],
                        bass.AP(d_w5.tensor, g * 8 * 128 * 128,
                                [[128, 128], [128 * 128, 8], [1, 128]]))
        for g in range(4):
            wd[1] = dma("gp", "WB", wbuf[1][:, g * 1024:(g + 1) * 1024],
                        bass.AP(d_w5.tensor, (32 + g * 8) * 128 * 128,
                                [[128, 128], [128 * 128, 8], [1, 128]]))
        dma("gp", "TV", tstg[:, 0:128], bass.AP(d_kscr, 1, [[1, 128], [1, 128]]),
            waits=(SP_KPAD, K_ODD, ("DV", DV_CONSTR)))
        GP_TREV = dma("gp", "TV", tstg[:, 128:256],
                      bass.AP(d_kscr, 129, [[1, 128], [1, 128]]))
        pe(lambda e: e.matmul(pb[6][:, 0:128], j128[:, :], tstg[:, 0:128],
                              start=True, stop=True),
           waits=(GP_TREV,))
        PE_TT = pe(lambda e: e.matmul(pb[6][:, 128:256], j128[:, :], tstg[:, 128:256],
                                      start=True, stop=True))
        dv(lambda e: e.tensor_copy(ttri[:, :], pb[6][:, 0:128]), waits=(("PE", PE_TT),))
        DV_TT = dv(lambda e: e.tensor_copy(tful[:, :], pb[6][:, 128:256]))
        dvd()
        PE_RBT = pe(lambda e: e.transpose(pb[6][0:1, 376:391], rb15[:, 0:1],
                                          i128[0:15, 0:15]),
                    waits=(("PE", PE_RTH), ("DV", DV_CONSTR), ("DV", DV_TT)))
        DV_RBT = dv(lambda e: e.tensor_copy(scr15[0:1, 0:15], pb[6][0:1, 376:391]),
                    waits=(("PE", PE_RBT),))
        dvd()
        PE_RBX = pe(lambda e: e.matmul(pb[6][0:128, 392:407], onesr[0:1, 0:128],
                                       scr15[0:1, 0:15], start=True, stop=True),
                    waits=(("DV", DV_RBT),))
        DV_RC = dv(lambda e: e.tensor_copy(rbx[:, :], pb[6][0:128, 392:407]),
                   waits=(("PE", PE_RBX),))
        dvd()
        PE_TRL = 0
        DV_LGH = {}
        for h in range(2):
            DV_LG = dv(lambda e, h=h: e.tensor_scalar(
                lgsb[:, 0:H], pb[0 + h][0:15, 0:H], 1.0, None, ALU.mult),
                waits=[("DV", DV_RC)] + ([("PE", PE_TRL)] if h else []))
            DV_LGH[h] = DV_LG
            dvd()
            for tt in range(HTIL):
                t = h * HTIL + tt
                PE_TRL = pe(lambda e, t=t, tt=tt: e.transpose(
                    pb[6][0:128, t * 15:(t + 1) * 15],
                    lgsb[:, tt * 128:(tt + 1) * 128],
                    i128[0:15, 0:15]), waits=(("DV", DV_LG), ("DV", DV_TT)))
        DV_LGT = dv(lambda e: e.tensor_copy(lgt[:, :], pb[6][0:128, 0:NTIL * 15]),
                    waits=(("PE", PE_TRL),))
        dvd()

        web = [wd[0], ("DV", DV_LGH[1])]
        for h in range(2):
            for db in range(32):
                pe(lambda e, h=h, db=db: e.matmul(
                    pb[0 + h][0:128, 0:H],
                    wbuf[0][:, db * 128:(db + 1) * 128],
                    xhi[:, db * ntok + h * H:db * ntok + (h + 1) * H],
                    start=(db == 0), stop=False),
                   waits=web if (h == 0 and db == 0) else ())
        web1 = [wd[1], ("DV", DV_LGH[1])]
        for h in range(2):
            for db in range(32):
                pe(lambda e, h=h, db=db: e.matmul(
                    pb[2 + h][0:128, 0:H],
                    wbuf[1][:, db * 128:(db + 1) * 128],
                    xhi[:, db * ntok + h * H:db * ntok + (h + 1) * H],
                    start=(db == 0), stop=False),
                   waits=web1 if (h == 0 and db == 0) else ())
        W15 = NTIL * 15
        for t in range(NTIL):
            dv(lambda e, t=t: e.tensor_tensor(
                lgt[:, t * 15:(t + 1) * 15], lgt[:, t * 15:(t + 1) * 15],
                rbx[:, :], ALU.add))
        dvd()
        for t in range(NTIL):
            dv(lambda e, t=t: e.max(mx8[:, t * 8:(t + 1) * 8],
                                    lgt[:, t * 15:(t + 1) * 15]))
        dvd()
        for t in range(NTIL):
            dv(lambda e, t=t: e.tensor_scalar(
                selm8[:, t * 15:(t + 1) * 15], lgt[:, t * 15:(t + 1) * 15],
                mx8[:, t * 8 + 7:t * 8 + 8], None, ALU.is_ge))
        for t in range(NTIL):
            dv(lambda e, t=t: e.tensor_scalar(
                lgt[:, t * 15:(t + 1) * 15], lgt[:, t * 15:(t + 1) * 15],
                mx8[:, t * 8:t * 8 + 1], None, ALU.subtract))
        DV_PRE = S.cnt["DV"]
        dvd()
        AC_EXP = ac(lambda e: e.activation(pmt[:, 0:W15], lgt[:, 0:W15], AF.Exp),
                    waits=(("DV", DV_PRE),))
        S.drain("ac")
        for t in range(NTIL):
            dv(lambda e, t=t: e.tensor_reduce(esum8[:, t:t + 1],
                                              pmt[:, t * 15:(t + 1) * 15],
                                              AX.X, ALU.add),
               waits=(("AC", AC_EXP),) if t == 0 else ())
        dvd()
        dv(lambda e: e.reciprocal(esum8[:, 8:8 + NTIL], esum8[:, 0:NTIL]))
        dvd()
        dv(lambda e: e.tensor_tensor(pmt[:, 0:W15], pmt[:, 0:W15], selm8[:, 0:W15],
                                     ALU.mult))
        dvd()
        dv_pm = 0
        for t in range(NTIL):
            dv_pm = dv(lambda e, t=t: e.tensor_scalar(
                pmt[:, t * 15:(t + 1) * 15], pmt[:, t * 15:(t + 1) * 15],
                esum8[:, 8 + t:9 + t], None, ALU.mult))
        dvd()
        DV_PMTS = 0
        for h in range(2):
            PE_TB = 0
            for tt in range(HTIL):
                t = h * HTIL + tt
                PE_TB = pe(lambda e, t=t, tt=tt: e.transpose(
                    pb[7][0:15, tt * 128:(tt + 1) * 128], pmt[:, t * 15:(t + 1) * 15],
                    i128[0:128, 0:128]),
                    waits=(("DV", dv_pm if h == 0 else DV_PMTS),) if tt == 0 else ())
            DV_PMTS = dv(lambda e, h=h: e.tensor_copy(pmts[:, h * H:(h + 1) * H],
                                                      pb[7][0:15, 0:H]),
                         waits=(("PE", PE_TB),))
            dvd()
        DV_HW = 0
        for h in range(2):
            PE_W8 = pe(lambda e, h=h: e.matmul(pb[6][0:8, 0:H], mmapb[:, :],
                                               pmts[:, h * H:(h + 1) * H],
                                               start=True, stop=True),
                       waits=(("DV", DV_PMTS), ("DV", DV_CONSTR)))
            DV_W8 = dv(lambda e, h=h: e.tensor_copy(w8s[:, h * H:(h + 1) * H],
                                                    pb[6][0:8, 0:H]),
                       waits=(("PE", PE_W8),))
            dvd()
            PE_W16 = pe(lambda e, h=h: e.matmul(pb[7][0:128, 0:H], rep8[:, :],
                                                w8s[:, h * H:(h + 1) * H],
                                                start=True, stop=True),
                        waits=(("DV", DV_W8),))
            DV_W16 = dv(lambda e, h=h: e.tensor_copy(w16s[:, h * H:(h + 1) * H],
                                                     pb[7][0:128, 0:H]),
                        waits=(("PE", PE_W16),))
            dvd()
            DV_HW = dv(lambda e, h=h: e.tensor_tensor(
                hwb[:, h * H:(h + 1) * H], pb[4 + h][0:128, 0:H],
                w16s[:, h * H:(h + 1) * H], ALU.mult), waits=(("DV", DV_W16),))
            dvd()

        pe_base = {}
        ac_g = {}
        dv_out = {}
        dv_mul = {}
        dv_sev = {}
        dv_yev = {}
        x3 = {}
        for oc in range(2, 32):
            w = [] if oc < 3 else [("PE", (lambda o=oc: pe_base[o - 3]))]
            if oc == 2:
                w = [("PE", (lambda: pe_h[NFC - 1]))]
            for g in range(4):
                wd[oc] = dma(
                    "gp", "W" + "ABC"[oc % 3],
                    wbuf[oc % 3][:, g * 1024:(g + 1) * 1024],
                    bass.AP(d_w5.tensor, (oc * 32 + g * 8) * 128 * 128,
                            [[128, 128], [128 * 128, 8], [1, 128]]),
                    waits=w if g == 0 else ())
        def emit_x3(oc):
            w = [("DV", (lambda o=oc: dv_sub[NFC - 4 + (o % 4)]))] if oc < 4 else \
                [("DV", (lambda o=oc: dv_out[o - 4]))]
            x3[oc] = dma("ac", f"X3{oc % 4}",
                         xr32[:, (oc % 4) * ntok:(oc % 4 + 1) * ntok],
                         d_x[oc * 128:(oc + 1) * 128, :], waits=w)
        for oc in range(4):
            emit_x3(oc)

        sp_out = {}
        for I in range(16):
            oc0, oc1 = 2 * I, 2 * I + 1
            u0 = lambda h, I=I: xhi[:, (2 * I) * ntok + h * H:(2 * I) * ntok + (h + 1) * H]
            u1 = lambda h, I=I: xhi[:, (2 * I + 1) * ntok + h * H:(2 * I + 1) * ntok + (h + 1) * H]
            sth = lambda h: stb[:, h * H:(h + 1) * H]
            w0 = [("DV", DV_TT), ("DV", DV_HW)] if I == 0 else [("DV", dv_sev[I - 1])]
            PE_Y = 0
            for h in range(2):
                pe(lambda e, h=h, I=I: e.matmul(pb[4 + 2 * h][0:128, 0:H], ttri[:, :],
                                                u0(h, I), start=True, stop=False),
                   waits=w0 if h == 0 else ())
                pe(lambda e, h=h: e.matmul(pb[4 + 2 * h][0:128, 0:H], moutt[:, 0:128],
                                           sth(h), start=False, stop=True))
                pe(lambda e, h=h, I=I: e.matmul(pb[5 + 2 * h][0:128, 0:H], tful[:, :],
                                                u0(h, I), start=True, stop=False))
                pe(lambda e, h=h, I=I: e.matmul(pb[5 + 2 * h][0:128, 0:H], ttri[:, :],
                                                u1(h, I), start=False, stop=False))
                PE_Y = pe(lambda e, h=h: e.matmul(pb[5 + 2 * h][0:128, 0:H],
                                                  moutt[:, 128:256], sth(h),
                                                  start=False, stop=True))
            s0 = (oc0 % 4) * ntok
            s1 = (oc1 % 4) * ntok
            dv(lambda e, s0=s0: e.tensor_copy(big32[:, s0:s0 + H], pb[4][0:128, 0:H]),
               waits=(("PE", PE_Y),))
            dv(lambda e, s0=s0: e.tensor_copy(big32[:, s0 + H:s0 + ntok],
                                              pb[6][0:128, 0:H]))
            dv(lambda e, s1=s1: e.tensor_copy(big32[:, s1:s1 + H], pb[5][0:128, 0:H]))
            DV_YEV = dv(lambda e, s1=s1: e.tensor_copy(big32[:, s1 + H:s1 + ntok],
                                                       pb[7][0:128, 0:H]))
            dvd()
            dv_yev[oc0] = DV_YEV
            dv_yev[oc1] = DV_YEV
            def emit_scan(I, DV_YEV=DV_YEV, u0=u0, u1=u1, sth=sth):
                PE_SC = 0
                for h in range(2):
                    pe(lambda e, h=h, I=I: e.matmul(pb[4 + 2 * h][0:128, 0:H],
                                                    mint[:, 0:128], u0(h, I),
                                                    start=True, stop=False),
                       waits=(("DV", DV_YEV),) if h == 0 else ())
                    pe(lambda e, h=h, I=I: e.matmul(pb[4 + 2 * h][0:128, 0:H],
                                                    mint[:, 128:256], u1(h, I),
                                                    start=False, stop=False))
                    PE_SC = pe(lambda e, h=h: e.matmul(pb[4 + 2 * h][0:128, 0:H],
                                                       ppk[:, :], sth(h),
                                                       start=False, stop=True))
                dv(lambda e: e.tensor_copy(stb[:, 0:H], pb[4][0:128, 0:H]),
                   waits=(("PE", PE_SC),))
                dv_sev[I] = dv(lambda e: e.tensor_copy(stb[:, H:ntok],
                                                       pb[6][0:128, 0:H]))
                dvd()
            for oc in (oc0, oc1):
                if oc == oc1:
                    emit_scan(I)
                bk = 0 if oc % 2 == 0 else 2
                wl = [wd[oc]]
                if oc >= 2:
                    wl += [("DV", (lambda o=oc: dv_mul[o - 2]))]
                if oc < 2:
                    wl += [("DV", DV_HW), W_PARB_LAST]
                PE_BASE = 0
                for h in range(2):
                    if oc > 1:
                        for db in range(32):
                            pe(lambda e, oc=oc, h=h, db=db, bk=bk: e.matmul(
                                pb[bk + h][0:128, 0:H],
                                wbuf[oc % 3][:, db * 128:(db + 1) * 128],
                                xhi[:, db * ntok + h * H:db * ntok + (h + 1) * H],
                                start=(db == 0), stop=False),
                               waits=wl if (h == 0 and db == 0) else ())
                    PE_BASE = pe(lambda e, oc=oc, h=h, bk=bk: e.matmul(
                        pb[bk + h][0:128, 0:H], bpb[:, oc * 128:(oc + 1) * 128],
                        hwb[:, h * H:(h + 1) * H], start=False, stop=True),
                        waits=wl if (oc <= 1 and h == 0) else ())
                pe_base[oc] = PE_BASE
                for h in range(2):
                    ac_g[oc] = ac(lambda e, oc=oc, h=h, bk=bk: e.activation(
                        pb[bk + h][0:128, 0:H], pb[bk + h][0:128, 0:H], AF.Silu,
                        bias=bb128[:, oc:oc + 1]),
                        waits=[("PE", PE_BASE)] if h == 0 else [])
                    S.drain("ac")
                if oc + 4 < 32:
                    emit_x3(oc + 4)
                for h in range(2):
                    ssl = (oc % 4) * ntok + h * H
                    osl = (oc % 2) * ntok + h * H
                    wmu = [("AC", ac_g[oc]), x3[oc], ("DV", dv_yev[oc])]
                    if oc >= 2:
                        wmu += [("OUT" + "AB"[oc % 2],
                                 (lambda o=oc: sp_out[(o - 2, 1)][1]))]
                    dv_mul[oc] = dv(lambda e, ssl=ssl, osl=osl, bk=bk, h=h:
                        e.tensor_tensor(
                            xn32[:, osl:osl + H], pb[bk + h][0:128, 0:H],
                            big32[:, ssl:ssl + H], ALU.mult),
                        waits=wmu)
                    dvd()
                    dv_out[oc] = dv(lambda e, osl=osl, oc=oc, h=h: e.tensor_tensor(
                        xn32[:, osl:osl + H], xn32[:, osl:osl + H],
                        xr32[:, (oc % 4) * ntok + h * H:(oc % 4) * ntok + (h + 1) * H],
                        ALU.add))
                    dvd()
                    sp_out[(oc, h)] = dma(
                        "sp", "OUT" + "AB"[oc % 2],
                        bass.AP(d_y.tensor, oc * 128 * ntok + h * H,
                                [[ntok, 128], [1, H]]),
                        xn32[:, osl:osl + H], waits=(("DV", dv_out[oc]),))
        OUT_A = S.cnt["OUTA"]
        OUT_B = S.cnt["OUTB"]
        S.ops["sp"].append(lambda e, sems: (e.wait_ge(sems["OUTA"], OUT_A),
                                            e.wait_ge(sems["OUTB"], OUT_B)))

        build_block()

    return nc


def _host_arrays(inputs):
    x = np.ascontiguousarray(np.asarray(inputs["L0_attn_u"], dtype=np.float32)[0].T)
    Wb = np.asarray(inputs["L0_ffn_base_W"], dtype=np.float32)
    w5 = np.ascontiguousarray(
        Wb.T.reshape(32, 128, 32, 128).transpose(2, 0, 1, 3)).reshape(32 * 32 * 128, 128)
    A = np.asarray(inputs["L0_ffn_A"], dtype=np.float32)
    at = np.ascontiguousarray(A.reshape(128, D).T)
    Bm = np.asarray(inputs["L0_ffn_B"], dtype=np.float32)
    bpT = np.ascontiguousarray(Bm.transpose(0, 2, 1).reshape(128, D))
    rwt = np.ascontiguousarray(np.asarray(inputs["L0_ffn_router_W"], np.float32).T)
    bbT = np.ascontiguousarray(
        np.asarray(inputs["L0_ffn_base_b"], np.float32).reshape(32, 128).T)
    rb15 = np.ascontiguousarray(np.asarray(inputs["L0_ffn_router_b"], np.float32)[:, None])
    emap = np.ascontiguousarray(np.asarray(inputs["L0_ffn_expert_map"], np.int32)[:, None])
    svec = np.zeros((1, 640), np.float32)
    svec[0, 0:64] = inputs["L0_attn_Lambda_ri"][:, 0]
    svec[0, 64:128] = inputs["L0_attn_Lambda_ri"][:, 1]
    svec[0, 128:192] = inputs["L0_attn_p_ri"][:, 0]
    svec[0, 192:256] = inputs["L0_attn_p_ri"][:, 1]
    svec[0, 256:320] = inputs["L0_attn_q_ri"][:, 0]
    svec[0, 320:384] = inputs["L0_attn_q_ri"][:, 1]
    svec[0, 384:448] = np.asarray(inputs["L0_attn_Ct_ri"])[0, :, 0]
    svec[0, 448:512] = np.asarray(inputs["L0_attn_Ct_ri"])[0, :, 1]
    svec[0, 512:576] = np.asarray(inputs["L0_attn_Bp"])[:, 0]
    svec[0, 576] = np.asarray(inputs["L0_attn_D"])[0]
    svec[0, 577] = np.asarray(inputs["L0_attn_log_step"])[0]
    vctr = np.ascontiguousarray(np.asarray(inputs["L0_attn_Vc_ri"])[:, :, 0].T)
    vcti = np.ascontiguousarray(np.asarray(inputs["L0_attn_Vc_ri"])[:, :, 1].T)
    i128 = np.eye(128, dtype=np.float32)
    j128 = np.ascontiguousarray(i128[::-1, :])
    onesr = np.ones((1, 128), np.float32)
    onescb = np.ones((128, 128), bf16)
    rep8 = np.ascontiguousarray(
        (np.arange(128)[None, :] // 16 == np.arange(8)[:, None]).astype(bf16))
    io15 = np.ascontiguousarray(
        np.broadcast_to(np.arange(8, dtype=np.float32)[None, :], (15, 8)))
    shared = dict(w5=w5, at=at, bpT=bpT, rwt=rwt, bbT=bbT, rb15=rb15, emap=emap,
                  svec=svec, vctr=vctr, vcti=vcti, i128=i128, j128=j128,
                  onesr=onesr, onescb=onescb, rep8=rep8, io15=io15)
    return x, shared


def kernel(**inputs):
    xT, shared = _host_arrays(inputs)
    ntok = xT.shape[1] // NCORES
    nc = build_nc(ntok)
    in_maps = []
    for c in range(NCORES):
        m = dict(shared)
        m["xT"] = np.ascontiguousarray(xT[:, c * ntok:(c + 1) * ntok])
        in_maps.append(m)
    res = run_bass_kernel_spmd(nc, in_maps, core_ids=list(range(NCORES)))
    out = np.empty((1, NCORES * ntok, D), np.float32)
    for c in range(NCORES):
        out[0, c * ntok:(c + 1) * ntok, :] = res.results[c]["yT"].T
    return out


# version 22
# speedup vs baseline: 1.0010x; 1.0010x over previous
import numpy as np
import ml_dtypes
from contextlib import ExitStack

import concourse.bass as bass
import concourse.mybir as mybir
from concourse.bass_utils import run_bass_kernel_spmd

F32 = mybir.dt.float32
BF16 = mybir.dt.bfloat16
I32 = mybir.dt.int32
AF = mybir.ActivationFunctionType
ALU = mybir.AluOpType
AX = mybir.AxisListType
bf16 = ml_dtypes.bfloat16

NCORES = 8
D = 4096
NFC = 32


class Sched:
    ENGS = ("pe", "dv", "ac", "gp", "sp", "sc")

    def __init__(self):
        self.ops = {e: [] for e in self.ENGS}
        self.cnt = {}

    def emit(self, eng, fn, waits=(), inc=None, incby=1):
        if inc is not None:
            self.cnt[inc] = self.cnt.get(inc, 0) + incby
            tgt = self.cnt[inc]
        else:
            tgt = None

        def g(e, sems, fn=fn, waits=tuple(waits), inc=inc, incby=incby):
            for s, v in waits:
                v = v() if callable(v) else v
                if v > 0:
                    e.wait_ge(sems[s], v)
            r = fn(e)
            if inc is not None:
                r.then_inc(sems[inc], incby)
        self.ops[eng].append(g)
        return tgt

    def drain(self, eng):
        self.ops[eng].append(lambda e, sems: e.drain())


def build_nc(ntok=1024, stage=99):
    H = ntok // 2
    NTIL = ntok // 128
    HTIL = H // 128
    nc = bass.Bass("TRN2", target_bir_lowering=False)

    d_x = nc.dram_tensor("xT", [D, ntok], F32, kind="ExternalInput").ap()
    d_w5 = nc.dram_tensor("w5", [32 * 32 * 128, 128], F32, kind="ExternalInput").ap()
    d_at = nc.dram_tensor("at", [D, 128], F32, kind="ExternalInput").ap()
    d_bp = nc.dram_tensor("bpT", [128, D], F32, kind="ExternalInput").ap()
    d_rwt = nc.dram_tensor("rwt", [D, 15], F32, kind="ExternalInput").ap()
    d_bb = nc.dram_tensor("bbT", [128, 32], F32, kind="ExternalInput").ap()
    d_rb = nc.dram_tensor("rb15", [15, 1], F32, kind="ExternalInput").ap()
    d_em = nc.dram_tensor("emap", [15, 1], I32, kind="ExternalInput").ap()
    d_sv = nc.dram_tensor("svec", [1, 640], F32, kind="ExternalInput").ap()
    d_vr = nc.dram_tensor("vctr", [64, 64], F32, kind="ExternalInput").ap()
    d_vi = nc.dram_tensor("vcti", [64, 64], F32, kind="ExternalInput").ap()
    d_i128 = nc.dram_tensor("i128", [128, 128], F32, kind="ExternalInput").ap()
    d_j128 = nc.dram_tensor("j128", [128, 128], F32, kind="ExternalInput").ap()
    d_ones = nc.dram_tensor("onesr", [1, 128], F32, kind="ExternalInput").ap()
    d_onescb = nc.dram_tensor("onescb", [128, 128], BF16, kind="ExternalInput").ap()
    d_rep8 = nc.dram_tensor("rep8", [8, 128], BF16, kind="ExternalInput").ap()
    d_io15 = nc.dram_tensor("io15", [15, 8], F32, kind="ExternalInput").ap()
    d_kscr = nc.dram_tensor("kscr", [1, 4608], F32)
    d_y = nc.dram_tensor("yT", [D, ntok], F32, kind="ExternalOutput").ap()

    S = Sched()
    with ExitStack() as ctx:
        E = ctx.enter_context
        xhi = E(nc.sbuf_tensor("xhi", [128, NFC * ntok], BF16))
        wbuf = [E(nc.sbuf_tensor(f"wbuf{i}", [128, 32 * 128], BF16)) for i in range(2)]
        atb = E(nc.sbuf_tensor("atb", [128, NFC * 128], BF16))
        wbuf.append(atb)
        bpb = E(nc.sbuf_tensor("bpb", [128, D], BF16))
        rwt = E(nc.sbuf_tensor("rwt_s", [128, NFC * 15], BF16))
        hwb = E(nc.sbuf_tensor("hwb", [128, ntok], BF16))
        stb = E(nc.sbuf_tensor("stb", [128, ntok], BF16))
        big32 = E(nc.sbuf_tensor("big32", [128, 4096], F32))
        xr32 = E(nc.sbuf_tensor("xr32", [128, 4 * ntok], F32))
        xb16 = E(nc.sbuf_tensor("xb16", [128, 4 * ntok], BF16))
        xq16 = E(nc.sbuf_tensor("xq16", [128, 2 * ntok], BF16))
        xn32 = E(nc.sbuf_tensor("xn32", [128, 2 * ntok], F32))
        krow = E(nc.sbuf_tensor("krow", [1, 1024], F32))
        mbr = E(nc.sbuf_tensor("mbr", [128, 2 * ntok], F32))
        rows = E(nc.sbuf_tensor("rows", [1, 3072], F32))
        mats = E(nc.sbuf_tensor("mats", [128, 23 * 64], F32))
        wpr = E(nc.sbuf_tensor("wpr", [128, 256], F32))
        bstk = E(nc.sbuf_tensor("bstk", [128, 1], F32))
        cols = E(nc.sbuf_tensor("cols", [128, 16], F32))
        mint = E(nc.sbuf_tensor("mint", [128, 256], BF16))
        moutt = E(nc.sbuf_tensor("moutt", [128, 256], BF16))
        ppk = E(nc.sbuf_tensor("ppk", [128, 128], BF16))
        ttri = E(nc.sbuf_tensor("ttri", [128, 128], BF16))
        tful = E(nc.sbuf_tensor("tful", [128, 128], BF16))
        lgsb = E(nc.sbuf_tensor("lgsb", [15, H], F32))
        lgt = E(nc.sbuf_tensor("lgt", [128, NTIL * 15], F32))
        mx8 = E(nc.sbuf_tensor("mx8", [128, NTIL * 8], F32))
        pmt = E(nc.sbuf_tensor("pmt", [128, NTIL * 15], F32))
        scr15 = E(nc.sbuf_tensor("scr15", [128, 15], F32))
        rcol = E(nc.sbuf_tensor("rcol", [128, 8], F32))
        selm8 = E(nc.sbuf_tensor("selm8", [128, 120], F32))
        esum8 = E(nc.sbuf_tensor("esum8", [128, 16], F32))
        rbx = E(nc.sbuf_tensor("rbx", [128, 15], F32))
        scr1 = E(nc.sbuf_tensor("scr1", [128, 2], F32))
        pmts = E(nc.sbuf_tensor("pmts", [15, ntok], BF16))
        w8s = E(nc.sbuf_tensor("w8s", [8, ntok], BF16))
        w16s = E(nc.sbuf_tensor("w16s", [128, ntok], BF16))
        i128 = E(nc.sbuf_tensor("i128_s", [128, 128], F32))
        j128 = E(nc.sbuf_tensor("j128_s", [128, 128], F32))
        onesr = E(nc.sbuf_tensor("onesr_s", [1, 128], F32))
        onescb = E(nc.sbuf_tensor("onescb_s", [128, 128], BF16))
        rep8 = E(nc.sbuf_tensor("rep8_s", [8, 128], BF16))
        io15 = E(nc.sbuf_tensor("io15_s", [15, 8], F32))
        mmapb = E(nc.sbuf_tensor("mmapb", [15, 8], BF16))
        emf = E(nc.sbuf_tensor("emf", [15, 1], F32))
        emi = E(nc.sbuf_tensor("emi", [15, 1], I32))
        rb15 = E(nc.sbuf_tensor("rb15_s", [15, 1], F32))
        bb128 = E(nc.sbuf_tensor("bb128", [128, 32], F32))
        svec = E(nc.sbuf_tensor("svec_s", [1, 640], F32))
        vctr = E(nc.sbuf_tensor("vctr_s", [64, 64], F32))
        vcti = E(nc.sbuf_tensor("vcti_s", [64, 64], F32))
        sc64 = E(nc.sbuf_tensor("sc64", [128, 4], F32))
        tstg = E(nc.sbuf_tensor("tstg", [128, 256], F32))
        rop = E(nc.sbuf_tensor("rop", [128, 128], F32))
        pkstg = E(nc.sbuf_tensor("pkstg", [64, 512], F32))
        wop = E(nc.sbuf_tensor("wop", [128, 128], F32))

        pb = [E(nc.psum_tensor(f"pb{i}", [128, 512], F32)) for i in range(8)]

        sems = {}
        dma_sem_names = (["PAR", "PARB", "K", "K2", "TV", "RW", "X1A", "X1B", "X1C", "X1D", "WA", "WB", "WC",
                          "OUTA", "OUTB"]
                         + [f"X2{i}" for i in range(4)]
                         + [f"X3{i}" for i in range(4)])
        for sname in ["PE", "DV", "AC"] + dma_sem_names:
            sems[sname] = E(nc.semaphore(sname))

        def dma(eng, sem, dst, src, waits=()):
            v = S.emit(eng, lambda e, d=dst, s=src: e.dma_start(d, s),
                       waits=waits, inc=sem, incby=16)
            return (sem, v)

        def build_block():
            block = E(nc.Block())

            @block.tensor
            def _(t):
                for f in S.ops["pe"]:
                    f(t, sems)

            @block.vector
            def _(v):
                for f in S.ops["dv"]:
                    f(v, sems)

            @block.scalar
            def _(a):
                for f in S.ops["sc"]:
                    f(a, sems)
                for f in S.ops["ac"]:
                    f(a, sems)

            @block.gpsimd
            def _(g):
                for f in S.ops["gp"]:
                    f(g, sems)

            @block.sync
            def _(s):
                for f in S.ops["sp"]:
                    f(s, sems)

        def pe(fn, waits=()):
            return S.emit("pe", fn, waits=waits, inc="PE")

        def dv(fn, waits=()):
            return S.emit("dv", fn, waits=waits, inc="DV")

        def ac(fn, waits=()):
            return S.emit("ac", fn, waits=waits, inc="AC")

        def dvd():
            S.drain("dv")

        def M(k):
            return mats[0:64, 64 * k:64 * k + 64]

        def R_(off, ln=64):
            return rows[0:1, off:off + ln]

        def col(k):
            return cols[0:64, k:k + 1]

        LR, LI, PR, PI, QR, QI, CTR, CTI, BPR = (64 * i for i in range(9))
        DLT, HS, NHS = 576, 578, 580
        NDR, NDI, DEN, INVD, FR, FI = 640, 704, 768, 832, 896, 960
        QFR, QFI, UR, UI, T1, T2 = 1024, 1088, 1152, 1216, 1280, 1344
        SWR, SWI, CR_, CI_, WRR, WRI = 1408, 1410, 1412, 1414, 1472, 1536
        E1PR, E1PI, NQI, NUI = 1600, 1664, 1728, 1792
        B0R, B0I = 1856, 1920
        MEAN, VTMP, RSTD = 0, 1024, 2048

        XR, XRT, XI, XIT, XIN, XITN = 0, 1, 2, 3, 4, 5
        YR, YRT, YI, YIT, YIN, YITN = 6, 7, 8, 9, 10, 11
        E1R, E1I, E1RT, E1IT, E1ITN = 12, 13, 14, 15, 16
        TM1, TM2, TM3, TM4 = 17, 18, 19, 20
        WBR, WBI = 21, 22
        X8TR, X8TI = 23, 24
        D_R, D_IN, D_RT, D_ITN, X8TR64, X8TI64 = 0, 1, 2, 3, 4, 5

        def M64(k):
            return mats[64:128, 64 * k:64 * k + 64]

        CPR, CPI, CBP, CB0R, CB0I, CBBR, CBBI, CNPI = range(8)

        dma("gp", "PAR", svec[:, :], d_sv)
        dma("gp", "PAR", vctr[:, :], d_vr)
        dma("gp", "PAR", vcti[:, :], d_vi)
        dma("gp", "PAR", i128[:, :], d_i128)
        dma("gp", "PAR", j128[:, :], d_j128)
        dma("gp", "PAR", onesr[:, :], d_ones)
        dma("gp", "PAR", onescb[:, :], d_onescb)
        dma("gp", "PAR", rep8[:, :], d_rep8)
        dma("gp", "PAR", io15[:, :], d_io15)
        dma("gp", "PAR", emi[:, :], d_em)
        dma("gp", "PAR", rb15[:, :], d_rb)
        W_PAR_LAST = dma("gp", "PAR", bb128[:, :], d_bb)
        W_PAR = (W_PAR_LAST,)

        ac_sq = {}
        pe_st = {}
        x1g = {}
        x1 = {}

        def emit_x1g(g):
            w = [] if g < 2 else                 [("AC", (lambda f=g: ac_sq[2 * (f - 2) + 1])),
                 ("PE", (lambda f=g: pe_st[2 * (f - 2) + 1]))]
            sl = (g % 2) * 2 * ntok
            x1g[g] = dma("gp", "X1" + "ABCD"[g % 4],
                         xb16[:, sl:sl + 2 * ntok],
                         bass.AP(d_x.tensor, 2 * g * 128 * ntok,
                                 [[ntok, 128], [128 * ntok, 2], [1, ntok]]),
                         waits=w)
            x1[2 * g] = x1g[g]
            x1[2 * g + 1] = x1g[g]

        emit_x1g(0)
        emit_x1g(1)
        for g in range(4):
            dma("gp", "PARB", atb[:, g * 1024:(g + 1) * 1024],
                bass.AP(d_at.tensor, g * 8 * 128 * 128,
                        [[128, 128], [128 * 128, 8], [1, 128]]))
        W_PARB_LAST = dma("gp", "PARB", bpb[:, :], d_bp)
        SC_RWT = None
        for g in range(4):
            SC_RWT = dma("gp", "RW", rwt[:, g * 8 * 15:(g + 1) * 8 * 15],
                         bass.AP(d_rwt.tensor, g * 8 * 128 * 15,
                                 [[15, 128], [128 * 15, 8], [1, 15]]))
        for g in range(2, NFC // 2):
            emit_x1g(g)

        EXP = {}

        def _constr_gen():
            def cdv(fn, waits=()):
                r = S.emit("dv", fn, waits=waits, inc="DV")
                S.drain("dv")
                return r

            ac(lambda e: e.activation(R_(DLT, 1), svec[0:1, 577:578], AF.Exp), waits=W_PAR)
            S.drain("ac")
            ac(lambda e: e.activation(R_(HS, 1), R_(DLT, 1), AF.Copy, scale=0.5))
            S.drain("ac")
            AC_H = ac(lambda e: e.activation(R_(NHS, 1), R_(DLT, 1), AF.Copy, scale=-0.5))
            S.drain("ac")
            pe(lambda e: e.matmul(pb[6][0:128, 4:5], onesr[0:1, 0:128], R_(HS, 1),
                                  start=True, stop=True), waits=(("AC", AC_H),))
            pe(lambda e: e.matmul(pb[6][0:128, 5:6], onesr[0:1, 0:128], R_(NHS, 1),
                                  start=True, stop=True))
            PE_BC = pe(lambda e: e.matmul(pb[6][0:128, 6:7], onesr[0:1, 0:128], R_(DLT, 1),
                                          start=True, stop=True))
            cdv(lambda e: e.tensor_copy(sc64[:, 0:1], pb[6][0:128, 4:5]), waits=(("PE", PE_BC),))
            cdv(lambda e: e.tensor_copy(sc64[:, 1:2], pb[6][0:128, 5:6]))
            cdv(lambda e: e.tensor_copy(sc64[:, 2:3], pb[6][0:128, 6:7]))
            dvd()
            yield
            for dst, src in ((LR, 0), (LI, 64), (PR, 128), (PI, 192), (QR, 256),
                             (QI, 320), (CTR, 384), (CTI, 448), (BPR, 512)):
                cdv(lambda e, d=dst, s=src: e.tensor_copy(R_(d), svec[0:1, s:s + 64]),
                   waits=W_PAR)
            dvd()
            yield
            cdv(lambda e: e.tensor_scalar(R_(NDR), R_(LR), R_(HS, 1), -1.0, ALU.mult, ALU.add),
               waits=(("AC", AC_H),))
            cdv(lambda e: e.tensor_scalar(R_(NDI), R_(LI), R_(HS, 1), None, ALU.mult))
            dvd()
            yield
            cdv(lambda e: e.tensor_tensor(R_(DEN), R_(NDR), R_(NDR), ALU.mult))
            cdv(lambda e: e.tensor_tensor(R_(T1), R_(NDI), R_(NDI), ALU.mult))
            dvd()
            yield
            cdv(lambda e: e.tensor_tensor(R_(DEN), R_(DEN), R_(T1), ALU.add))
            dvd()
            yield
            cdv(lambda e: e.reciprocal(R_(INVD), R_(DEN)))
            dvd()
            yield
            cdv(lambda e: e.scalar_tensor_tensor(R_(FR), R_(NDR), -1.0, R_(INVD),
                                                ALU.mult, ALU.mult))
            cdv(lambda e: e.tensor_tensor(R_(FI), R_(NDI), R_(INVD), ALU.mult))
            dvd()
            yield
            cdv(lambda e: e.tensor_tensor(R_(QFR), R_(QR), R_(FR), ALU.mult))
            cdv(lambda e: e.tensor_tensor(R_(T1), R_(QI), R_(FI), ALU.mult))
            cdv(lambda e: e.tensor_tensor(R_(QFI), R_(QR), R_(FI), ALU.mult))
            cdv(lambda e: e.tensor_tensor(R_(T2), R_(QI), R_(FR), ALU.mult))
            dvd()
            yield
            cdv(lambda e: e.tensor_tensor(R_(QFR), R_(QFR), R_(T1), ALU.add))
            cdv(lambda e: e.tensor_tensor(R_(QFI), R_(QFI), R_(T2), ALU.subtract))
            cdv(lambda e: e.tensor_tensor(R_(UR), R_(PR), R_(FR), ALU.mult))
            cdv(lambda e: e.tensor_tensor(R_(T1), R_(PI), R_(FI), ALU.mult))
            cdv(lambda e: e.tensor_tensor(R_(UI), R_(PR), R_(FI), ALU.mult))
            cdv(lambda e: e.tensor_tensor(R_(T2), R_(PI), R_(FR), ALU.mult))
            dvd()
            yield
            cdv(lambda e: e.tensor_tensor(R_(UR), R_(UR), R_(T1), ALU.subtract))
            cdv(lambda e: e.tensor_tensor(R_(UI), R_(UI), R_(T2), ALU.add))
            dvd()
            yield
            cdv(lambda e: e.tensor_tensor(R_(T1), R_(QFR), R_(PR), ALU.mult))
            cdv(lambda e: e.tensor_tensor(R_(T2), R_(QFI), R_(PI), ALU.mult))
            dvd()
            yield
            cdv(lambda e: e.tensor_tensor(R_(T1), R_(T1), R_(T2), ALU.subtract))
            dvd()
            yield
            cdv(lambda e: e.tensor_reduce(R_(SWR, 1), R_(T1), AX.X, ALU.add))
            cdv(lambda e: e.tensor_tensor(R_(T1), R_(QFR), R_(PI), ALU.mult))
            cdv(lambda e: e.tensor_tensor(R_(T2), R_(QFI), R_(PR), ALU.mult))
            dvd()
            yield
            cdv(lambda e: e.tensor_tensor(R_(T1), R_(T1), R_(T2), ALU.add))
            dvd()
            yield
            cdv(lambda e: e.tensor_reduce(R_(SWI, 1), R_(T1), AX.X, ALU.add))
            dvd()
            yield
            cdv(lambda e: e.tensor_scalar(R_(CR_, 1), R_(SWR, 1), R_(HS, 1), 1.0,
                                         ALU.mult, ALU.add))
            cdv(lambda e: e.tensor_scalar(R_(CI_, 1), R_(SWI, 1), R_(HS, 1), None, ALU.mult))
            dvd()
            yield
            cdv(lambda e: e.tensor_tensor(R_(T1, 1), R_(CR_, 1), R_(CR_, 1), ALU.mult))
            cdv(lambda e: e.tensor_tensor(R_(T2, 1), R_(CI_, 1), R_(CI_, 1), ALU.mult))
            dvd()
            yield
            cdv(lambda e: e.tensor_tensor(R_(T1, 1), R_(T1, 1), R_(T2, 1), ALU.add))
            dvd()
            yield
            cdv(lambda e: e.reciprocal(R_(T2, 1), R_(T1, 1)))
            dvd()
            yield
            cdv(lambda e: e.tensor_tensor(R_(T1, 1), R_(CR_, 1), R_(T2, 1), ALU.mult))
            dvd()
            yield
            cdv(lambda e: e.tensor_scalar(R_(CR_, 1), R_(T1, 1), R_(HS, 1), None, ALU.mult))
            cdv(lambda e: e.tensor_tensor(R_(T1, 1), R_(CI_, 1), R_(T2, 1), ALU.mult))
            dvd()
            yield
            cdv(lambda e: e.tensor_scalar(R_(CI_, 1), R_(T1, 1), R_(NHS, 1), None, ALU.mult))
            dvd()
            yield
            cdv(lambda e: e.tensor_scalar(R_(T1), R_(UR), R_(CR_, 1), None, ALU.mult))
            cdv(lambda e: e.tensor_scalar(R_(T2), R_(UI), R_(CI_, 1), None, ALU.mult))
            dvd()
            yield
            cdv(lambda e: e.tensor_tensor(R_(T1), R_(T1), R_(T2), ALU.subtract))
            cdv(lambda e: e.tensor_scalar(R_(T2), R_(UR), R_(CI_, 1), None, ALU.mult))
            dvd()
            yield
            cdv(lambda e: e.tensor_scalar(R_(UI), R_(UI), R_(CR_, 1), None, ALU.mult))
            dvd()
            yield
            cdv(lambda e: e.tensor_tensor(R_(UI), R_(UI), R_(T2), ALU.add))
            cdv(lambda e: e.tensor_copy(R_(UR), R_(T1)))
            dvd()
            yield
            cdv(lambda e: e.tensor_scalar(R_(NUI), R_(UI), -1.0, None, ALU.mult))
            DV_WOOD = cdv(lambda e: e.tensor_scalar(R_(NQI), R_(QI), -1.0, None, ALU.mult))
            dvd()
            yield

            PE_DG = pe(lambda e: e.matmul(pb[6][0:64, 0:64], onesr[0:1, 0:64], R_(FR),
                                          start=True, stop=True), waits=(("DV", DV_WOOD),))
            pe(lambda e: e.matmul(pb[6][0:64, 64:128], onesr[0:1, 0:64], R_(FI),
                                  start=True, stop=True))
            pe(lambda e: e.matmul(pb[7][0:64, 0:64], R_(UR), R_(QFR), start=True, stop=False))
            pe(lambda e: e.matmul(pb[7][0:64, 0:64], R_(NUI), R_(QFI), start=False, stop=True))
            pe(lambda e: e.matmul(pb[7][0:64, 64:128], R_(UR), R_(QFI), start=True, stop=False))
            PE_OUT = pe(lambda e: e.matmul(pb[7][0:64, 64:128], R_(UI), R_(QFR),
                                           start=False, stop=True))
            cdv(lambda e: e.tensor_tensor(M(TM1), pb[6][0:64, 0:64], i128[0:64, 0:64], ALU.mult),
               waits=(("PE", PE_OUT),))
            cdv(lambda e: e.tensor_tensor(M(TM2), pb[6][0:64, 64:128], i128[0:64, 0:64], ALU.mult))
            dvd()
            yield
            cdv(lambda e: e.tensor_tensor(M(E1R), M(TM1), pb[7][0:64, 0:64], ALU.subtract))
            DV_E1 = cdv(lambda e: e.tensor_tensor(M(E1I), M(TM2), pb[7][0:64, 64:128],
                                                 ALU.subtract))
            dvd()
            yield
            pe(lambda e: e.transpose(pb[6][0:64, 0:64], M(E1R), i128[0:64, 0:64]),
               waits=(("DV", DV_E1),))
            PE_E1T = pe(lambda e: e.transpose(pb[6][0:64, 64:128], M(E1I), i128[0:64, 0:64]))
            cdv(lambda e: e.tensor_copy(M(E1RT), pb[6][0:64, 0:64]), waits=(("PE", PE_E1T),))
            cdv(lambda e: e.tensor_copy(M(E1IT), pb[6][0:64, 64:128]))
            dvd()
            yield
            DV_E1TN = cdv(lambda e: e.tensor_scalar(M(E1ITN), M(E1IT), -1.0, None, ALU.mult))
            dvd()
            yield
            pe(lambda e: e.transpose(pb[6][0:64, 0:1], R_(PR), i128[0:1, 0:1]),
               waits=(("DV", DV_E1TN),))
            pe(lambda e: e.transpose(pb[6][0:64, 1:2], R_(PI), i128[0:1, 0:1]))
            PE_COLS = pe(lambda e: e.transpose(pb[6][0:64, 2:3], R_(BPR), i128[0:1, 0:1]))
            cdv(lambda e: e.tensor_copy(col(CPR), pb[6][0:64, 0:1]), waits=(("PE", PE_COLS),))
            cdv(lambda e: e.tensor_copy(col(CPI), pb[6][0:64, 1:2]))
            cdv(lambda e: e.tensor_copy(col(CBP), pb[6][0:64, 2:3]))
            cdv(lambda e: e.tensor_copy(krow[0:1, 0:64], R_(CTR)))
            DV_CT = cdv(lambda e: e.tensor_scalar(krow[0:1, 64:128], R_(CTI), -1.0,
                                                 None, ALU.mult))
            PE_CT = pe(lambda e: e.transpose(pb[6][0:128, 8:9], krow[0:1, 0:128],
                                             i128[0:1, 0:1]), waits=(("DV", DV_CT),))
            cdv(lambda e: e.tensor_copy(big32[:, 0:1], pb[6][0:128, 8:9]),
               waits=(("PE", PE_CT),))
            DV_NPI = cdv(lambda e: e.tensor_scalar(col(CNPI), col(CPI), -1.0, None, ALU.mult))
            pe(lambda e: e.matmul(pb[7][0:1, 0:64], col(CPR), M(E1RT), start=True, stop=False),
               waits=(("DV", DV_NPI),))
            pe(lambda e: e.matmul(pb[7][0:1, 0:64], col(CNPI), M(E1IT), start=False, stop=True))
            pe(lambda e: e.matmul(pb[7][0:1, 64:128], col(CPR), M(E1IT), start=True, stop=False))
            PE_E1P = pe(lambda e: e.matmul(pb[7][0:1, 64:128], col(CPI), M(E1RT),
                                           start=False, stop=True))
            cdv(lambda e: e.tensor_copy(R_(E1PR), pb[7][0:1, 0:64]), waits=(("PE", PE_E1P),))
            cdv(lambda e: e.tensor_copy(R_(E1PI), pb[7][0:1, 64:128]))
            cdv(lambda e: e.tensor_scalar(R_(WRR), R_(NDR), 2.0, None, ALU.add))
            DV_WR = cdv(lambda e: e.tensor_copy(R_(WRI), R_(NDI)))
            pe(lambda e: e.matmul(pb[6][0:64, 0:64], onesr[0:1, 0:64], R_(WRR),
                                  start=True, stop=True), waits=(("DV", DV_WR),))
            PE_WB = pe(lambda e: e.matmul(pb[6][0:64, 64:128], onesr[0:1, 0:64], R_(WRI),
                                          start=True, stop=True))
            cdv(lambda e: e.tensor_copy(M(WBR), pb[6][0:64, 0:64]), waits=(("PE", PE_WB),))
            cdv(lambda e: e.tensor_copy(M(WBI), pb[6][0:64, 64:128]))
            cdv(lambda e: e.tensor_tensor(M(TM1), M(E1R), M(WBR), ALU.mult))
            cdv(lambda e: e.tensor_tensor(M(TM2), M(E1I), M(WBI), ALU.mult))
            cdv(lambda e: e.tensor_tensor(M(TM3), M(E1R), M(WBI), ALU.mult))
            cdv(lambda e: e.tensor_tensor(M(TM4), M(E1I), M(WBR), ALU.mult))
            cdv(lambda e: e.tensor_tensor(M(TM1), M(TM1), M(TM2), ALU.subtract))
            DV_CSC = cdv(lambda e: e.tensor_tensor(M(TM3), M(TM3), M(TM4), ALU.add))
            pe(lambda e: e.matmul(pb[7][0:64, 0:64], R_(E1PR), R_(QR), start=True, stop=False),
               waits=(("DV", DV_CSC),))
            pe(lambda e: e.matmul(pb[7][0:64, 0:64], R_(E1PI), R_(QI), start=False, stop=True))
            pe(lambda e: e.matmul(pb[7][0:64, 64:128], R_(E1PR), R_(NQI), start=True, stop=False))
            PE_HO = pe(lambda e: e.matmul(pb[7][0:64, 64:128], R_(E1PI), R_(QR),
                                          start=False, stop=True))
            cdv(lambda e: e.scalar_tensor_tensor(M(XR), pb[7][0:64, 0:64], sc64[0:64, 1:2],
                                                M(TM1), ALU.mult, ALU.add),
               waits=(("PE", PE_HO),))
            DV_AB = cdv(lambda e: e.scalar_tensor_tensor(M(XI), pb[7][0:64, 64:128],
                                                        sc64[0:64, 1:2], M(TM3),
                                                        ALU.mult, ALU.add))
            pe(lambda e: e.matmul(pb[6][0:1, 0:64], col(CBP), vctr[:, :], start=True,
                                  stop=True), waits=(("DV", DV_AB),))
            PE_B0 = pe(lambda e: e.matmul(pb[6][0:1, 64:128], col(CBP), vcti[:, :],
                                          start=True, stop=True))
            cdv(lambda e: e.tensor_copy(R_(B0R), pb[6][0:1, 0:64]), waits=(("PE", PE_B0),))
            DV_B0 = cdv(lambda e: e.tensor_copy(R_(B0I), pb[6][0:1, 64:128]))
            pe(lambda e: e.transpose(pb[6][0:64, 0:1], R_(B0R), i128[0:1, 0:1]),
               waits=(("DV", DV_B0),))
            PE_B0C = pe(lambda e: e.transpose(pb[6][0:64, 1:2], R_(B0I), i128[0:1, 0:1]))
            cdv(lambda e: e.tensor_copy(col(CB0R), pb[6][0:64, 0:1]), waits=(("PE", PE_B0C),))
            DV_B0C = cdv(lambda e: e.tensor_copy(col(CB0I), pb[6][0:64, 1:2]))
            pe(lambda e: e.matmul(pb[7][0:64, 0:1], M(E1RT), col(CB0R), start=True, stop=False),
               waits=(("DV", DV_B0C),))
            pe(lambda e: e.matmul(pb[7][0:64, 0:1], M(E1ITN), col(CB0I), start=False, stop=True))
            pe(lambda e: e.matmul(pb[7][0:64, 1:2], M(E1IT), col(CB0R), start=True, stop=False))
            PE_BB = pe(lambda e: e.matmul(pb[7][0:64, 1:2], M(E1RT), col(CB0I),
                                          start=False, stop=True))
            cdv(lambda e: e.tensor_scalar(col(CBBR), pb[7][0:64, 0:1], sc64[0:64, 2:3], None,
                                         ALU.mult), waits=(("PE", PE_BB),))
            DV_BB = cdv(lambda e: e.tensor_scalar(col(CBBI), pb[7][0:64, 1:2], sc64[0:64, 2:3],
                                                 None, ALU.mult))
            pe(lambda e: e.transpose(pb[6][0:64, 0:64], M(XR), i128[0:64, 0:64]),
               waits=(("DV", DV_BB),))
            PE_X0T = pe(lambda e: e.transpose(pb[6][0:64, 64:128], M(XI), i128[0:64, 0:64]))
            cdv(lambda e: e.tensor_copy(M(XRT), pb[6][0:64, 0:64]), waits=(("PE", PE_X0T),))
            cdv(lambda e: e.tensor_copy(M(XIT), pb[6][0:64, 64:128]))
            cdv(lambda e: e.tensor_scalar(M(XITN), M(XIT), -1.0, None, ALU.mult))
            DV_LAST = cdv(lambda e: e.tensor_scalar(M(XIN), M(XI), -1.0, None, ALU.mult))

            STG = mats[0:64, 64 * TM3:64 * TM3 + 128]

            def packer(c, target):
                nonlocal DV_LAST, DV_PKC
                blocks = []
                if target in ("rop", "both"):
                    blocks += [(rop, 0, c["rt"], c["itn"]), (rop, 1, c["it"], c["rt"])]
                if target in ("wop", "both"):
                    blocks += [(wop, 0, c["r"], c["inn"]), (wop, 1, c["i"], c["r"])]
                first = True
                for idx, (dstT, cb, p1, p2) in enumerate(blocks):
                    dv(lambda e, idx=idx, p1=p1: e.tensor_copy(
                        pkstg[:, idx * 128:idx * 128 + 64], M(p1)),
                       waits=(("DV", DV_LAST),) if first else ())
                    first = False
                    DV_S = dv(lambda e, idx=idx, p2=p2: e.tensor_copy(
                        pkstg[:, idx * 128 + 64:idx * 128 + 128], M(p2)))
                DV_PKC = DV_S
                dvd()
                PE_P = 0
                for idx, (dstT, cb, p1, p2) in enumerate(blocks):
                    PE_P = pe(lambda e, idx=idx: e.transpose(
                        pb[5][0:128, idx * 64:(idx + 1) * 64],
                        pkstg[:, idx * 128:(idx + 1) * 128], i128[0:64, 0:64]),
                        waits=(("DV", DV_S),) if idx == 0 else ())
                for idx, (dstT, cb, p1, p2) in enumerate(blocks):
                    DV_LAST = dv(lambda e, idx=idx, dstT=dstT, cb=cb: e.tensor_copy(
                        dstT[:, cb * 64:(cb + 1) * 64],
                        pb[5][0:128, idx * 64:(idx + 1) * 64]),
                        waits=(("PE", PE_P),) if idx == 0 else ())
                dvd()

            def stack2(dst, cr, ci, neg_i):
                nonlocal DV_LAST
                pe(lambda e, cr=cr: e.transpose(pb[5][0:1, 384:448], cr, i128[0:64, 0:64]),
                   waits=(("DV", DV_LAST),))
                PE_T = pe(lambda e, ci=ci: e.transpose(pb[5][0:1, 448:512], ci,
                                                       i128[0:64, 0:64]))
                cdv(lambda e: e.tensor_copy(krow[0:1, 0:64], pb[5][0:1, 384:448]),
                   waits=(("PE", PE_T),))
                DV_R = cdv(lambda e: e.tensor_scalar(krow[0:1, 64:128], pb[5][0:1, 448:512],
                                                    -1.0 if neg_i else 1.0, None, ALU.mult))
                PE_T2 = pe(lambda e: e.transpose(pb[5][0:128, 128:129], krow[0:1, 0:128],
                                                 i128[0:1, 0:1]), waits=(("DV", DV_R),))
                DV_LAST = cdv(lambda e, dst=dst: e.tensor_copy(dst, pb[5][0:128, 128:129]),
                             waits=(("PE", PE_T2),))

            def sqfuse(c, n, wait=None):
                nonlocal DV_LAST, DV_SQL
                pe(lambda e, c=c: e.matmul(pb[7][0:64, 0:64], M(c["rt"]), M(c["r"]),
                                           start=True, stop=False),
                   waits=(("DV", DV_LAST if wait is None else wait),))
                pe(lambda e, c=c: e.matmul(pb[7][0:64, 0:64], M(c["itn"]), M(c["i"]),
                                           start=False, stop=True))
                pe(lambda e, c=c: e.matmul(pb[4][0:64, 0:64], M(c["rt"]), M(c["i"]),
                                           start=True, stop=False))
                pe(lambda e, c=c: e.matmul(pb[4][0:64, 0:64], M(c["it"]), M(c["r"]),
                                           start=False, stop=True))
                pe(lambda e, c=c: e.matmul(pb[7][0:64, 64:128], M(c["r"]), M(c["rt"]),
                                           start=True, stop=False))
                pe(lambda e, c=c: e.matmul(pb[7][0:64, 64:128], M(c["inn"]), M(c["it"]),
                                           start=False, stop=True))
                pe(lambda e, c=c: e.matmul(pb[4][0:64, 64:128], M(c["r"]), M(c["it"]),
                                           start=True, stop=False))
                PE_SQ = pe(lambda e, c=c: e.matmul(pb[4][0:64, 64:128], M(c["i"]),
                                                   M(c["rt"]), start=False, stop=True))
                dv(lambda e, n=n: e.tensor_copy(M(n["r"]), pb[7][0:64, 0:64]),
                   waits=(("PE", PE_SQ),))
                dv(lambda e, n=n: e.tensor_copy(M(n["i"]), pb[7][0:64, 64:128]))
                dv(lambda e, n=n: e.tensor_copy(M(n["rt"]), pb[4][0:64, 0:64]))
                dv(lambda e, n=n: e.tensor_copy(M(n["it"]), pb[4][0:64, 64:128]))
                dvd()
                dv(lambda e, n=n: e.tensor_scalar(M(n["itn"]), M(n["it"]), -1.0, None,
                                                  ALU.mult))
                DV_LAST = dv(lambda e, n=n: e.tensor_scalar(M(n["inn"]), M(n["i"]), -1.0,
                                                            None, ALU.mult))
                DV_SQL = DV_LAST
                dvd()

            DV_SQL = 0
            DV_PKC = 0
            cur = dict(r=XR, i=XI, rt=XRT, it=XIT, itn=XITN, inn=XIN)
            nxt = dict(r=YR, i=YI, rt=YRT, it=YIT, itn=YITN, inn=YIN)

            packer(cur, "wop")
            stack2(bstk[:, 0:1], col(CBBR), col(CBBI), False)
            PE_W0 = pe(lambda e: e.matmul(pb[7][0:128, 0:1], wop[:, :], bstk[:, 0:1],
                                          start=True, stop=True), waits=(("DV", DV_LAST),))
            DV_LAST = cdv(lambda e: e.tensor_copy(wpr[:, 0:1], pb[7][0:128, 0:1]),
                         waits=(("PE", PE_W0),))

            setC = dict(r=12, rt=13, i=14, it=15, inn=16, itn=17)
            sets = [cur, nxt, setC]

            def blocks_of(c, target):
                b = []
                if target in ("rop", "both"):
                    b += [(rop, 0, c["rt"], c["itn"]), (rop, 1, c["it"], c["rt"])]
                if target in ("wop", "both"):
                    b += [(wop, 0, c["r"], c["inn"]), (wop, 1, c["i"], c["r"])]
                return b

            def pk_copy(c, target, w0=()):
                bl = blocks_of(c, target)
                AC_S = 0
                for idx, (dstT, cb, p1, p2) in enumerate(bl):
                    ac(lambda e, idx=idx, p1=p1: e.activation(
                        pkstg[:, idx * 128:idx * 128 + 64], M(p1), AF.Copy),
                       waits=w0 if idx == 0 else ())
                    AC_S = ac(lambda e, idx=idx, p2=p2: e.activation(
                        pkstg[:, idx * 128 + 64:idx * 128 + 128], M(p2), AF.Copy))
                S.drain("ac")
                return AC_S

            def pk_t(c, target, ac_s, dv_pke):
                bl = blocks_of(c, target)
                PE_P = 0
                for idx in range(len(bl)):
                    PE_P = pe(lambda e, idx=idx: e.transpose(
                        pb[5][0:128, idx * 64:(idx + 1) * 64],
                        pkstg[:, idx * 128:(idx + 1) * 128], i128[0:64, 0:64]),
                        waits=(("AC", ac_s), ("DV", dv_pke)) if idx == 0 else ())
                return PE_P

            def pk_evac(c, target, pe_p):
                nonlocal DV_LAST
                bl = blocks_of(c, target)
                first = True
                o = 0
                while o < len(bl):
                    dstT = bl[o][0]
                    DV_LAST = dv(lambda e, o=o, dstT=dstT: e.tensor_copy(
                        dstT[:, 0:128], pb[5][0:128, o * 64:o * 64 + 128]),
                        waits=(("PE", pe_p),) if first else ())
                    first = False
                    o += 2
                dvd()
                return DV_LAST

            def sq_mm(c, wait):
                pe(lambda e, c=c: e.matmul(pb[7][0:64, 0:64], M(c["rt"]), M(c["r"]),
                                           start=True, stop=False),
                   waits=(("DV", wait),))
                pe(lambda e, c=c: e.matmul(pb[7][0:64, 0:64], M(c["itn"]), M(c["i"]),
                                           start=False, stop=True))
                pe(lambda e, c=c: e.matmul(pb[4][0:64, 0:64], M(c["rt"]), M(c["i"]),
                                           start=True, stop=False))
                pe(lambda e, c=c: e.matmul(pb[4][0:64, 0:64], M(c["it"]), M(c["r"]),
                                           start=False, stop=True))
                pe(lambda e, c=c: e.matmul(pb[7][0:64, 64:128], M(c["r"]), M(c["rt"]),
                                           start=True, stop=False))
                pe(lambda e, c=c: e.matmul(pb[7][0:64, 64:128], M(c["inn"]), M(c["it"]),
                                           start=False, stop=True))
                pe(lambda e, c=c: e.matmul(pb[4][0:64, 64:128], M(c["r"]), M(c["it"]),
                                           start=True, stop=False))
                return pe(lambda e, c=c: e.matmul(pb[4][0:64, 64:128], M(c["i"]),
                                                  M(c["rt"]), start=False, stop=True))

            def M2(k):
                return mats[0:64, 64 * k:64 * k + 128]

            def sq_evac(n, pe_sq, acw=0):
                nonlocal DV_LAST
                dv(lambda e, n=n: e.tensor_copy(M2(n["r"]), pb[7][0:64, 0:128]),
                   waits=(("PE", pe_sq), ("AC", acw)))
                dv(lambda e, n=n: e.tensor_copy(M2(n["i"]), pb[4][0:64, 0:128]))
                dvd()
                DV_LAST = dv(lambda e, n=n: e.tensor_scalar(M2(n["inn"]), M2(n["i"]),
                                                            -1.0, None, ALU.mult))
                dvd()
                return DV_LAST

            def tgt(k):
                return "both" if k <= 8 else ("rop" if k <= 11 else "wop")

            DV_NEG = DV_LAST
            AC_PKC = {-1: 0, -2: 0}
            PKE_H = {-1: 0}
            PKT_H = {-1: 0}
            DVN_H = {-1: DV_LAST}
            PE_SQn = {}
            for k in range(min(18, stage + 1)):
                ck = sets[k % 3]
                nk = sets[(k + 1) % 3]
                km = k - 1
                cm = sets[km % 3] if km >= 0 else None
                if k < 16:
                    PE_SQn[k] = sq_mm(ck, DV_NEG)
                if km >= 0:
                    PE_PKT = pk_t(cm, tgt(km), AC_PKC[km], PKE_H[km - 1])
                    PKT_H[km] = PE_PKT
                if k < 16:
                    DV_NEG = sq_evac(nk, PE_SQn[k], AC_PKC[k - 2])
                    DVN_H[k] = DV_NEG
                if km >= 0:
                    DV_PKE = pk_evac(cm, tgt(km), PE_PKT)
                    PKE_H[km] = DV_PKE
                if km >= 0 and km <= 11:
                    m = 1 << km
                    ngrp = (m + 511) // 512
                    pe_rg = []
                    dv_rg = []
                    for g in range(ngrp):
                        w = min(512, m - g * 512)
                        o0 = g * 512
                        wt = DV_PKE if g == 0 else dv_rg[g - 1]
                        pe_rg.append(pe(lambda e, o0=o0, w=w: e.matmul(
                            pb[6][0:128, 0:w], rop[:, :], big32[:, o0:o0 + w],
                            start=True, stop=True), waits=(("DV", wt),)))
                        dv_rg.append(dv(lambda e, o0=o0, w=w, m=m: e.tensor_copy(
                            big32[:, m + o0:m + o0 + w], pb[6][0:128, 0:w]),
                            waits=(("PE", pe_rg[g]),)))
                        dvd()
                    DV_LAST = dv_rg[-1]
                    if km < 8:
                        S.ops["pe"].append(lambda e, sems: e.drain())
                        PE_W = pe(lambda e, m=m: e.matmul(
                            pb[6][0:128, 256:256 + m], wop[:, :], wpr[:, 0:m],
                            start=True, stop=True), waits=(("DV", DV_PKE),))
                        DV_LAST = dv(lambda e, m=m: e.tensor_copy(
                            wpr[:, m:2 * m], pb[6][0:128, 256:256 + m]),
                            waits=(("PE", PE_W),))
                        dvd()
                    if km == 8:
                        DV_LAST = dv(lambda e: e.tensor_copy(ppk[:, :], wop[:, :]))
                        dvd()
                if km >= 12:
                    PE_NW = pe(lambda e: e.matmul(pb[6][0:128, 0:256], wop[:, :],
                                                  wpr[:, :], start=True, stop=True),
                               waits=(("DV", DV_PKE),))
                    pe(lambda e, c=cm: e.matmul(pb[6][0:64, 256:257], M(c["rt"]),
                                                col(CBBR), start=True, stop=False))
                    pe(lambda e, c=cm: e.matmul(pb[6][0:64, 256:257], M(c["itn"]),
                                                col(CBBI), start=False, stop=True))
                    pe(lambda e, c=cm: e.matmul(pb[6][0:64, 257:258], M(c["it"]),
                                                col(CBBR), start=True, stop=False))
                    PE_NM = pe(lambda e, c=cm: e.matmul(pb[6][0:64, 257:258], M(c["rt"]),
                                                        col(CBBI), start=False, stop=True))
                    dv(lambda e: e.tensor_tensor(wpr[:, :], wpr[:, :], pb[6][0:128, 0:256],
                                                 ALU.add), waits=(("PE", PE_NM),))
                    dv(lambda e: e.tensor_tensor(col(CBBR), col(CBBR),
                                                 pb[6][0:64, 256:257], ALU.add))
                    DV_LAST = dv(lambda e: e.tensor_tensor(col(CBBI), col(CBBI),
                                                           pb[6][0:64, 257:258], ALU.add))
                    dvd()
                if k <= 16:
                    AC_PKC[k] = pk_copy(ck, tgt(k), w0=(
                        ("DV", DVN_H[k - 1]), ("PE", PKT_H[km])))
                yield

            if stage < 18:
                SP_KPAD = ("K", 0)
                cdv(lambda e: e.memset(stb[:, :], 0.0))
                cdv(lambda e: e.tensor_copy(emf[:, :], emi[:, :]), waits=W_PAR)
                DV_CONSTR = cdv(lambda e: e.tensor_scalar(mmapb[:, :], io15[:, :],
                                                          emf[:, 0:1], 2.0,
                                                          ALU.is_equal, ALU.mult))
                EXP["SP_KPAD"] = SP_KPAD
                EXP["K_ODD"] = ("K2", 0)
                EXP["DV_CONSTR"] = DV_CONSTR
                yield
                return
            stack2(bstk[:, 0:1], col(CBBR), col(CBBI), True)
            DV_BSTK = DV_LAST
            pe_k = {}
            dv_ke = {}
            kdma = {}
            for g in range(8):
                bank = 6 if g % 2 == 0 else 7
                ks = (g % 2) * 512
                wk_pe = (("DV", DV_BSTK),) if g < 2 else (("DV", dv_ke[g - 2]),)
                pe_k[g] = pe(lambda e, g=g, bank=bank: e.matmul(
                    pb[bank][0:1, 0:512], bstk[:, :],
                    big32[:, g * 512:(g + 1) * 512],
                    start=True, stop=True), waits=wk_pe)
                wk = [("PE", pe_k[g])] + ([kdma[g - 2]] if g >= 2 else [])
                dv_ke[g] = cdv(lambda e, bank=bank, ks=ks: e.tensor_copy(
                    krow[0:1, ks:ks + 512], pb[bank][0:1, 0:512]), waits=wk)
                if g == 0:
                    dv_ke[g] = cdv(lambda e: e.tensor_tensor(
                        krow[0:1, 0:1], krow[0:1, 0:1], svec[0:1, 576:577], ALU.add))
                kdma[g] = dma("sp", "K" if g % 2 == 0 else "K2",
                              bass.AP(d_kscr, 128 + g * 512, [[512, 1], [1, 512]]),
                              krow[0:1, ks:ks + 512], waits=(("DV", dv_ke[g]),))
            DV_Z = cdv(lambda e: e.memset(krow[0:1, 0:512], 0.0), waits=(kdma[6],))
            dma("sp", "K", bass.AP(d_kscr, 0, [[128, 1], [1, 128]]), krow[0:1, 0:128],
                waits=(("DV", DV_Z),))
            SP_KPAD = dma("sp", "K", bass.AP(d_kscr, 4224, [[384, 1], [1, 384]]),
                          krow[0:1, 0:384])
            EXP["K_ODD"] = kdma[7]

            cdv(lambda e: e.tensor_copy(moutt[0:64, 0:256], big32[0:64, 0:256]))
            DV_MO = cdv(lambda e: e.tensor_scalar(moutt[64:128, 0:256], big32[64:128, 0:256],
                                                 -1.0, None, ALU.mult))
            for (sub, c0) in ((0, 128), (1, 0)):
                pe(lambda e, c0=c0: e.transpose(pb[6][0:128, 0:64], wpr[0:64, c0:c0 + 128],
                                                i128[0:64, 0:64]), waits=(("DV", DV_MO),))
                S.ops["pe"].append(lambda e, sems: e.drain())
                PE_MT = pe(lambda e, c0=c0: e.transpose(pb[6][0:128, 64:128],
                                                        wpr[64:128, c0:c0 + 128],
                                                        i128[64:128, 64:128]))
                DV_MT = cdv(lambda e: e.tensor_copy(tstg[:, 0:128], pb[6][0:128, 0:128]),
                           waits=(("PE", PE_MT),))
                PE_MJ = pe(lambda e: e.matmul(pb[7][:, 0:128], j128[:, :], tstg[:, 0:128],
                                              start=True, stop=True), waits=(("DV", DV_MT),))
                DV_MO = cdv(lambda e, sub=sub: e.tensor_copy(mint[:, sub * 128:(sub + 1) * 128],
                                                            pb[7][:, 0:128]),
                           waits=(("PE", PE_MJ),))
            DV_MIN = DV_MO
            cdv(lambda e: e.memset(stb[:, :], 0.0))
            cdv(lambda e: e.tensor_copy(emf[:, :], emi[:, :]), waits=W_PAR)
            DV_CONSTR = cdv(lambda e: e.tensor_scalar(mmapb[:, :], io15[:, :], emf[:, 0:1],
                                                     2.0, ALU.is_equal, ALU.mult))


            EXP["SP_KPAD"] = SP_KPAD
            EXP["DV_CONSTR"] = DV_CONSTR
            yield

        _cgen = _constr_gen()

        def pump(n=1):
            for _ in range(n):
                try:
                    next(_cgen)
                except StopIteration:
                    return
        pump(15)
        for fc in range(NFC):
            sl = (fc % 4) * ntok
            ql = (fc % 2) * ntok
            wsq = [x1[fc]]
            if fc >= 2:
                wsq += [("PE", (lambda f=fc: pe_st[f - 2]))]
            ac_sq[fc] = ac(lambda e, sl=sl, ql=ql: e.activation(
                xq16[:, ql:ql + ntok], xb16[:, sl:sl + ntok], AF.Square),
                waits=wsq)
            S.drain("ac")
            for h in range(2):
                pe(lambda e, sl=sl, h=h, fc=fc: e.matmul(
                    pb[h][0:128, 0:H], onescb[:, :],
                    xb16[:, sl + h * H:sl + (h + 1) * H],
                    start=(fc == 0), stop=(fc == NFC - 1)),
                   waits=[x1[fc]] if h == 0 else ())
                pe_st[fc] = pe(lambda e, h=h, fc=fc, ql=ql: e.matmul(
                    pb[2 + h][0:128, 0:H], onescb[:, :],
                    xq16[:, ql + h * H:ql + (h + 1) * H],
                    start=(fc == 0), stop=(fc == NFC - 1)),
                    waits=[("AC", ac_sq[fc])] if h == 0 else ())
            pump(1)
        PE_STATS = pe_st[NFC - 1]
        for h in range(2):
            dv(lambda e, h=h: e.tensor_scalar(mbr[:, h * H:(h + 1) * H],
                                              pb[h][0:128, 0:H],
                                              1.0 / D, None, ALU.mult),
               waits=(("PE", PE_STATS),))
            dv(lambda e, h=h: e.tensor_scalar(mbr[:, ntok + h * H:ntok + (h + 1) * H],
                                              pb[2 + h][0:128, 0:H],
                                              1.0 / D, None, ALU.mult))
        dvd()
        dv(lambda e: e.tensor_tensor(xn32[:, 0:ntok], mbr[:, 0:ntok], mbr[:, 0:ntok],
                                     ALU.mult))
        dvd()
        dv(lambda e: e.tensor_tensor(mbr[:, ntok:2 * ntok], mbr[:, ntok:2 * ntok],
                                     xn32[:, 0:ntok], ALU.subtract))
        dvd()
        DV_VAR = dv(lambda e: e.tensor_scalar(mbr[:, ntok:2 * ntok],
                                              mbr[:, ntok:2 * ntok], 1e-5,
                                              None, ALU.add))
        dvd()
        ac(lambda e: e.activation(xn32[:, ntok:2 * ntok], mbr[:, ntok:2 * ntok], AF.Ln),
           waits=(("DV", DV_VAR),))
        S.drain("ac")
        AC_RSTD = ac(lambda e: e.activation(mbr[:, ntok:2 * ntok], xn32[:, ntok:2 * ntok],
                                            AF.Exp, scale=-0.5))
        S.drain("ac")
        DV_BCAST = DV_VAR

        pump(9999)

        x2 = {}
        dv_xn = {}
        dv_sub = {}
        for fc in range(NFC):
            w = [("PE", PE_STATS)] if fc < 4 else [("DV", (lambda f=fc: dv_sub[f - 4]))]
            x2[fc] = dma("sp", f"X2{fc % 4}",
                         xr32[:, (fc % 4) * ntok:(fc % 4 + 1) * ntok],
                         d_x[fc * 128:(fc + 1) * 128, :], waits=w)
        pe_h = {}
        for fc in range(NFC):
            xsl = (fc % 4) * ntok
            nsl = (fc % 2) * ntok
            dv_sub[fc] = dv(lambda e, xsl=xsl, nsl=nsl: e.tensor_tensor(
                xn32[:, nsl:nsl + ntok], xr32[:, xsl:xsl + ntok], mbr[:, 0:ntok],
                ALU.subtract),
                waits=[x2[fc], ("DV", DV_BCAST), ("AC", AC_RSTD)]
                + ([("PE", pe_h[fc - 2])] if fc >= 2 else []))
            dvd()
            dv_xn[fc] = dv(lambda e, nsl=nsl, fc=fc: e.tensor_tensor(
                xhi[:, fc * ntok:(fc + 1) * ntok], xn32[:, nsl:nsl + ntok],
                mbr[:, ntok:2 * ntok], ALU.mult))
            dvd()
            w = [("DV", dv_xn[fc])]
            if fc == 0:
                w += [SC_RWT, W_PARB_LAST]
            for h in range(2):
                pe(lambda e, fc=fc, h=h: e.matmul(
                    pb[0 + h][0:15, 0:H], rwt[:, fc * 15:(fc + 1) * 15],
                    xhi[:, fc * ntok + h * H:fc * ntok + (h + 1) * H],
                    start=(fc == 0), stop=(fc == NFC - 1)), waits=w if h == 0 else ())
                pe_h[fc] = pe(lambda e, fc=fc, h=h: e.matmul(
                    pb[4 + h][0:128, 0:H], atb[:, fc * 128:(fc + 1) * 128],
                    xhi[:, fc * ntok + h * H:fc * ntok + (h + 1) * H],
                    start=(fc == 0), stop=(fc == NFC - 1)))
            pump(6)
        PE_RTH = pe_h[NFC - 1]
        pump(9999)
        SP_KPAD = EXP["SP_KPAD"]
        K_ODD = EXP["K_ODD"]
        DV_CONSTR = EXP["DV_CONSTR"]

        wd = {}
        for g in range(4):
            wd[0] = dma("gp", "WA", wbuf[0][:, g * 1024:(g + 1) * 1024],
                        bass.AP(d_w5.tensor, g * 8 * 128 * 128,
                                [[128, 128], [128 * 128, 8], [1, 128]]))
        for g in range(4):
            wd[1] = dma("gp", "WB", wbuf[1][:, g * 1024:(g + 1) * 1024],
                        bass.AP(d_w5.tensor, (32 + g * 8) * 128 * 128,
                                [[128, 128], [128 * 128, 8], [1, 128]]))
        dma("gp", "TV", tstg[:, 0:128], bass.AP(d_kscr, 1, [[1, 128], [1, 128]]),
            waits=(SP_KPAD, K_ODD, ("DV", DV_CONSTR)))
        GP_TREV = dma("gp", "TV", tstg[:, 128:256],
                      bass.AP(d_kscr, 129, [[1, 128], [1, 128]]))
        pe(lambda e: e.matmul(pb[6][:, 0:128], j128[:, :], tstg[:, 0:128],
                              start=True, stop=True),
           waits=(GP_TREV,))
        PE_TT = pe(lambda e: e.matmul(pb[6][:, 128:256], j128[:, :], tstg[:, 128:256],
                                      start=True, stop=True))
        dv(lambda e: e.tensor_copy(ttri[:, :], pb[6][:, 0:128]), waits=(("PE", PE_TT),))
        DV_TT = dv(lambda e: e.tensor_copy(tful[:, :], pb[6][:, 128:256]))
        dvd()
        PE_RBT = pe(lambda e: e.transpose(pb[6][0:1, 376:391], rb15[:, 0:1],
                                          i128[0:15, 0:15]),
                    waits=(("PE", PE_RTH), ("DV", DV_CONSTR), ("DV", DV_TT)))
        DV_RBT = dv(lambda e: e.tensor_copy(scr15[0:1, 0:15], pb[6][0:1, 376:391]),
                    waits=(("PE", PE_RBT),))
        dvd()
        PE_RBX = pe(lambda e: e.matmul(pb[6][0:128, 392:407], onesr[0:1, 0:128],
                                       scr15[0:1, 0:15], start=True, stop=True),
                    waits=(("DV", DV_RBT),))
        DV_RC = dv(lambda e: e.tensor_copy(rbx[:, :], pb[6][0:128, 392:407]),
                   waits=(("PE", PE_RBX),))
        dvd()
        PE_TRL = 0
        DV_LGH = {}
        for h in range(2):
            DV_LG = dv(lambda e, h=h: e.tensor_scalar(
                lgsb[:, 0:H], pb[0 + h][0:15, 0:H], 1.0, None, ALU.mult),
                waits=[("DV", DV_RC)] + ([("PE", PE_TRL)] if h else []))
            DV_LGH[h] = DV_LG
            dvd()
            for tt in range(HTIL):
                t = h * HTIL + tt
                PE_TRL = pe(lambda e, t=t, tt=tt: e.transpose(
                    pb[6][0:128, t * 15:(t + 1) * 15],
                    lgsb[:, tt * 128:(tt + 1) * 128],
                    i128[0:15, 0:15]), waits=(("DV", DV_LG), ("DV", DV_TT)))
        DV_LGT = dv(lambda e: e.tensor_copy(lgt[:, :], pb[6][0:128, 0:NTIL * 15]),
                    waits=(("PE", PE_TRL),))
        dvd()

        web = [wd[0], ("DV", DV_LGH[1])]
        for h in range(2):
            for db in range(32):
                pe(lambda e, h=h, db=db: e.matmul(
                    pb[0 + h][0:128, 0:H],
                    wbuf[0][:, db * 128:(db + 1) * 128],
                    xhi[:, db * ntok + h * H:db * ntok + (h + 1) * H],
                    start=(db == 0), stop=False),
                   waits=web if (h == 0 and db == 0) else ())
        W15 = NTIL * 15
        for t in range(NTIL):
            dv(lambda e, t=t: e.tensor_tensor(
                lgt[:, t * 15:(t + 1) * 15], lgt[:, t * 15:(t + 1) * 15],
                rbx[:, :], ALU.add))
        dvd()
        for t in range(NTIL):
            dv(lambda e, t=t: e.max(mx8[:, t * 8:(t + 1) * 8],
                                    lgt[:, t * 15:(t + 1) * 15]))
        dvd()
        for t in range(NTIL):
            dv(lambda e, t=t: e.tensor_scalar(
                selm8[:, t * 15:(t + 1) * 15], lgt[:, t * 15:(t + 1) * 15],
                mx8[:, t * 8 + 7:t * 8 + 8], None, ALU.is_ge))
        for t in range(NTIL):
            dv(lambda e, t=t: e.tensor_scalar(
                lgt[:, t * 15:(t + 1) * 15], lgt[:, t * 15:(t + 1) * 15],
                mx8[:, t * 8:t * 8 + 1], None, ALU.subtract))
        DV_PRE = S.cnt["DV"]
        dvd()
        AC_EXP = ac(lambda e: e.activation(pmt[:, 0:W15], lgt[:, 0:W15], AF.Exp),
                    waits=(("DV", DV_PRE),))
        S.drain("ac")
        for t in range(NTIL):
            dv(lambda e, t=t: e.tensor_reduce(esum8[:, t:t + 1],
                                              pmt[:, t * 15:(t + 1) * 15],
                                              AX.X, ALU.add),
               waits=(("AC", AC_EXP),) if t == 0 else ())
        dvd()
        dv(lambda e: e.reciprocal(esum8[:, 8:8 + NTIL], esum8[:, 0:NTIL]))
        dvd()
        dv(lambda e: e.tensor_tensor(pmt[:, 0:W15], pmt[:, 0:W15], selm8[:, 0:W15],
                                     ALU.mult))
        dvd()
        dv_pm = 0
        for t in range(NTIL):
            dv_pm = dv(lambda e, t=t: e.tensor_scalar(
                pmt[:, t * 15:(t + 1) * 15], pmt[:, t * 15:(t + 1) * 15],
                esum8[:, 8 + t:9 + t], None, ALU.mult))
        dvd()
        DV_PMTS = 0
        for h in range(2):
            PE_TB = 0
            for tt in range(HTIL):
                t = h * HTIL + tt
                PE_TB = pe(lambda e, t=t, tt=tt: e.transpose(
                    pb[7][0:15, tt * 128:(tt + 1) * 128], pmt[:, t * 15:(t + 1) * 15],
                    i128[0:128, 0:128]),
                    waits=(("DV", dv_pm if h == 0 else DV_PMTS),) if tt == 0 else ())
            DV_PMTS = dv(lambda e, h=h: e.tensor_copy(pmts[:, h * H:(h + 1) * H],
                                                      pb[7][0:15, 0:H]),
                         waits=(("PE", PE_TB),))
            dvd()
        DV_HW = 0
        for h in range(2):
            PE_W8 = pe(lambda e, h=h: e.matmul(pb[6][0:8, 0:H], mmapb[:, :],
                                               pmts[:, h * H:(h + 1) * H],
                                               start=True, stop=True),
                       waits=(("DV", DV_PMTS), ("DV", DV_CONSTR)))
            DV_W8 = dv(lambda e, h=h: e.tensor_copy(w8s[:, h * H:(h + 1) * H],
                                                    pb[6][0:8, 0:H]),
                       waits=(("PE", PE_W8),))
            dvd()
            PE_W16 = pe(lambda e, h=h: e.matmul(pb[7][0:128, 0:H], rep8[:, :],
                                                w8s[:, h * H:(h + 1) * H],
                                                start=True, stop=True),
                        waits=(("DV", DV_W8),))
            DV_W16 = dv(lambda e, h=h: e.tensor_copy(w16s[:, h * H:(h + 1) * H],
                                                     pb[7][0:128, 0:H]),
                        waits=(("PE", PE_W16),))
            dvd()
            DV_HW = dv(lambda e, h=h: e.tensor_tensor(
                hwb[:, h * H:(h + 1) * H], pb[4 + h][0:128, 0:H],
                w16s[:, h * H:(h + 1) * H], ALU.mult), waits=(("DV", DV_W16),))
            dvd()

        web1 = [wd[1], ("DV", DV_LGH[1])]
        for h in range(2):
            for db in range(32):
                pe(lambda e, h=h, db=db: e.matmul(
                    pb[2 + h][0:128, 0:H],
                    wbuf[1][:, db * 128:(db + 1) * 128],
                    xhi[:, db * ntok + h * H:db * ntok + (h + 1) * H],
                    start=(db == 0), stop=False),
                   waits=web1 if (h == 0 and db == 0) else ())

        pe_base = {}
        ac_g = {}
        dv_out = {}
        dv_mul = {}
        dv_sev = {}
        dv_yev = {}
        x3 = {}
        for oc in range(2, 32):
            w = [] if oc < 3 else [("PE", (lambda o=oc: pe_base[o - 3]))]
            if oc == 2:
                w = [("PE", (lambda: pe_h[NFC - 1]))]
            for g in range(4):
                wd[oc] = dma(
                    "gp", "W" + "ABC"[oc % 3],
                    wbuf[oc % 3][:, g * 1024:(g + 1) * 1024],
                    bass.AP(d_w5.tensor, (oc * 32 + g * 8) * 128 * 128,
                            [[128, 128], [128 * 128, 8], [1, 128]]),
                    waits=w if g == 0 else ())
        def emit_x3(oc):
            w = [("DV", (lambda o=oc: dv_sub[NFC - 4 + (o % 4)]))] if oc < 4 else \
                [("DV", (lambda o=oc: dv_out[o - 4]))]
            x3[oc] = dma("ac", f"X3{oc % 4}",
                         xr32[:, (oc % 4) * ntok:(oc % 4 + 1) * ntok],
                         d_x[oc * 128:(oc + 1) * 128, :], waits=w)
        for oc in range(4):
            emit_x3(oc)

        sp_out = {}
        for I in range(16):
            oc0, oc1 = 2 * I, 2 * I + 1
            u0 = lambda h, I=I: xhi[:, (2 * I) * ntok + h * H:(2 * I) * ntok + (h + 1) * H]
            u1 = lambda h, I=I: xhi[:, (2 * I + 1) * ntok + h * H:(2 * I + 1) * ntok + (h + 1) * H]
            sth = lambda h: stb[:, h * H:(h + 1) * H]
            w0 = [("DV", DV_TT), ("DV", DV_HW)] if I == 0 else [("DV", dv_sev[I - 1])]
            PE_Y = 0
            for h in range(2):
                pe(lambda e, h=h, I=I: e.matmul(pb[4 + 2 * h][0:128, 0:H], ttri[:, :],
                                                u0(h, I), start=True, stop=False),
                   waits=w0 if h == 0 else ())
                pe(lambda e, h=h: e.matmul(pb[4 + 2 * h][0:128, 0:H], moutt[:, 0:128],
                                           sth(h), start=False, stop=True))
                pe(lambda e, h=h, I=I: e.matmul(pb[5 + 2 * h][0:128, 0:H], tful[:, :],
                                                u0(h, I), start=True, stop=False))
                pe(lambda e, h=h, I=I: e.matmul(pb[5 + 2 * h][0:128, 0:H], ttri[:, :],
                                                u1(h, I), start=False, stop=False))
                PE_Y = pe(lambda e, h=h: e.matmul(pb[5 + 2 * h][0:128, 0:H],
                                                  moutt[:, 128:256], sth(h),
                                                  start=False, stop=True))
            s0 = (oc0 % 4) * ntok
            s1 = (oc1 % 4) * ntok
            dv(lambda e, s0=s0: e.tensor_copy(big32[:, s0:s0 + H], pb[4][0:128, 0:H]),
               waits=(("PE", PE_Y),))
            dv(lambda e, s0=s0: e.tensor_copy(big32[:, s0 + H:s0 + ntok],
                                              pb[6][0:128, 0:H]))
            dv(lambda e, s1=s1: e.tensor_copy(big32[:, s1:s1 + H], pb[5][0:128, 0:H]))
            DV_YEV = dv(lambda e, s1=s1: e.tensor_copy(big32[:, s1 + H:s1 + ntok],
                                                       pb[7][0:128, 0:H]))
            dvd()
            dv_yev[oc0] = DV_YEV
            dv_yev[oc1] = DV_YEV
            def emit_scan(I, DV_YEV=DV_YEV, u0=u0, u1=u1, sth=sth):
                PE_SC = 0
                for h in range(2):
                    pe(lambda e, h=h, I=I: e.matmul(pb[4 + 2 * h][0:128, 0:H],
                                                    mint[:, 0:128], u0(h, I),
                                                    start=True, stop=False),
                       waits=(("DV", DV_YEV),) if h == 0 else ())
                    pe(lambda e, h=h, I=I: e.matmul(pb[4 + 2 * h][0:128, 0:H],
                                                    mint[:, 128:256], u1(h, I),
                                                    start=False, stop=False))
                    PE_SC = pe(lambda e, h=h: e.matmul(pb[4 + 2 * h][0:128, 0:H],
                                                       ppk[:, :], sth(h),
                                                       start=False, stop=True))
                dv(lambda e: e.tensor_copy(stb[:, 0:H], pb[4][0:128, 0:H]),
                   waits=(("PE", PE_SC),))
                dv_sev[I] = dv(lambda e: e.tensor_copy(stb[:, H:ntok],
                                                       pb[6][0:128, 0:H]))
                dvd()
            for oc in (oc0, oc1):
                if oc == oc1:
                    emit_scan(I)
                bk = 0 if oc % 2 == 0 else 2
                wl = [wd[oc]]
                if oc >= 2:
                    wl += [("DV", (lambda o=oc: dv_mul[o - 2]))]
                if oc < 2:
                    wl += [("DV", DV_HW), W_PARB_LAST]
                PE_BASE = 0
                for h in range(2):
                    if oc > 1:
                        for db in range(32):
                            pe(lambda e, oc=oc, h=h, db=db, bk=bk: e.matmul(
                                pb[bk + h][0:128, 0:H],
                                wbuf[oc % 3][:, db * 128:(db + 1) * 128],
                                xhi[:, db * ntok + h * H:db * ntok + (h + 1) * H],
                                start=(db == 0), stop=False),
                               waits=wl if (h == 0 and db == 0) else ())
                    PE_BASE = pe(lambda e, oc=oc, h=h, bk=bk: e.matmul(
                        pb[bk + h][0:128, 0:H], bpb[:, oc * 128:(oc + 1) * 128],
                        hwb[:, h * H:(h + 1) * H], start=False, stop=True),
                        waits=wl if (oc <= 1 and h == 0) else ())
                pe_base[oc] = PE_BASE
                for h in range(2):
                    ac_g[oc] = ac(lambda e, oc=oc, h=h, bk=bk: e.activation(
                        pb[bk + h][0:128, 0:H], pb[bk + h][0:128, 0:H], AF.Silu,
                        bias=bb128[:, oc:oc + 1]),
                        waits=[("PE", PE_BASE)] if h == 0 else [])
                    S.drain("ac")
                if oc + 4 < 32:
                    emit_x3(oc + 4)
                for h in range(2):
                    ssl = (oc % 4) * ntok + h * H
                    osl = (oc % 2) * ntok + h * H
                    wmu = [("AC", ac_g[oc]), x3[oc], ("DV", dv_yev[oc])]
                    if oc >= 2:
                        wmu += [("OUT" + "AB"[oc % 2],
                                 (lambda o=oc: sp_out[(o - 2, 1)][1]))]
                    dv_mul[oc] = dv(lambda e, ssl=ssl, osl=osl, bk=bk, h=h:
                        e.tensor_tensor(
                            xn32[:, osl:osl + H], pb[bk + h][0:128, 0:H],
                            big32[:, ssl:ssl + H], ALU.mult),
                        waits=wmu)
                    dvd()
                    dv_out[oc] = dv(lambda e, osl=osl, oc=oc, h=h: e.tensor_tensor(
                        xn32[:, osl:osl + H], xn32[:, osl:osl + H],
                        xr32[:, (oc % 4) * ntok + h * H:(oc % 4) * ntok + (h + 1) * H],
                        ALU.add))
                    dvd()
                    sp_out[(oc, h)] = dma(
                        "sp", "OUT" + "AB"[oc % 2],
                        bass.AP(d_y.tensor, oc * 128 * ntok + h * H,
                                [[ntok, 128], [1, H]]),
                        xn32[:, osl:osl + H], waits=(("DV", dv_out[oc]),))
        OUT_A = S.cnt["OUTA"]
        OUT_B = S.cnt["OUTB"]
        S.ops["sp"].append(lambda e, sems: (e.wait_ge(sems["OUTA"], OUT_A),
                                            e.wait_ge(sems["OUTB"], OUT_B)))

        build_block()

    return nc


def _host_arrays(inputs):
    x = np.ascontiguousarray(np.asarray(inputs["L0_attn_u"], dtype=np.float32)[0].T)
    Wb = np.asarray(inputs["L0_ffn_base_W"], dtype=np.float32)
    w5 = np.ascontiguousarray(
        Wb.T.reshape(32, 128, 32, 128).transpose(2, 0, 1, 3)).reshape(32 * 32 * 128, 128)
    A = np.asarray(inputs["L0_ffn_A"], dtype=np.float32)
    at = np.ascontiguousarray(A.reshape(128, D).T)
    Bm = np.asarray(inputs["L0_ffn_B"], dtype=np.float32)
    bpT = np.ascontiguousarray(Bm.transpose(0, 2, 1).reshape(128, D))
    rwt = np.ascontiguousarray(np.asarray(inputs["L0_ffn_router_W"], np.float32).T)
    bbT = np.ascontiguousarray(
        np.asarray(inputs["L0_ffn_base_b"], np.float32).reshape(32, 128).T)
    rb15 = np.ascontiguousarray(np.asarray(inputs["L0_ffn_router_b"], np.float32)[:, None])
    emap = np.ascontiguousarray(np.asarray(inputs["L0_ffn_expert_map"], np.int32)[:, None])
    svec = np.zeros((1, 640), np.float32)
    svec[0, 0:64] = inputs["L0_attn_Lambda_ri"][:, 0]
    svec[0, 64:128] = inputs["L0_attn_Lambda_ri"][:, 1]
    svec[0, 128:192] = inputs["L0_attn_p_ri"][:, 0]
    svec[0, 192:256] = inputs["L0_attn_p_ri"][:, 1]
    svec[0, 256:320] = inputs["L0_attn_q_ri"][:, 0]
    svec[0, 320:384] = inputs["L0_attn_q_ri"][:, 1]
    svec[0, 384:448] = np.asarray(inputs["L0_attn_Ct_ri"])[0, :, 0]
    svec[0, 448:512] = np.asarray(inputs["L0_attn_Ct_ri"])[0, :, 1]
    svec[0, 512:576] = np.asarray(inputs["L0_attn_Bp"])[:, 0]
    svec[0, 576] = np.asarray(inputs["L0_attn_D"])[0]
    svec[0, 577] = np.asarray(inputs["L0_attn_log_step"])[0]
    vctr = np.ascontiguousarray(np.asarray(inputs["L0_attn_Vc_ri"])[:, :, 0].T)
    vcti = np.ascontiguousarray(np.asarray(inputs["L0_attn_Vc_ri"])[:, :, 1].T)
    i128 = np.eye(128, dtype=np.float32)
    j128 = np.ascontiguousarray(i128[::-1, :])
    onesr = np.ones((1, 128), np.float32)
    onescb = np.ones((128, 128), bf16)
    rep8 = np.ascontiguousarray(
        (np.arange(128)[None, :] // 16 == np.arange(8)[:, None]).astype(bf16))
    io15 = np.ascontiguousarray(
        np.broadcast_to(np.arange(8, dtype=np.float32)[None, :], (15, 8)))
    shared = dict(w5=w5, at=at, bpT=bpT, rwt=rwt, bbT=bbT, rb15=rb15, emap=emap,
                  svec=svec, vctr=vctr, vcti=vcti, i128=i128, j128=j128,
                  onesr=onesr, onescb=onescb, rep8=rep8, io15=io15)
    return x, shared


def kernel(**inputs):
    xT, shared = _host_arrays(inputs)
    ntok = xT.shape[1] // NCORES
    nc = build_nc(ntok)
    in_maps = []
    for c in range(NCORES):
        m = dict(shared)
        m["xT"] = np.ascontiguousarray(xT[:, c * ntok:(c + 1) * ntok])
        in_maps.append(m)
    res = run_bass_kernel_spmd(nc, in_maps, core_ids=list(range(NCORES)))
    out = np.empty((1, NCORES * ntok, D), np.float32)
    for c in range(NCORES):
        out[0, c * ntok:(c + 1) * ntok, :] = res.results[c]["yT"].T
    return out


# version 29
# speedup vs baseline: 1.0173x; 1.0163x over previous
import numpy as np
import ml_dtypes
from contextlib import ExitStack

import concourse.bass as bass
import concourse.mybir as mybir
from concourse.bass_utils import run_bass_kernel_spmd

F32 = mybir.dt.float32
BF16 = mybir.dt.bfloat16
I32 = mybir.dt.int32
AF = mybir.ActivationFunctionType
ALU = mybir.AluOpType
AX = mybir.AxisListType
bf16 = ml_dtypes.bfloat16

NCORES = 8
D = 4096
NFC = 32


class Sched:
    ENGS = ("pe", "dv", "ac", "gp", "sp", "sc")

    def __init__(self):
        self.ops = {e: [] for e in self.ENGS}
        self.cnt = {}

    def emit(self, eng, fn, waits=(), inc=None, incby=1):
        if inc is not None:
            self.cnt[inc] = self.cnt.get(inc, 0) + incby
            tgt = self.cnt[inc]
        else:
            tgt = None

        def g(e, sems, fn=fn, waits=tuple(waits), inc=inc, incby=incby):
            for s, v in waits:
                v = v() if callable(v) else v
                if v > 0:
                    e.wait_ge(sems[s], v)
            r = fn(e)
            if inc is not None:
                r.then_inc(sems[inc], incby)
        self.ops[eng].append(g)
        return tgt

    def drain(self, eng):
        self.ops[eng].append(lambda e, sems: e.drain())


def build_nc(ntok=1024, stage=99):
    H = ntok // 2
    NTIL = ntok // 128
    HTIL = H // 128
    nc = bass.Bass("TRN2", target_bir_lowering=False)

    d_x = nc.dram_tensor("xT", [D, ntok], F32, kind="ExternalInput").ap()
    d_w5 = nc.dram_tensor("w5", [32 * 32 * 128, 128], F32, kind="ExternalInput").ap()
    d_at = nc.dram_tensor("at", [D, 128], F32, kind="ExternalInput").ap()
    d_bp = nc.dram_tensor("bpT", [128, D], F32, kind="ExternalInput").ap()
    d_rwt = nc.dram_tensor("rwt", [D, 15], F32, kind="ExternalInput").ap()
    d_bb = nc.dram_tensor("bbT", [128, 32], F32, kind="ExternalInput").ap()
    d_rb = nc.dram_tensor("rb15", [15, 1], F32, kind="ExternalInput").ap()
    d_em = nc.dram_tensor("emap", [15, 1], I32, kind="ExternalInput").ap()
    d_sv = nc.dram_tensor("svec", [1, 640], F32, kind="ExternalInput").ap()
    d_vr = nc.dram_tensor("vctr", [64, 64], F32, kind="ExternalInput").ap()
    d_vi = nc.dram_tensor("vcti", [64, 64], F32, kind="ExternalInput").ap()
    d_i128 = nc.dram_tensor("i128", [128, 128], F32, kind="ExternalInput").ap()
    d_j128 = nc.dram_tensor("j128", [128, 128], F32, kind="ExternalInput").ap()
    d_ones = nc.dram_tensor("onesr", [1, 128], F32, kind="ExternalInput").ap()
    d_onescb = nc.dram_tensor("onescb", [128, 128], BF16, kind="ExternalInput").ap()
    d_rep8 = nc.dram_tensor("rep8", [8, 128], BF16, kind="ExternalInput").ap()
    d_io15 = nc.dram_tensor("io15", [15, 8], F32, kind="ExternalInput").ap()
    d_kscr = nc.dram_tensor("kscr", [1, 4608], F32)
    d_y = nc.dram_tensor("yT", [D, ntok], F32, kind="ExternalOutput").ap()

    S = Sched()
    with ExitStack() as ctx:
        E = ctx.enter_context
        xhi = E(nc.sbuf_tensor("xhi", [128, NFC * ntok], BF16))
        wbuf = [E(nc.sbuf_tensor(f"wbuf{i}", [128, 32 * 128], BF16)) for i in range(2)]
        atb = E(nc.sbuf_tensor("atb", [128, NFC * 128], BF16))
        wbuf.append(atb)
        bpb = E(nc.sbuf_tensor("bpb", [128, D], BF16))
        rwt = E(nc.sbuf_tensor("rwt_s", [128, NFC * 15], BF16))
        hwb = E(nc.sbuf_tensor("hwb", [128, ntok], BF16))
        stb = E(nc.sbuf_tensor("stb", [128, ntok], BF16))
        big32 = E(nc.sbuf_tensor("big32", [128, 4096], F32))
        xr32 = E(nc.sbuf_tensor("xr32", [128, 4 * ntok], F32))
        xb16 = E(nc.sbuf_tensor("xb16", [128, 6 * ntok], BF16))
        xq16 = E(nc.sbuf_tensor("xq16", [128, 2 * ntok], BF16))
        xn32 = E(nc.sbuf_tensor("xn32", [128, 2 * ntok], F32))
        krow = E(nc.sbuf_tensor("krow", [1, 1024], F32))
        mbr = E(nc.sbuf_tensor("mbr", [128, 2 * ntok], F32))
        rows = E(nc.sbuf_tensor("rows", [1, 2048], F32))
        mats = E(nc.sbuf_tensor("mats", [128, 23 * 64], F32))
        wpr = E(nc.sbuf_tensor("wpr", [128, 256], F32))
        bstk = E(nc.sbuf_tensor("bstk", [128, 1], F32))
        cols = E(nc.sbuf_tensor("cols", [128, 16], F32))
        mint = E(nc.sbuf_tensor("mint", [128, 256], BF16))
        moutt = E(nc.sbuf_tensor("moutt", [128, 256], BF16))
        ppk = E(nc.sbuf_tensor("ppk", [128, 128], BF16))
        ttri = E(nc.sbuf_tensor("ttri", [128, 128], BF16))
        tful = E(nc.sbuf_tensor("tful", [128, 128], BF16))
        lgsb = E(nc.sbuf_tensor("lgsb", [15, H], F32))
        lgt = E(nc.sbuf_tensor("lgt", [128, NTIL * 15], F32))
        mx8 = E(nc.sbuf_tensor("mx8", [128, NTIL * 8], F32))
        pmt = E(nc.sbuf_tensor("pmt", [128, NTIL * 15], F32))
        scr15 = E(nc.sbuf_tensor("scr15", [128, 15], F32))
        rcol = E(nc.sbuf_tensor("rcol", [128, 8], F32))
        selm8 = E(nc.sbuf_tensor("selm8", [128, 120], F32))
        esum8 = E(nc.sbuf_tensor("esum8", [128, 16], F32))
        rbx = E(nc.sbuf_tensor("rbx", [128, 15], F32))
        scr1 = E(nc.sbuf_tensor("scr1", [128, 2], F32))
        pmts = E(nc.sbuf_tensor("pmts", [15, ntok], BF16))
        w8s = E(nc.sbuf_tensor("w8s", [8, ntok], BF16))
        w16s = E(nc.sbuf_tensor("w16s", [128, ntok], BF16))
        i128 = E(nc.sbuf_tensor("i128_s", [128, 128], F32))
        j128 = E(nc.sbuf_tensor("j128_s", [128, 128], F32))
        onesr = E(nc.sbuf_tensor("onesr_s", [1, 128], F32))
        onescb = E(nc.sbuf_tensor("onescb_s", [128, 128], BF16))
        rep8 = E(nc.sbuf_tensor("rep8_s", [8, 128], BF16))
        io15 = E(nc.sbuf_tensor("io15_s", [15, 8], F32))
        mmapb = E(nc.sbuf_tensor("mmapb", [15, 8], BF16))
        emf = E(nc.sbuf_tensor("emf", [15, 1], F32))
        emi = E(nc.sbuf_tensor("emi", [15, 1], I32))
        rb15 = E(nc.sbuf_tensor("rb15_s", [15, 1], F32))
        bb128 = E(nc.sbuf_tensor("bb128", [128, 32], F32))
        svec = E(nc.sbuf_tensor("svec_s", [1, 640], F32))
        vctr = E(nc.sbuf_tensor("vctr_s", [64, 64], F32))
        vcti = E(nc.sbuf_tensor("vcti_s", [64, 64], F32))
        sc64 = E(nc.sbuf_tensor("sc64", [128, 4], F32))
        tstg = E(nc.sbuf_tensor("tstg", [128, 256], F32))
        rop = E(nc.sbuf_tensor("rop", [128, 128], F32))
        pkstg = E(nc.sbuf_tensor("pkstg", [64, 512], F32))
        wop = E(nc.sbuf_tensor("wop", [128, 128], F32))

        pb = [E(nc.psum_tensor(f"pb{i}", [128, 512], F32)) for i in range(8)]

        sems = {}
        dma_sem_names = (["PAR", "PARB", "K", "K2", "TV", "RW", "X1A", "X1B", "X1C", "X1D", "WA", "WB", "WC",
                          "OUTA", "OUTB"]
                         + [f"X2{i}" for i in range(4)]
                         + [f"X3{i}" for i in range(4)])
        for sname in ["PE", "DV", "AC"] + dma_sem_names:
            sems[sname] = E(nc.semaphore(sname))

        def dma(eng, sem, dst, src, waits=()):
            v = S.emit(eng, lambda e, d=dst, s=src: e.dma_start(d, s),
                       waits=waits, inc=sem, incby=16)
            return (sem, v)

        def build_block():
            block = E(nc.Block())

            @block.tensor
            def _(t):
                for f in S.ops["pe"]:
                    f(t, sems)

            @block.vector
            def _(v):
                for f in S.ops["dv"]:
                    f(v, sems)

            @block.scalar
            def _(a):
                for f in S.ops["sc"]:
                    f(a, sems)
                for f in S.ops["ac"]:
                    f(a, sems)

            @block.gpsimd
            def _(g):
                for f in S.ops["gp"]:
                    f(g, sems)

            @block.sync
            def _(s):
                for f in S.ops["sp"]:
                    f(s, sems)

        def pe(fn, waits=()):
            return S.emit("pe", fn, waits=waits, inc="PE")

        def dv(fn, waits=()):
            return S.emit("dv", fn, waits=waits, inc="DV")

        def ac(fn, waits=()):
            return S.emit("ac", fn, waits=waits, inc="AC")

        def dvd():
            S.drain("dv")

        def M(k):
            return mats[0:64, 64 * k:64 * k + 64]

        def R_(off, ln=64):
            return rows[0:1, off:off + ln]

        def col(k):
            return cols[0:64, k:k + 1]

        LR, LI, PR, PI, QR, QI, CTR, CTI, BPR = (64 * i for i in range(9))
        DLT, HS, NHS = 576, 578, 580
        NDR, NDI, DEN, INVD, FR, FI = 640, 704, 768, 832, 896, 960
        QFR, QFI, UR, UI, T1, T2 = 1024, 1088, 1152, 1216, 1280, 1344
        SWR, SWI, CR_, CI_, WRR, WRI = 1408, 1410, 1412, 1414, 1472, 1536
        E1PR, E1PI, NQI, NUI = 1600, 1664, 1728, 1792
        B0R, B0I = 1856, 1920
        MEAN, VTMP, RSTD = 0, 1024, 2048

        XR, XRT, XI, XIT, XIN, XITN = 0, 1, 2, 3, 4, 5
        YR, YRT, YI, YIT, YIN, YITN = 6, 7, 8, 9, 10, 11
        E1R, E1I, E1RT, E1IT, E1ITN = 12, 13, 14, 15, 16
        TM1, TM2, TM3, TM4 = 17, 18, 19, 20
        WBR, WBI = 21, 22
        X8TR, X8TI = 23, 24
        D_R, D_IN, D_RT, D_ITN, X8TR64, X8TI64 = 0, 1, 2, 3, 4, 5

        def M64(k):
            return mats[64:128, 64 * k:64 * k + 64]

        CPR, CPI, CBP, CB0R, CB0I, CBBR, CBBI, CNPI = range(8)

        dma("gp", "PAR", svec[:, :], d_sv)
        dma("gp", "PAR", vctr[:, :], d_vr)
        dma("gp", "PAR", vcti[:, :], d_vi)
        dma("gp", "PAR", i128[:, :], d_i128)
        dma("gp", "PAR", j128[:, :], d_j128)
        dma("gp", "PAR", onesr[:, :], d_ones)
        dma("gp", "PAR", onescb[:, :], d_onescb)
        dma("gp", "PAR", rep8[:, :], d_rep8)
        dma("gp", "PAR", io15[:, :], d_io15)
        dma("gp", "PAR", emi[:, :], d_em)
        dma("gp", "PAR", rb15[:, :], d_rb)
        W_PAR_LAST = dma("gp", "PAR", bb128[:, :], d_bb)
        W_PAR = (W_PAR_LAST,)

        ac_sq = {}
        pe_st = {}
        x1g = {}
        x1 = {}

        def emit_x1g(g):
            w = [] if g < 3 else \
                [("AC", (lambda f=g: ac_sq[2 * (f - 3) + 1])),
                 ("PE", (lambda f=g: pe_st[2 * (f - 3) + 1]))]
            sl = (g % 3) * 2 * ntok
            x1g[g] = dma("gp", "X1" + "ABCD"[g % 4],
                         xb16[:, sl:sl + 2 * ntok],
                         bass.AP(d_x.tensor, 2 * g * 128 * ntok,
                                 [[ntok, 128], [128 * ntok, 2], [1, ntok]]),
                         waits=w)
            x1[2 * g] = x1g[g]
            x1[2 * g + 1] = x1g[g]

        emit_x1g(0)
        emit_x1g(1)
        emit_x1g(2)
        for g in range(4):
            dma("gp", "PARB", atb[:, g * 1024:(g + 1) * 1024],
                bass.AP(d_at.tensor, g * 8 * 128 * 128,
                        [[128, 128], [128 * 128, 8], [1, 128]]))
        W_PARB_LAST = dma("gp", "PARB", bpb[:, :], d_bp)
        SC_RWT = None
        for g in range(4):
            SC_RWT = dma("gp", "RW", rwt[:, g * 8 * 15:(g + 1) * 8 * 15],
                         bass.AP(d_rwt.tensor, g * 8 * 128 * 15,
                                 [[15, 128], [128 * 15, 8], [1, 15]]))
        for g in range(3, NFC // 2):
            emit_x1g(g)

        EXP = {}

        def _constr_gen():
            def cdv(fn, waits=()):
                r = S.emit("dv", fn, waits=waits, inc="DV")
                S.drain("dv")
                return r

            ac(lambda e: e.activation(R_(DLT, 1), svec[0:1, 577:578], AF.Exp), waits=W_PAR)
            S.drain("ac")
            ac(lambda e: e.activation(R_(HS, 1), R_(DLT, 1), AF.Copy, scale=0.5))
            S.drain("ac")
            AC_H = ac(lambda e: e.activation(R_(NHS, 1), R_(DLT, 1), AF.Copy, scale=-0.5))
            S.drain("ac")
            pe(lambda e: e.matmul(pb[6][0:128, 4:5], onesr[0:1, 0:128], R_(HS, 1),
                                  start=True, stop=True), waits=(("AC", AC_H),))
            pe(lambda e: e.matmul(pb[6][0:128, 5:6], onesr[0:1, 0:128], R_(NHS, 1),
                                  start=True, stop=True))
            PE_BC = pe(lambda e: e.matmul(pb[6][0:128, 6:7], onesr[0:1, 0:128], R_(DLT, 1),
                                          start=True, stop=True))
            cdv(lambda e: e.tensor_copy(sc64[:, 0:1], pb[6][0:128, 4:5]), waits=(("PE", PE_BC),))
            cdv(lambda e: e.tensor_copy(sc64[:, 1:2], pb[6][0:128, 5:6]))
            cdv(lambda e: e.tensor_copy(sc64[:, 2:3], pb[6][0:128, 6:7]))
            dvd()
            yield
            for dst, src in ((LR, 0), (LI, 64), (PR, 128), (PI, 192), (QR, 256),
                             (QI, 320), (CTR, 384), (CTI, 448), (BPR, 512)):
                cdv(lambda e, d=dst, s=src: e.tensor_copy(R_(d), svec[0:1, s:s + 64]),
                   waits=W_PAR)
            dvd()
            yield
            cdv(lambda e: e.tensor_scalar(R_(NDR), R_(LR), R_(HS, 1), -1.0, ALU.mult, ALU.add),
               waits=(("AC", AC_H),))
            cdv(lambda e: e.tensor_scalar(R_(NDI), R_(LI), R_(HS, 1), None, ALU.mult))
            dvd()
            yield
            cdv(lambda e: e.tensor_tensor(R_(DEN), R_(NDR), R_(NDR), ALU.mult))
            cdv(lambda e: e.tensor_tensor(R_(T1), R_(NDI), R_(NDI), ALU.mult))
            dvd()
            yield
            cdv(lambda e: e.tensor_tensor(R_(DEN), R_(DEN), R_(T1), ALU.add))
            dvd()
            yield
            cdv(lambda e: e.reciprocal(R_(INVD), R_(DEN)))
            dvd()
            yield
            cdv(lambda e: e.scalar_tensor_tensor(R_(FR), R_(NDR), -1.0, R_(INVD),
                                                ALU.mult, ALU.mult))
            cdv(lambda e: e.tensor_tensor(R_(FI), R_(NDI), R_(INVD), ALU.mult))
            dvd()
            yield
            cdv(lambda e: e.tensor_tensor(R_(QFR), R_(QR), R_(FR), ALU.mult))
            cdv(lambda e: e.tensor_tensor(R_(T1), R_(QI), R_(FI), ALU.mult))
            cdv(lambda e: e.tensor_tensor(R_(QFI), R_(QR), R_(FI), ALU.mult))
            cdv(lambda e: e.tensor_tensor(R_(T2), R_(QI), R_(FR), ALU.mult))
            dvd()
            yield
            cdv(lambda e: e.tensor_tensor(R_(QFR), R_(QFR), R_(T1), ALU.add))
            cdv(lambda e: e.tensor_tensor(R_(QFI), R_(QFI), R_(T2), ALU.subtract))
            cdv(lambda e: e.tensor_tensor(R_(UR), R_(PR), R_(FR), ALU.mult))
            cdv(lambda e: e.tensor_tensor(R_(T1), R_(PI), R_(FI), ALU.mult))
            cdv(lambda e: e.tensor_tensor(R_(UI), R_(PR), R_(FI), ALU.mult))
            cdv(lambda e: e.tensor_tensor(R_(T2), R_(PI), R_(FR), ALU.mult))
            dvd()
            yield
            cdv(lambda e: e.tensor_tensor(R_(UR), R_(UR), R_(T1), ALU.subtract))
            cdv(lambda e: e.tensor_tensor(R_(UI), R_(UI), R_(T2), ALU.add))
            dvd()
            yield
            cdv(lambda e: e.tensor_tensor(R_(T1), R_(QFR), R_(PR), ALU.mult))
            cdv(lambda e: e.tensor_tensor(R_(T2), R_(QFI), R_(PI), ALU.mult))
            dvd()
            yield
            cdv(lambda e: e.tensor_tensor(R_(T1), R_(T1), R_(T2), ALU.subtract))
            dvd()
            yield
            cdv(lambda e: e.tensor_reduce(R_(SWR, 1), R_(T1), AX.X, ALU.add))
            cdv(lambda e: e.tensor_tensor(R_(T1), R_(QFR), R_(PI), ALU.mult))
            cdv(lambda e: e.tensor_tensor(R_(T2), R_(QFI), R_(PR), ALU.mult))
            dvd()
            yield
            cdv(lambda e: e.tensor_tensor(R_(T1), R_(T1), R_(T2), ALU.add))
            dvd()
            yield
            cdv(lambda e: e.tensor_reduce(R_(SWI, 1), R_(T1), AX.X, ALU.add))
            dvd()
            yield
            cdv(lambda e: e.tensor_scalar(R_(CR_, 1), R_(SWR, 1), R_(HS, 1), 1.0,
                                         ALU.mult, ALU.add))
            cdv(lambda e: e.tensor_scalar(R_(CI_, 1), R_(SWI, 1), R_(HS, 1), None, ALU.mult))
            dvd()
            yield
            cdv(lambda e: e.tensor_tensor(R_(T1, 1), R_(CR_, 1), R_(CR_, 1), ALU.mult))
            cdv(lambda e: e.tensor_tensor(R_(T2, 1), R_(CI_, 1), R_(CI_, 1), ALU.mult))
            dvd()
            yield
            cdv(lambda e: e.tensor_tensor(R_(T1, 1), R_(T1, 1), R_(T2, 1), ALU.add))
            dvd()
            yield
            cdv(lambda e: e.reciprocal(R_(T2, 1), R_(T1, 1)))
            dvd()
            yield
            cdv(lambda e: e.tensor_tensor(R_(T1, 1), R_(CR_, 1), R_(T2, 1), ALU.mult))
            dvd()
            yield
            cdv(lambda e: e.tensor_scalar(R_(CR_, 1), R_(T1, 1), R_(HS, 1), None, ALU.mult))
            cdv(lambda e: e.tensor_tensor(R_(T1, 1), R_(CI_, 1), R_(T2, 1), ALU.mult))
            dvd()
            yield
            cdv(lambda e: e.tensor_scalar(R_(CI_, 1), R_(T1, 1), R_(NHS, 1), None, ALU.mult))
            dvd()
            yield
            cdv(lambda e: e.tensor_scalar(R_(T1), R_(UR), R_(CR_, 1), None, ALU.mult))
            cdv(lambda e: e.tensor_scalar(R_(T2), R_(UI), R_(CI_, 1), None, ALU.mult))
            dvd()
            yield
            cdv(lambda e: e.tensor_tensor(R_(T1), R_(T1), R_(T2), ALU.subtract))
            cdv(lambda e: e.tensor_scalar(R_(T2), R_(UR), R_(CI_, 1), None, ALU.mult))
            dvd()
            yield
            cdv(lambda e: e.tensor_scalar(R_(UI), R_(UI), R_(CR_, 1), None, ALU.mult))
            dvd()
            yield
            cdv(lambda e: e.tensor_tensor(R_(UI), R_(UI), R_(T2), ALU.add))
            cdv(lambda e: e.tensor_copy(R_(UR), R_(T1)))
            dvd()
            yield
            cdv(lambda e: e.tensor_scalar(R_(NUI), R_(UI), -1.0, None, ALU.mult))
            DV_WOOD = cdv(lambda e: e.tensor_scalar(R_(NQI), R_(QI), -1.0, None, ALU.mult))
            dvd()
            yield

            PE_DG = pe(lambda e: e.matmul(pb[6][0:64, 0:64], onesr[0:1, 0:64], R_(FR),
                                          start=True, stop=True), waits=(("DV", DV_WOOD),))
            pe(lambda e: e.matmul(pb[6][0:64, 64:128], onesr[0:1, 0:64], R_(FI),
                                  start=True, stop=True))
            pe(lambda e: e.matmul(pb[7][0:64, 0:64], R_(UR), R_(QFR), start=True, stop=False))
            pe(lambda e: e.matmul(pb[7][0:64, 0:64], R_(NUI), R_(QFI), start=False, stop=True))
            pe(lambda e: e.matmul(pb[7][0:64, 64:128], R_(UR), R_(QFI), start=True, stop=False))
            PE_OUT = pe(lambda e: e.matmul(pb[7][0:64, 64:128], R_(UI), R_(QFR),
                                           start=False, stop=True))
            cdv(lambda e: e.tensor_tensor(M(TM1), pb[6][0:64, 0:64], i128[0:64, 0:64], ALU.mult),
               waits=(("PE", PE_OUT),))
            cdv(lambda e: e.tensor_tensor(M(TM2), pb[6][0:64, 64:128], i128[0:64, 0:64], ALU.mult))
            dvd()
            yield
            cdv(lambda e: e.tensor_tensor(M(E1R), M(TM1), pb[7][0:64, 0:64], ALU.subtract))
            DV_E1 = cdv(lambda e: e.tensor_tensor(M(E1I), M(TM2), pb[7][0:64, 64:128],
                                                 ALU.subtract))
            dvd()
            yield
            pe(lambda e: e.transpose(pb[6][0:64, 0:64], M(E1R), i128[0:64, 0:64]),
               waits=(("DV", DV_E1),))
            PE_E1T = pe(lambda e: e.transpose(pb[6][0:64, 64:128], M(E1I), i128[0:64, 0:64]))
            cdv(lambda e: e.tensor_copy(M(E1RT), pb[6][0:64, 0:64]), waits=(("PE", PE_E1T),))
            cdv(lambda e: e.tensor_copy(M(E1IT), pb[6][0:64, 64:128]))
            dvd()
            yield
            DV_E1TN = cdv(lambda e: e.tensor_scalar(M(E1ITN), M(E1IT), -1.0, None, ALU.mult))
            dvd()
            yield
            pe(lambda e: e.transpose(pb[6][0:64, 0:1], R_(PR), i128[0:1, 0:1]),
               waits=(("DV", DV_E1TN),))
            pe(lambda e: e.transpose(pb[6][0:64, 1:2], R_(PI), i128[0:1, 0:1]))
            PE_COLS = pe(lambda e: e.transpose(pb[6][0:64, 2:3], R_(BPR), i128[0:1, 0:1]))
            cdv(lambda e: e.tensor_copy(col(CPR), pb[6][0:64, 0:1]), waits=(("PE", PE_COLS),))
            cdv(lambda e: e.tensor_copy(col(CPI), pb[6][0:64, 1:2]))
            cdv(lambda e: e.tensor_copy(col(CBP), pb[6][0:64, 2:3]))
            cdv(lambda e: e.tensor_copy(krow[0:1, 0:64], R_(CTR)))
            DV_CT = cdv(lambda e: e.tensor_scalar(krow[0:1, 64:128], R_(CTI), -1.0,
                                                 None, ALU.mult))
            PE_CT = pe(lambda e: e.transpose(pb[6][0:128, 8:9], krow[0:1, 0:128],
                                             i128[0:1, 0:1]), waits=(("DV", DV_CT),))
            cdv(lambda e: e.tensor_copy(big32[:, 0:1], pb[6][0:128, 8:9]),
               waits=(("PE", PE_CT),))
            DV_NPI = cdv(lambda e: e.tensor_scalar(col(CNPI), col(CPI), -1.0, None, ALU.mult))
            pe(lambda e: e.matmul(pb[7][0:1, 0:64], col(CPR), M(E1RT), start=True, stop=False),
               waits=(("DV", DV_NPI),))
            pe(lambda e: e.matmul(pb[7][0:1, 0:64], col(CNPI), M(E1IT), start=False, stop=True))
            pe(lambda e: e.matmul(pb[7][0:1, 64:128], col(CPR), M(E1IT), start=True, stop=False))
            PE_E1P = pe(lambda e: e.matmul(pb[7][0:1, 64:128], col(CPI), M(E1RT),
                                           start=False, stop=True))
            cdv(lambda e: e.tensor_copy(R_(E1PR), pb[7][0:1, 0:64]), waits=(("PE", PE_E1P),))
            cdv(lambda e: e.tensor_copy(R_(E1PI), pb[7][0:1, 64:128]))
            cdv(lambda e: e.tensor_scalar(R_(WRR), R_(NDR), 2.0, None, ALU.add))
            DV_WR = cdv(lambda e: e.tensor_copy(R_(WRI), R_(NDI)))
            pe(lambda e: e.matmul(pb[6][0:64, 0:64], onesr[0:1, 0:64], R_(WRR),
                                  start=True, stop=True), waits=(("DV", DV_WR),))
            PE_WB = pe(lambda e: e.matmul(pb[6][0:64, 64:128], onesr[0:1, 0:64], R_(WRI),
                                          start=True, stop=True))
            cdv(lambda e: e.tensor_copy(M(WBR), pb[6][0:64, 0:64]), waits=(("PE", PE_WB),))
            cdv(lambda e: e.tensor_copy(M(WBI), pb[6][0:64, 64:128]))
            cdv(lambda e: e.tensor_tensor(M(TM1), M(E1R), M(WBR), ALU.mult))
            cdv(lambda e: e.tensor_tensor(M(TM2), M(E1I), M(WBI), ALU.mult))
            cdv(lambda e: e.tensor_tensor(M(TM3), M(E1R), M(WBI), ALU.mult))
            cdv(lambda e: e.tensor_tensor(M(TM4), M(E1I), M(WBR), ALU.mult))
            cdv(lambda e: e.tensor_tensor(M(TM1), M(TM1), M(TM2), ALU.subtract))
            DV_CSC = cdv(lambda e: e.tensor_tensor(M(TM3), M(TM3), M(TM4), ALU.add))
            pe(lambda e: e.matmul(pb[7][0:64, 0:64], R_(E1PR), R_(QR), start=True, stop=False),
               waits=(("DV", DV_CSC),))
            pe(lambda e: e.matmul(pb[7][0:64, 0:64], R_(E1PI), R_(QI), start=False, stop=True))
            pe(lambda e: e.matmul(pb[7][0:64, 64:128], R_(E1PR), R_(NQI), start=True, stop=False))
            PE_HO = pe(lambda e: e.matmul(pb[7][0:64, 64:128], R_(E1PI), R_(QR),
                                          start=False, stop=True))
            cdv(lambda e: e.scalar_tensor_tensor(M(XR), pb[7][0:64, 0:64], sc64[0:64, 1:2],
                                                M(TM1), ALU.mult, ALU.add),
               waits=(("PE", PE_HO),))
            DV_AB = cdv(lambda e: e.scalar_tensor_tensor(M(XI), pb[7][0:64, 64:128],
                                                        sc64[0:64, 1:2], M(TM3),
                                                        ALU.mult, ALU.add))
            pe(lambda e: e.matmul(pb[6][0:1, 0:64], col(CBP), vctr[:, :], start=True,
                                  stop=True), waits=(("DV", DV_AB),))
            PE_B0 = pe(lambda e: e.matmul(pb[6][0:1, 64:128], col(CBP), vcti[:, :],
                                          start=True, stop=True))
            cdv(lambda e: e.tensor_copy(R_(B0R), pb[6][0:1, 0:64]), waits=(("PE", PE_B0),))
            DV_B0 = cdv(lambda e: e.tensor_copy(R_(B0I), pb[6][0:1, 64:128]))
            pe(lambda e: e.transpose(pb[6][0:64, 0:1], R_(B0R), i128[0:1, 0:1]),
               waits=(("DV", DV_B0),))
            PE_B0C = pe(lambda e: e.transpose(pb[6][0:64, 1:2], R_(B0I), i128[0:1, 0:1]))
            cdv(lambda e: e.tensor_copy(col(CB0R), pb[6][0:64, 0:1]), waits=(("PE", PE_B0C),))
            DV_B0C = cdv(lambda e: e.tensor_copy(col(CB0I), pb[6][0:64, 1:2]))
            pe(lambda e: e.matmul(pb[7][0:64, 0:1], M(E1RT), col(CB0R), start=True, stop=False),
               waits=(("DV", DV_B0C),))
            pe(lambda e: e.matmul(pb[7][0:64, 0:1], M(E1ITN), col(CB0I), start=False, stop=True))
            pe(lambda e: e.matmul(pb[7][0:64, 1:2], M(E1IT), col(CB0R), start=True, stop=False))
            PE_BB = pe(lambda e: e.matmul(pb[7][0:64, 1:2], M(E1RT), col(CB0I),
                                          start=False, stop=True))
            cdv(lambda e: e.tensor_scalar(col(CBBR), pb[7][0:64, 0:1], sc64[0:64, 2:3], None,
                                         ALU.mult), waits=(("PE", PE_BB),))
            DV_BB = cdv(lambda e: e.tensor_scalar(col(CBBI), pb[7][0:64, 1:2], sc64[0:64, 2:3],
                                                 None, ALU.mult))
            pe(lambda e: e.transpose(pb[6][0:64, 0:64], M(XR), i128[0:64, 0:64]),
               waits=(("DV", DV_BB),))
            PE_X0T = pe(lambda e: e.transpose(pb[6][0:64, 64:128], M(XI), i128[0:64, 0:64]))
            cdv(lambda e: e.tensor_copy(M(XRT), pb[6][0:64, 0:64]), waits=(("PE", PE_X0T),))
            cdv(lambda e: e.tensor_copy(M(XIT), pb[6][0:64, 64:128]))
            cdv(lambda e: e.tensor_scalar(M(XITN), M(XIT), -1.0, None, ALU.mult))
            DV_LAST = cdv(lambda e: e.tensor_scalar(M(XIN), M(XI), -1.0, None, ALU.mult))

            STG = mats[0:64, 64 * TM3:64 * TM3 + 128]

            def packer(c, target):
                nonlocal DV_LAST, DV_PKC
                blocks = []
                if target in ("rop", "both"):
                    blocks += [(rop, 0, c["rt"], c["itn"]), (rop, 1, c["it"], c["rt"])]
                if target in ("wop", "both"):
                    blocks += [(wop, 0, c["r"], c["inn"]), (wop, 1, c["i"], c["r"])]
                first = True
                for idx, (dstT, cb, p1, p2) in enumerate(blocks):
                    dv(lambda e, idx=idx, p1=p1: e.tensor_copy(
                        pkstg[:, idx * 128:idx * 128 + 64], M(p1)),
                       waits=(("DV", DV_LAST),) if first else ())
                    first = False
                    DV_S = dv(lambda e, idx=idx, p2=p2: e.tensor_copy(
                        pkstg[:, idx * 128 + 64:idx * 128 + 128], M(p2)))
                DV_PKC = DV_S
                dvd()
                PE_P = 0
                for idx, (dstT, cb, p1, p2) in enumerate(blocks):
                    PE_P = pe(lambda e, idx=idx: e.transpose(
                        pb[5][0:128, idx * 64:(idx + 1) * 64],
                        pkstg[:, idx * 128:(idx + 1) * 128], i128[0:64, 0:64]),
                        waits=(("DV", DV_S),) if idx == 0 else ())
                for idx, (dstT, cb, p1, p2) in enumerate(blocks):
                    DV_LAST = dv(lambda e, idx=idx, dstT=dstT, cb=cb: e.tensor_copy(
                        dstT[:, cb * 64:(cb + 1) * 64],
                        pb[5][0:128, idx * 64:(idx + 1) * 64]),
                        waits=(("PE", PE_P),) if idx == 0 else ())
                dvd()

            def stack2(dst, cr, ci, neg_i):
                nonlocal DV_LAST
                pe(lambda e, cr=cr: e.transpose(pb[5][0:1, 384:448], cr, i128[0:64, 0:64]),
                   waits=(("DV", DV_LAST),))
                PE_T = pe(lambda e, ci=ci: e.transpose(pb[5][0:1, 448:512], ci,
                                                       i128[0:64, 0:64]))
                cdv(lambda e: e.tensor_copy(krow[0:1, 0:64], pb[5][0:1, 384:448]),
                   waits=(("PE", PE_T),))
                DV_R = cdv(lambda e: e.tensor_scalar(krow[0:1, 64:128], pb[5][0:1, 448:512],
                                                    -1.0 if neg_i else 1.0, None, ALU.mult))
                PE_T2 = pe(lambda e: e.transpose(pb[5][0:128, 128:129], krow[0:1, 0:128],
                                                 i128[0:1, 0:1]), waits=(("DV", DV_R),))
                DV_LAST = cdv(lambda e, dst=dst: e.tensor_copy(dst, pb[5][0:128, 128:129]),
                             waits=(("PE", PE_T2),))

            def sqfuse(c, n, wait=None):
                nonlocal DV_LAST, DV_SQL
                pe(lambda e, c=c: e.matmul(pb[7][0:64, 0:64], M(c["rt"]), M(c["r"]),
                                           start=True, stop=False),
                   waits=(("DV", DV_LAST if wait is None else wait),))
                pe(lambda e, c=c: e.matmul(pb[7][0:64, 0:64], M(c["itn"]), M(c["i"]),
                                           start=False, stop=True))
                pe(lambda e, c=c: e.matmul(pb[4][0:64, 0:64], M(c["rt"]), M(c["i"]),
                                           start=True, stop=False))
                pe(lambda e, c=c: e.matmul(pb[4][0:64, 0:64], M(c["it"]), M(c["r"]),
                                           start=False, stop=True))
                pe(lambda e, c=c: e.matmul(pb[7][0:64, 64:128], M(c["r"]), M(c["rt"]),
                                           start=True, stop=False))
                pe(lambda e, c=c: e.matmul(pb[7][0:64, 64:128], M(c["inn"]), M(c["it"]),
                                           start=False, stop=True))
                pe(lambda e, c=c: e.matmul(pb[4][0:64, 64:128], M(c["r"]), M(c["it"]),
                                           start=True, stop=False))
                PE_SQ = pe(lambda e, c=c: e.matmul(pb[4][0:64, 64:128], M(c["i"]),
                                                   M(c["rt"]), start=False, stop=True))
                dv(lambda e, n=n: e.tensor_copy(M(n["r"]), pb[7][0:64, 0:64]),
                   waits=(("PE", PE_SQ),))
                dv(lambda e, n=n: e.tensor_copy(M(n["i"]), pb[7][0:64, 64:128]))
                dv(lambda e, n=n: e.tensor_copy(M(n["rt"]), pb[4][0:64, 0:64]))
                dv(lambda e, n=n: e.tensor_copy(M(n["it"]), pb[4][0:64, 64:128]))
                dvd()
                dv(lambda e, n=n: e.tensor_scalar(M(n["itn"]), M(n["it"]), -1.0, None,
                                                  ALU.mult))
                DV_LAST = dv(lambda e, n=n: e.tensor_scalar(M(n["inn"]), M(n["i"]), -1.0,
                                                            None, ALU.mult))
                DV_SQL = DV_LAST
                dvd()

            DV_SQL = 0
            DV_PKC = 0
            cur = dict(r=XR, i=XI, rt=XRT, it=XIT, itn=XITN, inn=XIN)
            nxt = dict(r=YR, i=YI, rt=YRT, it=YIT, itn=YITN, inn=YIN)

            packer(cur, "wop")
            stack2(bstk[:, 0:1], col(CBBR), col(CBBI), False)
            PE_W0 = pe(lambda e: e.matmul(pb[7][0:128, 0:1], wop[:, :], bstk[:, 0:1],
                                          start=True, stop=True), waits=(("DV", DV_LAST),))
            DV_LAST = cdv(lambda e: e.tensor_copy(wpr[:, 0:1], pb[7][0:128, 0:1]),
                         waits=(("PE", PE_W0),))

            setC = dict(r=12, rt=13, i=14, it=15, inn=16, itn=17)
            sets = [cur, nxt, setC]

            def blocks_of(c, target):
                b = []
                if target in ("rop", "both"):
                    b += [(rop, 0, c["rt"], c["itn"]), (rop, 1, c["it"], c["rt"])]
                if target in ("wop", "both"):
                    b += [(wop, 0, c["r"], c["inn"]), (wop, 1, c["i"], c["r"])]
                return b

            def pk_copy(c, target, w0=()):
                bl = blocks_of(c, target)
                AC_S = 0
                for idx, (dstT, cb, p1, p2) in enumerate(bl):
                    ac(lambda e, idx=idx, p1=p1: e.activation(
                        pkstg[:, idx * 128:idx * 128 + 64], M(p1), AF.Copy),
                       waits=w0 if idx == 0 else ())
                    AC_S = ac(lambda e, idx=idx, p2=p2: e.activation(
                        pkstg[:, idx * 128 + 64:idx * 128 + 128], M(p2), AF.Copy))
                S.drain("ac")
                return AC_S

            def pk_t(c, target, ac_s, dv_pke):
                bl = blocks_of(c, target)
                PE_P = 0
                for idx in range(len(bl)):
                    PE_P = pe(lambda e, idx=idx: e.transpose(
                        pb[5][0:128, idx * 64:(idx + 1) * 64],
                        pkstg[:, idx * 128:(idx + 1) * 128], i128[0:64, 0:64]),
                        waits=(("AC", ac_s), ("DV", dv_pke)) if idx == 0 else ())
                return PE_P

            def pk_evac(c, target, pe_p):
                nonlocal DV_LAST
                bl = blocks_of(c, target)
                first = True
                o = 0
                while o < len(bl):
                    dstT = bl[o][0]
                    DV_LAST = dv(lambda e, o=o, dstT=dstT: e.tensor_copy(
                        dstT[:, 0:128], pb[5][0:128, o * 64:o * 64 + 128]),
                        waits=(("PE", pe_p),) if first else ())
                    first = False
                    o += 2
                dvd()
                return DV_LAST

            def sq_mm(c, wait):
                pe(lambda e, c=c: e.matmul(pb[7][0:64, 0:64], M(c["rt"]), M(c["r"]),
                                           start=True, stop=False),
                   waits=(("DV", wait),))
                pe(lambda e, c=c: e.matmul(pb[7][0:64, 0:64], M(c["itn"]), M(c["i"]),
                                           start=False, stop=True))
                pe(lambda e, c=c: e.matmul(pb[4][0:64, 0:64], M(c["rt"]), M(c["i"]),
                                           start=True, stop=False))
                pe(lambda e, c=c: e.matmul(pb[4][0:64, 0:64], M(c["it"]), M(c["r"]),
                                           start=False, stop=True))
                pe(lambda e, c=c: e.matmul(pb[7][0:64, 64:128], M(c["r"]), M(c["rt"]),
                                           start=True, stop=False))
                pe(lambda e, c=c: e.matmul(pb[7][0:64, 64:128], M(c["inn"]), M(c["it"]),
                                           start=False, stop=True))
                pe(lambda e, c=c: e.matmul(pb[4][0:64, 64:128], M(c["r"]), M(c["it"]),
                                           start=True, stop=False))
                return pe(lambda e, c=c: e.matmul(pb[4][0:64, 64:128], M(c["i"]),
                                                  M(c["rt"]), start=False, stop=True))

            def M2(k):
                return mats[0:64, 64 * k:64 * k + 128]

            def sq_evac(n, pe_sq, acw=0):
                nonlocal DV_LAST
                dv(lambda e, n=n: e.tensor_copy(M2(n["r"]), pb[7][0:64, 0:128]),
                   waits=(("PE", pe_sq), ("AC", acw)))
                dv(lambda e, n=n: e.tensor_copy(M2(n["i"]), pb[4][0:64, 0:128]))
                dvd()
                DV_LAST = dv(lambda e, n=n: e.tensor_scalar(M2(n["inn"]), M2(n["i"]),
                                                            -1.0, None, ALU.mult))
                dvd()
                return DV_LAST

            def tgt(k):
                return "both" if k <= 8 else ("rop" if k <= 11 else "wop")

            DV_NEG = DV_LAST
            AC_PKC = {-1: 0, -2: 0}
            PKE_H = {-1: 0}
            PKT_H = {-1: 0}
            DVN_H = {-1: DV_LAST}
            PE_SQn = {}
            for k in range(min(18, stage + 1)):
                ck = sets[k % 3]
                nk = sets[(k + 1) % 3]
                km = k - 1
                cm = sets[km % 3] if km >= 0 else None
                if k < 16:
                    PE_SQn[k] = sq_mm(ck, DV_NEG)
                if km >= 0:
                    PE_PKT = pk_t(cm, tgt(km), AC_PKC[km], PKE_H[km - 1])
                    PKT_H[km] = PE_PKT
                if k < 16:
                    DV_NEG = sq_evac(nk, PE_SQn[k], AC_PKC[k - 2])
                    DVN_H[k] = DV_NEG
                if km >= 0:
                    DV_PKE = pk_evac(cm, tgt(km), PE_PKT)
                    PKE_H[km] = DV_PKE
                if km >= 0 and km <= 11:
                    m = 1 << km
                    ngrp = (m + 511) // 512
                    pe_rg = []
                    dv_rg = []
                    for g in range(ngrp):
                        w = min(512, m - g * 512)
                        o0 = g * 512
                        wt = DV_PKE if g == 0 else dv_rg[g - 1]
                        pe_rg.append(pe(lambda e, o0=o0, w=w: e.matmul(
                            pb[6][0:128, 0:w], rop[:, :], big32[:, o0:o0 + w],
                            start=True, stop=True), waits=(("DV", wt),)))
                        dv_rg.append(dv(lambda e, o0=o0, w=w, m=m: e.tensor_copy(
                            big32[:, m + o0:m + o0 + w], pb[6][0:128, 0:w]),
                            waits=(("PE", pe_rg[g]),)))
                        dvd()
                    DV_LAST = dv_rg[-1]
                    if km < 8:
                        S.ops["pe"].append(lambda e, sems: e.drain())
                        PE_W = pe(lambda e, m=m: e.matmul(
                            pb[6][0:128, 256:256 + m], wop[:, :], wpr[:, 0:m],
                            start=True, stop=True), waits=(("DV", DV_PKE),))
                        DV_LAST = dv(lambda e, m=m: e.tensor_copy(
                            wpr[:, m:2 * m], pb[6][0:128, 256:256 + m]),
                            waits=(("PE", PE_W),))
                        dvd()
                    if km == 8:
                        DV_LAST = dv(lambda e: e.tensor_copy(ppk[:, :], wop[:, :]))
                        dvd()
                if km >= 12:
                    PE_NW = pe(lambda e: e.matmul(pb[6][0:128, 0:256], wop[:, :],
                                                  wpr[:, :], start=True, stop=True),
                               waits=(("DV", DV_PKE),))
                    pe(lambda e, c=cm: e.matmul(pb[6][0:64, 256:257], M(c["rt"]),
                                                col(CBBR), start=True, stop=False))
                    pe(lambda e, c=cm: e.matmul(pb[6][0:64, 256:257], M(c["itn"]),
                                                col(CBBI), start=False, stop=True))
                    pe(lambda e, c=cm: e.matmul(pb[6][0:64, 257:258], M(c["it"]),
                                                col(CBBR), start=True, stop=False))
                    PE_NM = pe(lambda e, c=cm: e.matmul(pb[6][0:64, 257:258], M(c["rt"]),
                                                        col(CBBI), start=False, stop=True))
                    dv(lambda e: e.tensor_tensor(wpr[:, :], wpr[:, :], pb[6][0:128, 0:256],
                                                 ALU.add), waits=(("PE", PE_NM),))
                    dv(lambda e: e.tensor_tensor(col(CBBR), col(CBBR),
                                                 pb[6][0:64, 256:257], ALU.add))
                    DV_LAST = dv(lambda e: e.tensor_tensor(col(CBBI), col(CBBI),
                                                           pb[6][0:64, 257:258], ALU.add))
                    dvd()
                if k <= 16:
                    AC_PKC[k] = pk_copy(ck, tgt(k), w0=(
                        ("DV", DVN_H[k - 1]), ("PE", PKT_H[km])))
                yield

            if stage < 18:
                SP_KPAD = ("K", 0)
                cdv(lambda e: e.memset(stb[:, :], 0.0))
                cdv(lambda e: e.tensor_copy(emf[:, :], emi[:, :]), waits=W_PAR)
                DV_CONSTR = cdv(lambda e: e.tensor_scalar(mmapb[:, :], io15[:, :],
                                                          emf[:, 0:1], 2.0,
                                                          ALU.is_equal, ALU.mult))
                EXP["SP_KPAD"] = SP_KPAD
                EXP["K_ODD"] = ("K2", 0)
                EXP["DV_CONSTR"] = DV_CONSTR
                yield
                return
            stack2(bstk[:, 0:1], col(CBBR), col(CBBI), True)
            DV_BSTK = DV_LAST
            pe_k = {}
            dv_ke = {}
            kdma = {}
            for g in range(8):
                bank = 6 if g % 2 == 0 else 7
                ks = (g % 2) * 512
                wk_pe = (("DV", DV_BSTK),) if g < 2 else (("DV", dv_ke[g - 2]),)
                pe_k[g] = pe(lambda e, g=g, bank=bank: e.matmul(
                    pb[bank][0:1, 0:512], bstk[:, :],
                    big32[:, g * 512:(g + 1) * 512],
                    start=True, stop=True), waits=wk_pe)
                wk = [("PE", pe_k[g])] + ([kdma[g - 2]] if g >= 2 else [])
                dv_ke[g] = cdv(lambda e, bank=bank, ks=ks: e.tensor_copy(
                    krow[0:1, ks:ks + 512], pb[bank][0:1, 0:512]), waits=wk)
                if g == 0:
                    dv_ke[g] = cdv(lambda e: e.tensor_tensor(
                        krow[0:1, 0:1], krow[0:1, 0:1], svec[0:1, 576:577], ALU.add))
                kdma[g] = dma("sp", "K" if g % 2 == 0 else "K2",
                              bass.AP(d_kscr, 128 + g * 512, [[512, 1], [1, 512]]),
                              krow[0:1, ks:ks + 512], waits=(("DV", dv_ke[g]),))
            DV_Z = cdv(lambda e: e.memset(krow[0:1, 0:512], 0.0), waits=(kdma[6],))
            dma("sp", "K", bass.AP(d_kscr, 0, [[128, 1], [1, 128]]), krow[0:1, 0:128],
                waits=(("DV", DV_Z),))
            SP_KPAD = dma("sp", "K", bass.AP(d_kscr, 4224, [[384, 1], [1, 384]]),
                          krow[0:1, 0:384])
            EXP["K_ODD"] = kdma[7]

            cdv(lambda e: e.tensor_copy(moutt[0:64, 0:256], big32[0:64, 0:256]))
            DV_MO = cdv(lambda e: e.tensor_scalar(moutt[64:128, 0:256], big32[64:128, 0:256],
                                                 -1.0, None, ALU.mult))
            for (sub, c0) in ((0, 128), (1, 0)):
                pe(lambda e, c0=c0: e.transpose(pb[6][0:128, 0:64], wpr[0:64, c0:c0 + 128],
                                                i128[0:64, 0:64]), waits=(("DV", DV_MO),))
                S.ops["pe"].append(lambda e, sems: e.drain())
                PE_MT = pe(lambda e, c0=c0: e.transpose(pb[6][0:128, 64:128],
                                                        wpr[64:128, c0:c0 + 128],
                                                        i128[64:128, 64:128]))
                DV_MT = cdv(lambda e: e.tensor_copy(tstg[:, 0:128], pb[6][0:128, 0:128]),
                           waits=(("PE", PE_MT),))
                PE_MJ = pe(lambda e: e.matmul(pb[7][:, 0:128], j128[:, :], tstg[:, 0:128],
                                              start=True, stop=True), waits=(("DV", DV_MT),))
                DV_MO = cdv(lambda e, sub=sub: e.tensor_copy(mint[:, sub * 128:(sub + 1) * 128],
                                                            pb[7][:, 0:128]),
                           waits=(("PE", PE_MJ),))
            DV_MIN = DV_MO
            cdv(lambda e: e.memset(stb[:, :], 0.0))
            cdv(lambda e: e.tensor_copy(emf[:, :], emi[:, :]), waits=W_PAR)
            DV_CONSTR = cdv(lambda e: e.tensor_scalar(mmapb[:, :], io15[:, :], emf[:, 0:1],
                                                     2.0, ALU.is_equal, ALU.mult))


            EXP["SP_KPAD"] = SP_KPAD
            EXP["DV_CONSTR"] = DV_CONSTR
            yield

        _cgen = _constr_gen()

        def pump(n=1):
            for _ in range(n):
                try:
                    next(_cgen)
                except StopIteration:
                    return
        pump(15)
        for fc in range(NFC):
            sl = (fc % 6) * ntok
            ql = (fc % 2) * ntok
            wsq = [x1[fc]]
            if fc >= 2:
                wsq += [("PE", (lambda f=fc: pe_st[f - 2]))]
            ac_sq[fc] = ac(lambda e, sl=sl, ql=ql: e.activation(
                xq16[:, ql:ql + ntok], xb16[:, sl:sl + ntok], AF.Square),
                waits=wsq)
            S.drain("ac")
            for h in range(2):
                pe(lambda e, sl=sl, h=h, fc=fc: e.matmul(
                    pb[h][0:128, 0:H], onescb[:, :],
                    xb16[:, sl + h * H:sl + (h + 1) * H],
                    start=(fc == 0), stop=(fc == NFC - 1)),
                   waits=[x1[fc]] if h == 0 else ())
                pe_st[fc] = pe(lambda e, h=h, fc=fc, ql=ql: e.matmul(
                    pb[2 + h][0:128, 0:H], onescb[:, :],
                    xq16[:, ql + h * H:ql + (h + 1) * H],
                    start=(fc == 0), stop=(fc == NFC - 1)),
                    waits=[("AC", ac_sq[fc])] if h == 0 else ())
            pump(1)
        PE_STATS = pe_st[NFC - 1]
        for h in range(2):
            dv(lambda e, h=h: e.tensor_scalar(mbr[:, h * H:(h + 1) * H],
                                              pb[h][0:128, 0:H],
                                              1.0 / D, None, ALU.mult),
               waits=(("PE", PE_STATS),))
            dv(lambda e, h=h: e.tensor_scalar(mbr[:, ntok + h * H:ntok + (h + 1) * H],
                                              pb[2 + h][0:128, 0:H],
                                              1.0 / D, None, ALU.mult))
        dvd()
        dv(lambda e: e.tensor_tensor(xn32[:, 0:ntok], mbr[:, 0:ntok], mbr[:, 0:ntok],
                                     ALU.mult))
        dvd()
        dv(lambda e: e.tensor_tensor(mbr[:, ntok:2 * ntok], mbr[:, ntok:2 * ntok],
                                     xn32[:, 0:ntok], ALU.subtract))
        dvd()
        DV_VAR = dv(lambda e: e.tensor_scalar(mbr[:, ntok:2 * ntok],
                                              mbr[:, ntok:2 * ntok], 1e-5,
                                              None, ALU.add))
        dvd()
        ac(lambda e: e.activation(xn32[:, ntok:2 * ntok], mbr[:, ntok:2 * ntok], AF.Ln),
           waits=(("DV", DV_VAR),))
        S.drain("ac")
        AC_RSTD = ac(lambda e: e.activation(mbr[:, ntok:2 * ntok], xn32[:, ntok:2 * ntok],
                                            AF.Exp, scale=-0.5))
        S.drain("ac")
        DV_BCAST = DV_VAR

        pump(9999)

        x2 = {}
        dv_xn = {}
        dv_sub = {}
        for fc in range(NFC):
            w = [("PE", PE_STATS)] if fc < 4 else [("DV", (lambda f=fc: dv_sub[f - 4]))]
            x2[fc] = dma("sp", f"X2{fc % 4}",
                         xr32[:, (fc % 4) * ntok:(fc % 4 + 1) * ntok],
                         d_x[fc * 128:(fc + 1) * 128, :], waits=w)
        pe_h = {}
        for fc in range(NFC):
            xsl = (fc % 4) * ntok
            nsl = (fc % 2) * ntok
            dv_sub[fc] = dv(lambda e, xsl=xsl, nsl=nsl: e.tensor_tensor(
                xn32[:, nsl:nsl + ntok], xr32[:, xsl:xsl + ntok], mbr[:, 0:ntok],
                ALU.subtract),
                waits=[x2[fc], ("DV", DV_BCAST), ("AC", AC_RSTD)]
                + ([("PE", pe_h[fc - 2])] if fc >= 2 else []))
            dvd()
            dv_xn[fc] = dv(lambda e, nsl=nsl, fc=fc: e.tensor_tensor(
                xhi[:, fc * ntok:(fc + 1) * ntok], xn32[:, nsl:nsl + ntok],
                mbr[:, ntok:2 * ntok], ALU.mult))
            dvd()
            w = [("DV", dv_xn[fc])]
            if fc == 0:
                w += [SC_RWT, W_PARB_LAST]
            for h in range(2):
                pe(lambda e, fc=fc, h=h: e.matmul(
                    pb[0 + h][0:15, 0:H], rwt[:, fc * 15:(fc + 1) * 15],
                    xhi[:, fc * ntok + h * H:fc * ntok + (h + 1) * H],
                    start=(fc == 0), stop=(fc == NFC - 1)), waits=w if h == 0 else ())
                pe_h[fc] = pe(lambda e, fc=fc, h=h: e.matmul(
                    pb[2 + h][0:128, 0:H], atb[:, fc * 128:(fc + 1) * 128],
                    xhi[:, fc * ntok + h * H:fc * ntok + (h + 1) * H],
                    start=(fc == 0), stop=(fc == NFC - 1)))
            pump(6)
        PE_RTH = pe_h[NFC - 1]
        pump(9999)
        SP_KPAD = EXP["SP_KPAD"]
        K_ODD = EXP["K_ODD"]
        DV_CONSTR = EXP["DV_CONSTR"]

        wd = {}
        PE_RBT = pe(lambda e: e.transpose(pb[4][0:1, 0:15], rb15[:, 0:1],
                                          i128[0:15, 0:15]),
                    waits=(("PE", PE_RTH), ("DV", DV_CONSTR)))
        DV_RBT = dv(lambda e: e.tensor_copy(scr15[0:1, 0:15], pb[4][0:1, 0:15]),
                    waits=(("PE", PE_RBT),))
        dvd()
        PE_RBX = pe(lambda e: e.matmul(pb[4][0:128, 16:31], onesr[0:1, 0:128],
                                       scr15[0:1, 0:15], start=True, stop=True),
                    waits=(("DV", DV_RBT),))
        DV_RC = dv(lambda e: e.tensor_copy(rbx[:, :], pb[4][0:128, 16:31]),
                   waits=(("PE", PE_RBX),))
        dvd()
        PE_TRL = 0
        DV_LGH = {}
        for h in range(2):
            DV_LG = dv(lambda e, h=h: e.tensor_scalar(
                lgsb[:, 0:H], pb[0 + h][0:15, 0:H], 1.0, None, ALU.mult),
                waits=[("DV", DV_RC)] + ([("PE", PE_TRL)] if h else []))
            DV_LGH[h] = DV_LG
            dvd()
            for tt in range(HTIL):
                t = h * HTIL + tt
                PE_TRL = pe(lambda e, t=t, tt=tt: e.transpose(
                    pb[5][0:128, t * 15:(t + 1) * 15], lgsb[:, tt * 128:(tt + 1) * 128],
                    i128[0:15, 0:15]), waits=(("DV", DV_LG),))
        DV_LGT = dv(lambda e: e.tensor_copy(lgt[:, :], pb[5][0:128, 0:NTIL * 15]),
                    waits=(("PE", PE_TRL),))
        dvd()

        for g in range(4):
            wd[0] = dma("gp", "WA", wbuf[0][:, g * 1024:(g + 1) * 1024],
                        bass.AP(d_w5.tensor, g * 8 * 128 * 128,
                                [[128, 128], [128 * 128, 8], [1, 128]]))
        for g in range(4):
            wd[1] = dma("gp", "WB", wbuf[1][:, g * 1024:(g + 1) * 1024],
                        bass.AP(d_w5.tensor, (32 + g * 8) * 128 * 128,
                                [[128, 128], [128 * 128, 8], [1, 128]]))
        dma("gp", "TV", tstg[:, 0:128], bass.AP(d_kscr, 1, [[1, 128], [1, 128]]),
            waits=(SP_KPAD, K_ODD, ("DV", DV_CONSTR)))
        GP_TREV = dma("gp", "TV", tstg[:, 128:256],
                      bass.AP(d_kscr, 129, [[1, 128], [1, 128]]))
        pe(lambda e: e.matmul(pb[6][:, 0:128], j128[:, :], tstg[:, 0:128],
                              start=True, stop=True),
           waits=(GP_TREV,))
        PE_TT = pe(lambda e: e.matmul(pb[6][:, 128:256], j128[:, :], tstg[:, 128:256],
                                      start=True, stop=True))
        dv(lambda e: e.tensor_copy(ttri[:, :], pb[6][:, 0:128]), waits=(("PE", PE_TT),))
        DV_TT = dv(lambda e: e.tensor_copy(tful[:, :], pb[6][:, 128:256]))
        dvd()
        web = [wd[0], ("DV", DV_LGH[1])]
        for h in range(2):
            for db in range(32):
                pe(lambda e, h=h, db=db: e.matmul(
                    pb[0 + h][0:128, 0:H],
                    wbuf[0][:, db * 128:(db + 1) * 128],
                    xhi[:, db * ntok + h * H:db * ntok + (h + 1) * H],
                    start=(db == 0), stop=False),
                   waits=web if (h == 0 and db == 0) else ())
        W15 = NTIL * 15
        for t in range(NTIL):
            dv(lambda e, t=t: e.tensor_tensor(
                lgt[:, t * 15:(t + 1) * 15], lgt[:, t * 15:(t + 1) * 15],
                rbx[:, :], ALU.add))
        dvd()
        for t in range(NTIL):
            dv(lambda e, t=t: e.max(mx8[:, t * 8:(t + 1) * 8],
                                    lgt[:, t * 15:(t + 1) * 15]))
        dvd()
        for t in range(NTIL):
            dv(lambda e, t=t: e.tensor_scalar(
                selm8[:, t * 15:(t + 1) * 15], lgt[:, t * 15:(t + 1) * 15],
                mx8[:, t * 8 + 7:t * 8 + 8], None, ALU.is_ge))
        for t in range(NTIL):
            dv(lambda e, t=t: e.tensor_scalar(
                lgt[:, t * 15:(t + 1) * 15], lgt[:, t * 15:(t + 1) * 15],
                mx8[:, t * 8:t * 8 + 1], None, ALU.subtract))
        DV_PRE = S.cnt["DV"]
        dvd()
        AC_EXP = ac(lambda e: e.activation(pmt[:, 0:W15], lgt[:, 0:W15], AF.Exp),
                    waits=(("DV", DV_PRE),))
        S.drain("ac")
        for t in range(NTIL):
            dv(lambda e, t=t: e.tensor_reduce(esum8[:, t:t + 1],
                                              pmt[:, t * 15:(t + 1) * 15],
                                              AX.X, ALU.add),
               waits=(("AC", AC_EXP),) if t == 0 else ())
        dvd()
        dv(lambda e: e.reciprocal(esum8[:, 8:8 + NTIL], esum8[:, 0:NTIL]))
        dvd()
        dv(lambda e: e.tensor_tensor(pmt[:, 0:W15], pmt[:, 0:W15], selm8[:, 0:W15],
                                     ALU.mult))
        dvd()
        dv_pm = 0
        for t in range(NTIL):
            dv_pm = dv(lambda e, t=t: e.tensor_scalar(
                pmt[:, t * 15:(t + 1) * 15], pmt[:, t * 15:(t + 1) * 15],
                esum8[:, 8 + t:9 + t], None, ALU.mult))
        dvd()
        DV_PMTS = 0
        for h in range(2):
            PE_TB = 0
            for tt in range(HTIL):
                t = h * HTIL + tt
                PE_TB = pe(lambda e, t=t, tt=tt: e.transpose(
                    pb[7][0:15, tt * 128:(tt + 1) * 128], pmt[:, t * 15:(t + 1) * 15],
                    i128[0:128, 0:128]),
                    waits=(("DV", dv_pm if h == 0 else DV_PMTS),) if tt == 0 else ())
            DV_PMTS = dv(lambda e, h=h: e.tensor_copy(pmts[:, h * H:(h + 1) * H],
                                                      pb[7][0:15, 0:H]),
                         waits=(("PE", PE_TB),))
            dvd()
        DV_HW = 0
        for h in range(2):
            PE_W8 = pe(lambda e, h=h: e.matmul(pb[4][0:8, 0:H], mmapb[:, :],
                                               pmts[:, h * H:(h + 1) * H],
                                               start=True, stop=True),
                       waits=(("DV", DV_PMTS), ("DV", DV_CONSTR)))
            DV_W8 = dv(lambda e, h=h: e.tensor_copy(w8s[:, h * H:(h + 1) * H],
                                                    pb[4][0:8, 0:H]),
                       waits=(("PE", PE_W8),))
            dvd()
            PE_W16 = pe(lambda e, h=h: e.matmul(pb[7][0:128, 0:H], rep8[:, :],
                                                w8s[:, h * H:(h + 1) * H],
                                                start=True, stop=True),
                        waits=(("DV", DV_W8),))
            DV_W16 = dv(lambda e, h=h: e.tensor_copy(w16s[:, h * H:(h + 1) * H],
                                                     pb[7][0:128, 0:H]),
                        waits=(("PE", PE_W16),))
            dvd()
            DV_HW = dv(lambda e, h=h: e.tensor_tensor(
                hwb[:, h * H:(h + 1) * H], pb[2 + h][0:128, 0:H],
                w16s[:, h * H:(h + 1) * H], ALU.mult), waits=(("DV", DV_W16),))
            dvd()

        pe_base = {}
        ac_g = {}
        dv_out = {}
        dv_mul = {}
        dv_sev = {}
        dv_yev = {}
        x3 = {}
        for oc in range(2, 32):
            w = [] if oc < 3 else [("PE", (lambda o=oc: pe_base[o - 3]))]
            if oc == 2:
                w = [("PE", (lambda: pe_h[NFC - 1]))]
            for g in range(4):
                wd[oc] = dma(
                    "gp", "W" + "ABC"[oc % 3],
                    wbuf[oc % 3][:, g * 1024:(g + 1) * 1024],
                    bass.AP(d_w5.tensor, (oc * 32 + g * 8) * 128 * 128,
                            [[128, 128], [128 * 128, 8], [1, 128]]),
                    waits=w if g == 0 else ())
        def emit_x3(oc):
            w = [("DV", (lambda o=oc: dv_sub[NFC - 4 + (o % 4)]))] if oc < 4 else \
                [("DV", (lambda o=oc: dv_out[o - 4]))]
            x3[oc] = dma("ac", f"X3{oc % 4}",
                         xr32[:, (oc % 4) * ntok:(oc % 4 + 1) * ntok],
                         d_x[oc * 128:(oc + 1) * 128, :], waits=w)
        for oc in range(4):
            emit_x3(oc)

        sp_out = {}
        for I in range(16):
            oc0, oc1 = 2 * I, 2 * I + 1
            u0 = lambda h, I=I: xhi[:, (2 * I) * ntok + h * H:(2 * I) * ntok + (h + 1) * H]
            u1 = lambda h, I=I: xhi[:, (2 * I + 1) * ntok + h * H:(2 * I + 1) * ntok + (h + 1) * H]
            sth = lambda h: stb[:, h * H:(h + 1) * H]
            w0 = [("DV", DV_TT), ("DV", DV_HW)] if I == 0 else [("DV", dv_sev[I - 1])]
            PE_Y = 0
            for h in range(2):
                pe(lambda e, h=h, I=I: e.matmul(pb[4 + 2 * h][0:128, 0:H], ttri[:, :],
                                                u0(h, I), start=True, stop=False),
                   waits=w0 if h == 0 else ())
                pe(lambda e, h=h: e.matmul(pb[4 + 2 * h][0:128, 0:H], moutt[:, 0:128],
                                           sth(h), start=False, stop=True))
                pe(lambda e, h=h, I=I: e.matmul(pb[5 + 2 * h][0:128, 0:H], tful[:, :],
                                                u0(h, I), start=True, stop=False))
                pe(lambda e, h=h, I=I: e.matmul(pb[5 + 2 * h][0:128, 0:H], ttri[:, :],
                                                u1(h, I), start=False, stop=False))
                PE_Y = pe(lambda e, h=h: e.matmul(pb[5 + 2 * h][0:128, 0:H],
                                                  moutt[:, 128:256], sth(h),
                                                  start=False, stop=True))
            s0 = (oc0 % 4) * ntok
            s1 = (oc1 % 4) * ntok
            dv(lambda e, s0=s0: e.tensor_copy(big32[:, s0:s0 + H], pb[4][0:128, 0:H]),
               waits=(("PE", PE_Y),))
            dv(lambda e, s0=s0: e.tensor_copy(big32[:, s0 + H:s0 + ntok],
                                              pb[6][0:128, 0:H]))
            dv(lambda e, s1=s1: e.tensor_copy(big32[:, s1:s1 + H], pb[5][0:128, 0:H]))
            DV_YEV = dv(lambda e, s1=s1: e.tensor_copy(big32[:, s1 + H:s1 + ntok],
                                                       pb[7][0:128, 0:H]))
            dvd()
            dv_yev[oc0] = DV_YEV
            dv_yev[oc1] = DV_YEV
            def emit_scan(I, DV_YEV=DV_YEV, u0=u0, u1=u1, sth=sth):
                PE_SC = 0
                for h in range(2):
                    pe(lambda e, h=h, I=I: e.matmul(pb[4 + 2 * h][0:128, 0:H],
                                                    mint[:, 0:128], u0(h, I),
                                                    start=True, stop=False),
                       waits=(("DV", DV_YEV),) if h == 0 else ())
                    pe(lambda e, h=h, I=I: e.matmul(pb[4 + 2 * h][0:128, 0:H],
                                                    mint[:, 128:256], u1(h, I),
                                                    start=False, stop=False))
                    PE_SC = pe(lambda e, h=h: e.matmul(pb[4 + 2 * h][0:128, 0:H],
                                                       ppk[:, :], sth(h),
                                                       start=False, stop=True))
                dv(lambda e: e.tensor_copy(stb[:, 0:H], pb[4][0:128, 0:H]),
                   waits=(("PE", PE_SC),))
                dv_sev[I] = dv(lambda e: e.tensor_copy(stb[:, H:ntok],
                                                       pb[6][0:128, 0:H]))
                dvd()
            for oc in (oc0, oc1):
                if oc == oc1:
                    emit_scan(I)
                bk = 0 if oc % 2 == 0 else 2
                wl = [wd[oc]]
                if oc >= 2:
                    wl += [("DV", (lambda o=oc: dv_mul[o - 2]))]
                if oc < 2:
                    wl += [("DV", DV_HW), W_PARB_LAST]
                PE_BASE = 0
                for h in range(2):
                    if oc > 0:
                        for db in range(32):
                            pe(lambda e, oc=oc, h=h, db=db, bk=bk: e.matmul(
                                pb[bk + h][0:128, 0:H],
                                wbuf[oc % 3][:, db * 128:(db + 1) * 128],
                                xhi[:, db * ntok + h * H:db * ntok + (h + 1) * H],
                                start=(db == 0), stop=False),
                               waits=wl if (h == 0 and db == 0) else ())
                    PE_BASE = pe(lambda e, oc=oc, h=h, bk=bk: e.matmul(
                        pb[bk + h][0:128, 0:H], bpb[:, oc * 128:(oc + 1) * 128],
                        hwb[:, h * H:(h + 1) * H], start=False, stop=True),
                        waits=wl if (oc == 0 and h == 0) else ())
                pe_base[oc] = PE_BASE
                for h in range(2):
                    ac_g[oc] = ac(lambda e, oc=oc, h=h, bk=bk: e.activation(
                        pb[bk + h][0:128, 0:H], pb[bk + h][0:128, 0:H], AF.Silu,
                        bias=bb128[:, oc:oc + 1]),
                        waits=[("PE", PE_BASE)] if h == 0 else [])
                    S.drain("ac")
                if oc + 4 < 32:
                    emit_x3(oc + 4)
                for h in range(2):
                    ssl = (oc % 4) * ntok + h * H
                    osl = (oc % 2) * ntok + h * H
                    wmu = [("AC", ac_g[oc]), x3[oc], ("DV", dv_yev[oc])]
                    if oc >= 2:
                        wmu += [("OUT" + "AB"[oc % 2],
                                 (lambda o=oc: sp_out[(o - 2, 1)][1]))]
                    dv_mul[oc] = dv(lambda e, ssl=ssl, osl=osl, bk=bk, h=h:
                        e.tensor_tensor(
                            xn32[:, osl:osl + H], pb[bk + h][0:128, 0:H],
                            big32[:, ssl:ssl + H], ALU.mult),
                        waits=wmu)
                    dvd()
                    dv_out[oc] = dv(lambda e, osl=osl, oc=oc, h=h: e.tensor_tensor(
                        xn32[:, osl:osl + H], xn32[:, osl:osl + H],
                        xr32[:, (oc % 4) * ntok + h * H:(oc % 4) * ntok + (h + 1) * H],
                        ALU.add))
                    dvd()
                    sp_out[(oc, h)] = dma(
                        "sp", "OUT" + "AB"[oc % 2],
                        bass.AP(d_y.tensor, oc * 128 * ntok + h * H,
                                [[ntok, 128], [1, H]]),
                        xn32[:, osl:osl + H], waits=(("DV", dv_out[oc]),))
        OUT_A = S.cnt["OUTA"]
        OUT_B = S.cnt["OUTB"]
        S.ops["sp"].append(lambda e, sems: (e.wait_ge(sems["OUTA"], OUT_A),
                                            e.wait_ge(sems["OUTB"], OUT_B)))

        build_block()

    return nc


def _host_arrays(inputs):
    x = np.ascontiguousarray(np.asarray(inputs["L0_attn_u"], dtype=np.float32)[0].T)
    Wb = np.asarray(inputs["L0_ffn_base_W"], dtype=np.float32)
    w5 = np.ascontiguousarray(
        Wb.T.reshape(32, 128, 32, 128).transpose(2, 0, 1, 3)).reshape(32 * 32 * 128, 128)
    A = np.asarray(inputs["L0_ffn_A"], dtype=np.float32)
    at = np.ascontiguousarray(A.reshape(128, D).T)
    Bm = np.asarray(inputs["L0_ffn_B"], dtype=np.float32)
    bpT = np.ascontiguousarray(Bm.transpose(0, 2, 1).reshape(128, D))
    rwt = np.ascontiguousarray(np.asarray(inputs["L0_ffn_router_W"], np.float32).T)
    bbT = np.ascontiguousarray(
        np.asarray(inputs["L0_ffn_base_b"], np.float32).reshape(32, 128).T)
    rb15 = np.ascontiguousarray(np.asarray(inputs["L0_ffn_router_b"], np.float32)[:, None])
    emap = np.ascontiguousarray(np.asarray(inputs["L0_ffn_expert_map"], np.int32)[:, None])
    svec = np.zeros((1, 640), np.float32)
    svec[0, 0:64] = inputs["L0_attn_Lambda_ri"][:, 0]
    svec[0, 64:128] = inputs["L0_attn_Lambda_ri"][:, 1]
    svec[0, 128:192] = inputs["L0_attn_p_ri"][:, 0]
    svec[0, 192:256] = inputs["L0_attn_p_ri"][:, 1]
    svec[0, 256:320] = inputs["L0_attn_q_ri"][:, 0]
    svec[0, 320:384] = inputs["L0_attn_q_ri"][:, 1]
    svec[0, 384:448] = np.asarray(inputs["L0_attn_Ct_ri"])[0, :, 0]
    svec[0, 448:512] = np.asarray(inputs["L0_attn_Ct_ri"])[0, :, 1]
    svec[0, 512:576] = np.asarray(inputs["L0_attn_Bp"])[:, 0]
    svec[0, 576] = np.asarray(inputs["L0_attn_D"])[0]
    svec[0, 577] = np.asarray(inputs["L0_attn_log_step"])[0]
    vctr = np.ascontiguousarray(np.asarray(inputs["L0_attn_Vc_ri"])[:, :, 0].T)
    vcti = np.ascontiguousarray(np.asarray(inputs["L0_attn_Vc_ri"])[:, :, 1].T)
    i128 = np.eye(128, dtype=np.float32)
    j128 = np.ascontiguousarray(i128[::-1, :])
    onesr = np.ones((1, 128), np.float32)
    onescb = np.ones((128, 128), bf16)
    rep8 = np.ascontiguousarray(
        (np.arange(128)[None, :] // 16 == np.arange(8)[:, None]).astype(bf16))
    io15 = np.ascontiguousarray(
        np.broadcast_to(np.arange(8, dtype=np.float32)[None, :], (15, 8)))
    shared = dict(w5=w5, at=at, bpT=bpT, rwt=rwt, bbT=bbT, rb15=rb15, emap=emap,
                  svec=svec, vctr=vctr, vcti=vcti, i128=i128, j128=j128,
                  onesr=onesr, onescb=onescb, rep8=rep8, io15=io15)
    return x, shared


def kernel(**inputs):
    xT, shared = _host_arrays(inputs)
    ntok = xT.shape[1] // NCORES
    nc = build_nc(ntok)
    in_maps = []
    for c in range(NCORES):
        m = dict(shared)
        m["xT"] = np.ascontiguousarray(xT[:, c * ntok:(c + 1) * ntok])
        in_maps.append(m)
    res = run_bass_kernel_spmd(nc, in_maps, core_ids=list(range(NCORES)))
    out = np.empty((1, NCORES * ntok, D), np.float32)
    for c in range(NCORES):
        out[0, c * ntok:(c + 1) * ntok, :] = res.results[c]["yT"].T
    return out
